# version 01
# baseline (speedup 1.0000x reference)
import numpy as np
from contextlib import ExitStack

import concourse.bass as bass
import concourse.mybir as mybir
from concourse.bass_utils import run_bass_kernel_spmd

F32 = mybir.dt.float32
BF16 = mybir.dt.bfloat16
I32 = mybir.dt.int32
OP = mybir.AluOpType
AF = mybir.ActivationFunctionType
AX = mybir.AxisListType.X

B, S, D, H, HD = 4, 4096, 1024, 16, 64
TC = 2048
NTT = 16
NNT = 4
NC = 8
HID = 4096
NHC = 32
EPS = 1e-6
LNEPS = 1e-5
RG = [[0, 1], [2, 3], [4, 5], [6, 7]]

_DBG_STAGE = None
_DBG_RESULT = None
_TAP_SHAPES = {
    "x_loaded": [128, NTT, D],
    "ln1": [TC, D],
    "ln1stat": [128, 64],
    "q_fm": [128, NC, TC],
    "k_tm": [128, NTT, D],
    "v_tm": [128, NTT, D],
    "ksum": [128, 8],
    "qsum": [128, 8],
    "si": [16, TC],
    "so": [16, TC],
    "kso": [128, 8],
    "qsi": [128, 8],
    "sa": [16, TC],
    "e": [16, TC],
    "kvu_pre": [128, 512],
    "kvu": [128, 512],
    "norm": [16, 4],
    "scq": [16, TC],
    "qfin": [128, NC, TC],
    "outa": [128, NC, 512],
    "projw": [D, D],
    "w1t": [D, HID],
    "w2q": [D, HID],
    "x0b": [128, NTT, D],
    "x1": [128, NTT, D],
    "xq": [TC, D],
    "b1s": [128, NHC],
    "xqg": [128, NC, 512],
    "h0": [128, 512],
    "hq": [128, NHC, 512],
    "x2": [128, NTT, D],
    "bqm": [128, NC, TC],
    "bkm": [128, NTT, D],
    "bz": [16, TC],
    "bkvu": [128, 512],
    "x3": [128, NTT, D],
    "x4": [128, NTT, D],
}


class Prog:
    def __init__(self, nc, stack, sem_names):
        self.nc = nc
        self.h = {n: stack.enter_context(nc.semaphore(n)) for n in sem_names}
        self.cnt = {n: 0 for n in sem_names}
        self._auto = set()

    def inc(self, instr, name, k=1):
        key = (id(instr.ins), name)
        if key in self._auto:
            return instr
        instr.then_inc(self.h[name], k)
        self.cnt[name] += k
        return instr

    def dma(self, instr, name):
        return self.inc(instr, name, 16)

    def wait(self, eng, *names):
        for n in names:
            if self.cnt[n] > 0:
                eng.wait_ge(self.h[n], self.cnt[n])

    def wait_val(self, eng, name, val):
        if val > 0:
            eng.wait_ge(self.h[name], val)


class Pool:

    def __init__(self, p, n):
        self.p = p
        self.st = [None] * n

    def acq(self, i, eng):
        s = self.st[i % len(self.st)]
        if s is not None:
            self.p.wait_val(eng, s[0], s[1])

    def rel(self, i, sem):
        self.st[i % len(self.st)] = (sem, self.p.cnt[sem])


def build_core_program(dbg_stage=None):
    nc = bass.Bass("TRN2", target_bir_lowering=False, debug=False)

    x_d = nc.dram_tensor("x_in", [TC, D], F32, kind="ExternalInput").ap()
    qkvw_d = nc.dram_tensor("aqkv_w", [3 * D, D], F32, kind="ExternalInput").ap()
    qkvb_d = nc.dram_tensor("aqkv_b", [1, 3 * D], F32, kind="ExternalInput").ap()
    projw_d = nc.dram_tensor("aproj_w", [D, D], F32, kind="ExternalInput").ap()
    projb_d = nc.dram_tensor("aproj_b", [1, D], F32, kind="ExternalInput").ap()
    bwq_d = nc.dram_tensor("b_wq", [D, D], F32, kind="ExternalInput").ap()
    bwk_d = nc.dram_tensor("b_wk", [D, D], F32, kind="ExternalInput").ap()
    bwv_d = nc.dram_tensor("b_wv", [D, D], F32, kind="ExternalInput").ap()
    bwo_d = nc.dram_tensor("b_wo", [D, D], F32, kind="ExternalInput").ap()
    bbq_d = nc.dram_tensor("b_bq", [1, D], F32, kind="ExternalInput").ap()
    bbk_d = nc.dram_tensor("b_bk", [1, D], F32, kind="ExternalInput").ap()
    bbv_d = nc.dram_tensor("b_bv", [1, D], F32, kind="ExternalInput").ap()
    bbo_d = nc.dram_tensor("b_bo", [1, D], F32, kind="ExternalInput").ap()
    w1_d = nc.dram_tensor("f_w1", [HID, D], F32, kind="ExternalInput").ap()
    b1_d = nc.dram_tensor("f_b1", [128, NHC], F32, kind="ExternalInput").ap()
    w2_d = nc.dram_tensor("f_w2", [D, HID], F32, kind="ExternalInput").ap()
    b2_d = nc.dram_tensor("f_b2", [1, D], F32, kind="ExternalInput").ap()
    y_d = nc.dram_tensor("y_out", [TC, D], F32, kind="ExternalOutput").ap()

    scr_ln = nc.dram_tensor("scr_ln", [TC, D], BF16, kind="Internal").ap()
    scr_k = nc.dram_tensor("scr_k", [TC, D], BF16, kind="Internal").ap()
    scr_tmp = nc.dram_tensor("scr_tmp", [HID, D], BF16, kind="Internal").ap()
    scr_qkvT = nc.dram_tensor("scr_qkvT", [D, 3 * D], BF16, kind="Internal").ap()
    scr_projT = nc.dram_tensor("scr_projT", [D, D], BF16, kind="Internal").ap()
    scr_w1T = nc.dram_tensor("scr_w1T", [D, HID], BF16, kind="Internal").ap()
    scr_w2q = nc.dram_tensor("scr_w2q", [D, HID], BF16, kind="Internal").ap()
    scr_wq = nc.dram_tensor("scr_wq", [D, D], BF16, kind="Internal").ap()
    scr_wk = nc.dram_tensor("scr_wk", [D, D], BF16, kind="Internal").ap()
    scr_wv = nc.dram_tensor("scr_wv", [D, D], BF16, kind="Internal").ap()
    scr_wo = nc.dram_tensor("scr_wo", [D, D], BF16, kind="Internal").ap()
    scr_bias = nc.dram_tensor("scr_bias", [8, 3 * D], BF16, kind="Internal").ap()
    cc_a_i = nc.dram_tensor("cc_a_i", [128, 16], F32, kind="Internal").ap()
    cc_a_o = nc.dram_tensor("cc_a_o", [128, 16], F32, kind="Internal").ap()
    cc_b_i = nc.dram_tensor("cc_b_i", [16, 1024], F32, kind="Internal").ap()
    cc_b_o = nc.dram_tensor("cc_b_o", [16, 1024], F32, kind="Internal").ap()
    cc_c_i = nc.dram_tensor("cc_c_i", [128, 8], F32, kind="Internal").ap()
    cc_c_o = nc.dram_tensor("cc_c_o", [128, 8], F32, kind="Internal").ap()
    cc_d_i = nc.dram_tensor("cc_d_i", [128, 512], F32, kind="Internal").ap()
    cc_d_o = nc.dram_tensor("cc_d_o", [128, 512], F32, kind="Internal").ap()
    cc_e_i = nc.dram_tensor("cc_e_i", [16, 4], F32, kind="Internal").ap()
    cc_e_o = nc.dram_tensor("cc_e_o", [16, 4], F32, kind="Internal").ap()
    dbg_d = None
    if dbg_stage is not None:
        dbg_d = nc.dram_tensor("dbg", _TAP_SHAPES[dbg_stage], F32,
                               kind="ExternalOutput").ap()

    with ExitStack() as top:
        def sbt(name, shape, dt, stk):
            return stk.enter_context(nc.sbuf_tensor(name, shape, dt))

        sb = lambda name, shape, dt: sbt(name, shape, dt, top)

        X = sb("X", [128, NTT, D], F32)
        T1f = sb("T1f", [128, 1024], F32)
        T2f = sb("T2f", [128, 512], F32)
        QS1 = sb("QS1", [128, 512], F32)
        QS2 = sb("QS2", [128, 512], F32)
        NIi = sb("NIi", [128, 512], I32)
        LNSTG = sb("LNSTG", [128, 1024], BF16)
        STAT = sb("STAT", [128, 64], F32)
        TA = sb("TA", [128, 16], F32)
        TB = sb("TB", [128, 16], F32)
        TCH = sb("TCH", [128, 16], F32)
        IDENT = sb("IDENT", [128, 16], F32)
        IOTC = sb("IOTC", [128, 1], F32)
        M0 = sb("M0", [16, 1024], BF16)
        ONESC = sb("ONESC", [128, 1], BF16)
        ONESR = sb("ONESR", [1, 512], BF16)
        BROW = [sb("BROW0", [1, 512], BF16), sb("BROW1", [1, 512], BF16)]
        B1S = sb("B1S", [128, NHC], F32)
        KSUM = sb("KSUM", [128, 8], F32)
        QSUM = sb("QSUM", [128, 8], F32)
        QSI = sb("QSI", [128, 8], F32)
        KSOC = sb("KSOC", [128, 8], F32)
        CB = sb("CB", [16, 4], F32)
        ESCOL = sb("ESCOL", [16, 4], F32)
        MSK = sb("MSK", [128, 2, 16], BF16)
        NOPC = sb("NOPC", [1, 4], F32)
        CONSTS = sb("CONSTS", [128, 4], F32)
        CONSTS2 = sb("CONSTS2", [128, 2], F32)
        nc.const_aps.register(F32, EPS, CONSTS[:, 0:1])
        nc.const_aps.register(F32, 1024.0, CONSTS[:, 1:2])
        nc.const_aps.register(F32, -0.5, CONSTS[:, 2:3])
        nc.const_aps.register(F32, -1.0, CONSTS[:, 3:4])
        LOG4096 = 8.317766166719343
        nc.const_aps.register(F32, LOG4096, CONSTS2[:, 0:1])

        psA = [top.enter_context(nc.psum_tensor("psA%d" % i, [128, 512], F32))
               for i in range(4)]
        psB = [top.enter_context(nc.psum_tensor("psB%d" % i, [128, 512], F32))
               for i in range(2)]
        psT = top.enter_context(nc.psum_tensor("psT", [128, 128], F32))
        psK = top.enter_context(nc.psum_tensor("psK", [128, 64], F32))

        p = Prog(nc, top, ["sx", "sg", "sp", "sv", "ss", "scc"])
        blk = top.enter_context(nc.Block())
        bankA = Pool(p, 4)
        bankB = Pool(p, 2)
        browp = Pool(p, 2)
        mskp = Pool(p, 2)

        _VOPS = {"tensor_scalar", "tensor_tensor", "tensor_copy", "reduce_sum",
                 "reduce_max", "tensor_add", "tensor_sub", "tensor_mul",
                 "tensor_max", "memset", "scalar_tensor_tensor",
                 "tensor_single_scalar", "tensor_scalar_mul", "tensor_relu"}
        _SOPS = {"activation", "copy", "add", "mul", "square", "sqrt", "sign"}

        class Chained:
            def __init__(self, eng, sem, ops):
                self._e = eng
                self._sem = sem
                self._ops = ops

            def __getattr__(self, name):
                f = getattr(self._e, name)
                if name in self._ops:
                    def g(*a, **k):
                        p.wait(self._e, self._sem)
                        ins = f(*a, **k)
                        p.inc(ins, self._sem, 1)
                        p._auto.add((id(ins.ins), self._sem))
                        return ins
                    return g
                return f

        def bV(fn):
            blk.vector(lambda v: fn(Chained(v, "sv", _VOPS)))

        def bS(fn):
            blk.scalar(lambda sc: fn(Chained(sc, "ss", _SOPS)))

        def snop(sc):
            return sc.copy(NOPC[0:1, 0:1], NOPC[0:1, 2:3])

        def P_E_INC(t, ins, *a):
            ins = p.inc(pe_fence(t), "sp", 1)
            p.wait(t, "sp")
            return ins

        def pe_fence(t):
            import os as _o
            if _o.environ.get("NOFENCE"):
                return t.matmul(psT[0:1, 127:128], ONESC[0:1, 0:1],
                                ONESC[0:1, 0:1], start=True, stop=True,
                                skip_group_check=True)
            for _ in range(3):
                t.matmul(psT[0:1, 127:128], ONESC[0:1, 0:1],
                         ONESC[0:1, 0:1], start=True, stop=True,
                         skip_group_check=True)
            return t.matmul(psT[0:1, 127:128], ONESC[0:1, 0:1],
                            ONESC[0:1, 0:1], start=True, stop=True,
                            skip_group_check=True)

        def tap(stage, src_ap):
            if dbg_stage != stage:
                return

            def _t(g):
                p.wait(g, "sp", "sv", "ss", "sx", "sg", "scc")
                p.dma(g.dma_start(out=dbg_d, in_=src_ap), "sg")
                g.wait_ge(p.h["sg"], p.cnt["sg"])
                p.inc(g.engine_nop(), "sg", 1)
            blk.gpsimd(_t)
            barrier("vector", "scalar", "tensor", "sync")

        def barrier(*engs):
            for name in engs:
                def _b(e, name=name):
                    p.wait(e, "sx", "sg", "sp", "sv", "ss", "scc")
                getattr(blk, name)(_b)

        bias_ready = [0]

        def load_brow(slot, row, col0, n):
            def _l(s, slot=slot, row=row, col0=col0, n=n):
                p.wait_val(s, "sg", bias_ready[0])
                st0 = browp.st[slot]
                if st0 is not None:
                    p.wait_val(s, st0[0], st0[1])
                p.dma(s.dma_start(out=BROW[slot][0:1, 0:n],
                                  in_=scr_bias[row:row + 1, col0:col0 + n]), "sx")
            blk.sync(_l)

        with ExitStack() as cst:
            IOTJ = sbt("IOTJ", [128, 128], F32, cst)
            M0A = sbt("M0A", [16, 1024], F32, cst)
            M0B = sbt("M0B", [16, 1], F32, cst)

            def _c(g):
                g.iota(IOTC[:, :], pattern=[[1, 1]], channel_multiplier=1,
                       allow_small_or_imprecise_dtypes=True)
                g.iota(IOTJ[:, :], pattern=[[1, 128]], channel_multiplier=0,
                       allow_small_or_imprecise_dtypes=True)
                g.iota(M0A[0:16, :].rearrange("p (s d) -> p s d", d=64),
                       pattern=[[1, 16], [0, 64]], channel_multiplier=0,
                       allow_small_or_imprecise_dtypes=True)
                p.inc(g.iota(M0B[0:16, :], pattern=[[1, 1]], channel_multiplier=1,
                             allow_small_or_imprecise_dtypes=True), "sg", 1)
            blk.gpsimd(_c)

            def _c2(v):
                p.wait(v, "sg")
                v.tensor_copy(NIi[:, 0:1], IOTC[:, :])
                v.tensor_scalar(NIi[:, 0:1], NIi[:, 0:1], 15, None,
                                OP.bitwise_and)
                v.tensor_copy(T1f[:, 0:1], NIi[:, 0:1])
                v.tensor_tensor(IDENT[:, :], IOTJ[:, 0:16],
                                T1f[:, 0:1].broadcast_to([128, 16]),
                                OP.is_equal)
                v.tensor_tensor(M0[0:16, :], M0A[0:16, :],
                                M0B[0:16, 0:1].broadcast_to([16, 1024]),
                                OP.is_equal)
                v.memset(ONESC[:, :], 1.0)
                v.memset(ESCOL[0:16, 0:4], 0.0)
                v.memset(NOPC[:, :], 0.0)
                v.memset(CONSTS[:, 0:1], EPS)
                v.memset(CONSTS[:, 1:2], 1024.0)
                v.memset(CONSTS[:, 2:3], -0.5)
                v.memset(CONSTS[:, 3:4], -1.0)
                v.memset(CONSTS2[:, 0:1], 8.317766166719343)
                p.inc(v.memset(ONESR[:, :], 1.0), "sv", 1)
            bV(_c2)

        def _lx(s):
            xr = x_d.rearrange("(t p) f -> p t f", p=128)
            for g4 in range(4):
                p.dma(s.dma_start(out=X[:, 4 * g4:4 * g4 + 4, :],
                                  in_=xr[:, 4 * g4:4 * g4 + 4, :]), "sx")
        blk.sync(_lx)

        def _lb(g):
            p.dma(g.dma_start(out=scr_bias[0:1, 0:3 * D], in_=qkvb_d), "sg")
            p.dma(g.dma_start(out=scr_bias[1:2, 0:D], in_=projb_d), "sg")
            p.dma(g.dma_start(out=scr_bias[2:3, 0:D], in_=bbq_d), "sg")
            p.dma(g.dma_start(out=scr_bias[3:4, 0:D], in_=bbk_d), "sg")
            p.dma(g.dma_start(out=scr_bias[4:5, 0:D], in_=bbv_d), "sg")
            p.dma(g.dma_start(out=scr_bias[5:6, 0:D], in_=bbo_d), "sg")
            p.dma(g.dma_start(out=scr_wq, in_=bwq_d), "sg")
            p.dma(g.dma_start(out=scr_wk, in_=bwk_d), "sg")
            p.dma(g.dma_start(out=scr_wv, in_=bwv_d), "sg")
            p.dma(g.dma_start(out=scr_wo, in_=bwo_d), "sg")
        blk.gpsimd(_lb)
        bias_ready[0] = p.cnt["sg"]

        with ExitStack() as wst:
            WSTG = sbt("WSTG", [128, 4096], BF16, wst)
            WF = sbt("WF", [128, 1024], F32, wst)

            def emit_transposed_scratch(src_bf16, dst_T, O):
                for ic in range(NC):
                    def _t(s, ic=ic):
                        p.wait(s, "sx", "sg")
                        p.dma(s.dma_start(
                            out=WSTG[:, 0:O],
                            in_=src_bf16[0:O, ic * 128:(ic + 1) * 128],
                            transpose=True), "sx")
                        p.wait(s, "sx")
                        p.dma(s.dma_start(
                            out=dst_T[ic * 128:(ic + 1) * 128, 0:O],
                            in_=WSTG[:, 0:O]), "sx")
                    blk.sync(_t)

            def _cast_qkv(g):
                p.wait(g, "sg")
                p.dma(g.dma_start(out=scr_tmp[0:3 * D, :], in_=qkvw_d), "sg")
            blk.gpsimd(_cast_qkv)
            emit_transposed_scratch(scr_tmp[0:3 * D, :], scr_qkvT, 3 * D)

            def _cast_proj(g):
                p.wait(g, "sx")
                p.dma(g.dma_start(out=scr_tmp[0:D, :], in_=projw_d), "sg")
            blk.gpsimd(_cast_proj)
            emit_transposed_scratch(scr_tmp[0:D, :], scr_projT, D)

            def gq_chain(src, dst, pp, n, scale):
                def _v1(v):
                    p.wait(v, "sx", "ss", "sg")
                    p.inc(v.tensor_scalar(QS1[0:pp, 0:n], src, 64.0, 0.5,
                                          OP.mult, OP.add), "sv", 1)
                bV(_v1)

                def _s1(sc):
                    p.wait(sc, "sv")
                    sc.copy(NIi[0:pp, 0:n], QS1[0:pp, 0:n])
                    p.inc(sc.copy(QS2[0:pp, 0:n], NIi[0:pp, 0:n]), "ss", 1)
                bS(_s1)

                def _v2(v):
                    p.wait(v, "ss")
                    v.tensor_tensor(T2f[0:pp, 0:n], QS1[0:pp, 0:n],
                                    QS2[0:pp, 0:n], OP.is_ge)
                    v.scalar_tensor_tensor(QS1[0:pp, 0:n], QS2[0:pp, 0:n], 1.0,
                                           T2f[0:pp, 0:n], OP.subtract, OP.add)
                    v.tensor_scalar(QS1[0:pp, 0:n], QS1[0:pp, 0:n], scale,
                                    None, OP.mult)
                    p.inc(v.tensor_scalar(dst, QS1[0:pp, 0:n], -128.0 * scale,
                                          127.0 * scale, OP.max, OP.min),
                          "sv", 1)
                bV(_v2)

            w1r = w1_d.rearrange("(r p) i -> p r i", p=128)
            tmpr = scr_tmp.rearrange("(r p) i -> p r i", p=128)
            for r in range(HID // 128):
                def _lw(s, r=r):
                    p.wait(s, "sv")
                    p.dma(s.dma_start(out=WF[:, :], in_=w1r[:, r, :]), "sx")
                blk.sync(_lw)
                for hh in range(2):
                    gq_chain(WF[:, hh * 512:(hh + 1) * 512],
                             WSTG[:, hh * 512:(hh + 1) * 512], 128, 512,
                             1.0 / 1024.0)

                def _sw(g, r=r):
                    p.wait(g, "sv", "sx")
                    p.dma(g.dma_start(out=tmpr[:, r, :], in_=WSTG[:, 0:1024]),
                          "sg")
                blk.gpsimd(_sw)
            emit_transposed_scratch(scr_tmp[0:HID, :], scr_w1T, HID)

            w2r = w2_d.rearrange("(r p) i -> p r i", p=128)
            w2qr = scr_w2q.rearrange("(r p) i -> p r i", p=128)
            for r in range(D // 128):
                for hh in range(4):
                    def _l2(s, r=r, hh=hh):
                        p.wait(s, "sv", "sg")
                        p.dma(s.dma_start(
                            out=WF[:, :],
                            in_=w2r[:, r, hh * 1024:(hh + 1) * 1024]), "sx")
                    blk.sync(_l2)
                    for q2 in range(2):
                        gq_chain(WF[:, q2 * 512:(q2 + 1) * 512],
                                 WSTG[:, q2 * 512:(q2 + 1) * 512], 128, 512,
                                 1.0 / 1024.0)

                    def _s2(g, r=r, hh=hh):
                        p.wait(g, "sv")
                        p.dma(g.dma_start(
                            out=w2qr[:, r, hh * 1024:(hh + 1) * 1024],
                            in_=WSTG[:, 0:1024]), "sg")
                    blk.gpsimd(_s2)

            def _lb1(s):
                p.wait(s, "sv", "sg")
                p.dma(s.dma_start(out=WF[:, 0:NHC], in_=b1_d), "sx")
            blk.sync(_lb1)
            gq_chain(WF[:, 0:NHC], LNSTG[:, 0:NHC], 128, NHC, 1.0 / 64.0)

            def _b1s(v):
                p.wait(v, "sv")
                p.inc(v.tensor_scalar(B1S[:, :], LNSTG[:, 0:NHC], 1024.0, 32.0,
                                      OP.mult, OP.add), "sv", 1)
            bV(_b1s)

            def _lb2(s):
                p.wait(s, "sv")
                p.dma(s.dma_start(out=WF[0:1, 0:D], in_=b2_d), "sx")
            blk.sync(_lb2)
            for q2 in range(2):
                gq_chain(WF[0:1, q2 * 512:(q2 + 1) * 512],
                         LNSTG[0:1, q2 * 512:(q2 + 1) * 512], 1, 512, 1.0 / 64.0)

            def _sb2(g):
                p.wait(g, "sv")
                p.dma(g.dma_start(out=scr_bias[6:7, 0:D], in_=LNSTG[0:1, 0:D]),
                      "sg")
            blk.gpsimd(_sb2)
            bias_ready[0] = p.cnt["sg"]

        def emit_ln(mode, tag=""):
            barrier("vector", "scalar")

            def _st(v):
                for tt in range(NTT):
                    v.reduce_sum(STAT[:, tt:tt + 1], X[:, tt, :], AX)
                p.inc(v.engine_nop(), "sv", 1)
            bV(_st)

            def _sq(sc):
                for tt in range(NTT):
                    sc.activation(T1f[:, :], X[:, tt, :], AF.Square,
                                  accum_out=STAT[:, 16 + tt:17 + tt])
                p.inc(snop(sc), "ss", 1)
            bS(_sq)

            def _chain(v):
                p.wait(v, "sv", "ss")
                v.tensor_scalar(STAT[:, 32:48], STAT[:, 0:16], -1.0 / 1024.0,
                                None, OP.mult)
                v.tensor_tensor(TA[:, :], STAT[:, 32:48], STAT[:, 32:48],
                                OP.mult)
                v.tensor_scalar(TB[:, :], STAT[:, 16:32], 1.0 / 1024.0,
                                None, OP.mult)
                v.tensor_tensor(TCH[:, :], TB[:, :], TA[:, :], OP.subtract)
                p.inc(v.tensor_scalar(TA[:, :], TCH[:, :], LNEPS,
                                      None, OP.add), "sv", 1)
            bV(_chain)

            def _rs(sc):
                p.wait(sc, "sv")
                sc.activation(TB[:, :], TA[:, :], AF.Ln)
                p.inc(sc.activation(STAT[:, 0:16], TB[:, :], AF.Exp,
                                    scale=-0.5), "ss", 1)
            bS(_rs)
            tap(tag + "stat", STAT[:, :])
            if mode == "quant":
                def _r16(v):
                    p.wait(v, "ss")
                    p.inc(v.tensor_scalar(STAT[:, 0:16], STAT[:, 0:16], 16.0,
                                          None, OP.mult), "sv", 1)
                bV(_r16)

            lnw = scr_ln.rearrange("(t p) f -> p t f", p=128)
            for tt in range(NTT):
                def _ap(sc, tt=tt):
                    p.wait(sc, "sv")
                    p.inc(sc.activation(T1f[:, :], X[:, tt, :], AF.Identity,
                                        bias=STAT[:, 32 + tt:33 + tt],
                                        scale=1.0), "ss", 1)
                bS(_ap)
                if mode == "plain":
                    def _mul(v, tt=tt):
                        p.wait(v, "ss", "sg")
                        p.inc(v.tensor_scalar(LNSTG[:, :], T1f[:, :],
                                              STAT[:, tt:tt + 1], None,
                                              OP.mult), "sv", 1)
                    bV(_mul)
                else:
                    def _q(v, tt=tt):
                        p.wait(v, "ss")
                        p.inc(v.tensor_scalar(T1f[:, :], T1f[:, :],
                                              STAT[:, tt:tt + 1], 0.5,
                                              OP.mult, OP.add), "sv", 1)
                    bV(_q)

                    def _qs(sc, tt=tt):
                        p.wait(sc, "sv")
                        sc.copy(NIi[:, 0:512], T1f[:, 0:512])
                        sc.copy(QS1[:, 0:512], NIi[:, 0:512])
                        sc.copy(NIi[:, 0:512], T1f[:, 512:1024])
                        p.inc(sc.copy(QS2[:, 0:512], NIi[:, 0:512]), "ss", 1)
                    bS(_qs)

                    def _qv(v, tt=tt):
                        p.wait(v, "ss", "sg")
                        for hh, f0 in ((0, QS1), (1, QS2)):
                            sl = slice(hh * 512, (hh + 1) * 512)
                            v.tensor_tensor(T2f[:, :], T1f[:, sl], f0[:, :],
                                            OP.is_ge)
                            v.scalar_tensor_tensor(T1f[:, sl], f0[:, :], 1.0,
                                                   T2f[:, :], OP.subtract,
                                                   OP.add)
                            v.tensor_scalar(LNSTG[:, sl], T1f[:, sl], -128.0,
                                            127.0, OP.max, OP.min)
                        p.inc(v.engine_nop(), "sv", 1)
                    bV(_qv)

                def _wr(g, tt=tt):
                    p.wait(g, "sv")
                    p.dma(g.dma_start(out=lnw[:, tt, :], in_=LNSTG[:, :]), "sg")
                blk.gpsimd(_wr)

        def emit_allreduce(pairs):
            def _up(g):
                p.wait(g, "sv", "ss", "sp", "sx")
                for ap_s, d_i, d_o in pairs:
                    p.dma(g.dma_start(out=d_i, in_=ap_s), "sg")
                g.wait_ge(p.h["sg"], p.cnt["sg"])
                for ap_s, d_i, d_o in pairs:
                    p.inc(g.collective_compute("AllReduce", OP.add,
                                               replica_groups=RG,
                                               ins=[d_i], outs=[d_o]),
                          "scc", 1)
                g.wait_ge(p.h["scc"], p.cnt["scc"])
                for ap_s, d_i, d_o in pairs:
                    p.dma(g.dma_start(out=ap_s, in_=d_o), "sg")
            blk.gpsimd(_up)

        def build_mask(slot, fc, src_col, eps):
            def _m(v, slot=slot, fc=fc, eps=eps):
                mskp.acq(slot, v)
                v.memset(MSK[:, slot, :], 0.0)
                if eps != 0.0:
                    v.tensor_scalar(MSK[0:64, slot, 2 * fc:2 * fc + 1],
                                    src_col[0:64, :], eps, None, OP.add)
                    p.inc(v.tensor_scalar(
                        MSK[64:128, slot, 2 * fc + 1:2 * fc + 2],
                        src_col[64:128, :], eps, None, OP.add), "sv", 1)
                else:
                    v.tensor_copy(MSK[0:64, slot, 2 * fc:2 * fc + 1],
                                  src_col[0:64, :])
                    p.inc(v.tensor_copy(
                        MSK[64:128, slot, 2 * fc + 1:2 * fc + 2],
                        src_col[64:128, :]), "sv", 1)
            bV(_m)

        def emit_masked_stat(src_cols_fn, eps, rhs_fn, cb_col):
            for fc in range(NC):
                build_mask(fc % 2, fc, src_cols_fn(fc), eps)

                def _mm(t, fc=fc):
                    p.wait(t, "sv")
                    if fc == 0:
                        for i in range(4):
                            bankA.acq(i, t)
                    t.matmul(psT[0:16, cb_col:cb_col + 1], MSK[:, fc % 2, :],
                             ONESC[:, :], start=(fc == 0), stop=(fc == NC - 1))
                    for nt in range(NNT):
                        P_E_INC(t, t.matmul(psA[nt][0:16, :], MSK[:, fc % 2, :],
                                       rhs_fn(fc, nt), start=(fc == 0),
                                       stop=(fc == NC - 1)), "sp", 1)
                    mskp.rel(fc % 2, "sp")
                blk.tensor(_mm)

        def emit_bcast_scale(targ_fn, bstg):
            for fc in range(NC):
                for nt in range(NNT):
                    bi = (fc * NNT + nt) % 2

                    def _bm(t, fc=fc, nt=nt, bi=bi):
                        p.wait(t, "sv")
                        bankB.acq(bi, t)
                        P_E_INC(t, t.matmul(psB[bi][:, :],
                                       M0[0:16, fc * 128:(fc + 1) * 128],
                                       bstg[0:16, nt * 512:(nt + 1) * 512],
                                       start=True, stop=True), "sp", 1)
                    blk.tensor(_bm)

                    def _bvs(sc, bi=bi):
                        p.wait(sc, "sp", "sv")
                        p.inc(sc.copy(T2f[:, :], psB[bi][:, :]), "ss", 1)
                        bankB.rel(bi, "ss")
                    bS(_bvs)

                    def _bv(v, fc=fc, nt=nt):
                        p.wait(v, "ss")
                        targ = targ_fn(fc, nt)
                        p.inc(v.tensor_tensor(targ, targ, T2f[:, :],
                                              OP.mult), "sv", 1)
                    bV(_bv)

        def emit_out_proj(kvb, qf, projT_scr, pbias_row, stage_oa=None):
            with ExitStack() as ost:
                OA = sbt("OA" + str(p.cnt["sp"]), [128, NC, 512], BF16, ost)
                PWs = [sbt("PW" + str(p.cnt["sp"]) + "_" + str(i),
                           [128, NC, 512], BF16, ost) for i in range(2)]
                pwr = projT_scr.rearrange("(ic p) o -> p ic o", p=128)

                def _pwl(s):
                    p.wait(s, "sp", "sg")
                    for on in range(2):
                        p.dma(s.dma_start(
                            out=PWs[on][:, :, :],
                            in_=pwr[:, :, on * 512:(on + 1) * 512]), "sx")
                blk.sync(_pwl)
                oa_guard = [None]
                for nt in range(NNT):
                    for fc in range(NC):
                        bi = fc % 2

                        def _em(t, fc=fc, nt=nt, bi=bi):
                            p.wait(t, "sv")
                            bankB.acq(bi, t)
                            if fc == 0 and oa_guard[0] is not None:
                                p.wait_val(t, oa_guard[0][0], oa_guard[0][1])
                            t.matmul(psB[bi][0:64, :],
                                     kvb[0:64, fc * 64:(fc + 1) * 64],
                                     qf[0:64, fc, nt * 512:(nt + 1) * 512],
                                     start=True, stop=True,
                                     tile_position=(0, 0),
                                     skip_group_check=True)
                            P_E_INC(t, t.matmul(psB[bi][64:128, :],
                                           kvb[64:128, fc * 64:(fc + 1) * 64],
                                           qf[64:128, fc,
                                              nt * 512:(nt + 1) * 512],
                                           start=True, stop=True,
                                           tile_position=(64, 64),
                                           skip_group_check=True), "sp", 1)
                        blk.tensor(_em)

                        def _ev(sc, fc=fc, bi=bi):
                            p.wait(sc, "sp")
                            p.inc(sc.copy(OA[:, fc, :], psB[bi][:, :]), "ss", 1)
                            bankB.rel(bi, "ss")
                        bS(_ev)
                    if stage_oa is not None and nt == 0:
                        tap(stage_oa, OA[:, :, :])
                    for on in range(2):
                        load_brow(on % 2, pbias_row, on * 512, 512)
                        PW = PWs[on]
                        for ts4 in range(4):
                            tt = nt * 4 + ts4
                            bi = ts4 % 4

                            def _pm(t, on=on, ts4=ts4, bi=bi):
                                p.wait(t, "sx", "ss")
                                bankA.acq(bi, t)
                                for ic in range(NC):
                                    t.matmul(psA[bi][:, :],
                                             OA[:, ic, ts4 * 128:(ts4 + 1) * 128],
                                             PW[:, ic, :], start=(ic == 0),
                                             stop=False)
                                P_E_INC(t, t.matmul(psA[bi][:, :], ONESR[0:1, 0:128],
                                               BROW[on % 2][0:1, 0:512],
                                               start=False, stop=True), "sp", 1)
                                if ts4 == 3:
                                    browp.rel(on % 2, "sp")
                            blk.tensor(_pm)

                            def _pe(v, tt=tt, on=on, bi=bi, nt=nt):
                                import os as _o
                                p.wait(v, "sp", "ss", "sx", "sg")
                                xa = X[:, tt, on * 512:(on + 1) * 512]
                                if _o.environ.get("PSUMTAP") and tt == 0 and on == 0:
                                    p.inc(v.tensor_copy(xa, psA[bi][:, :]),
                                          "sv", 1)
                                else:
                                    p.inc(v.tensor_tensor(xa, xa,
                                                          psA[bi][:, :],
                                                          OP.add), "sv", 1)
                                bankA.rel(bi, "sv")
                            bV(_pe)
                    oa_guard[0] = ("sp", p.cnt["sp"])

        def emit_fm_proj(wT_scr, col0, bias_row, bcol0, LG, WG, evac_fn):
            wtr = wT_scr.rearrange("(ic p) o -> p ic o", p=128)
            lg_guard = [None]
            for nt in range(NNT):
                def _lg(s, nt=nt):
                    p.wait(s, "sg")
                    if lg_guard[0] is not None:
                        p.wait_val(s, lg_guard[0][0], lg_guard[0][1])
                    else:
                        p.wait(s, "sp")
                    for ic in range(NC):
                        p.dma(s.dma_start(
                            out=LG[:, ic, :],
                            in_=scr_ln[nt * 512:(nt + 1) * 512,
                                       ic * 128:(ic + 1) * 128],
                            transpose=True), "sx")
                blk.sync(_lg)
                for oc in range(NC):
                    def _wg(s, nt=nt, oc=oc):
                        p.wait(s, "sp")
                        p.dma(s.dma_start(
                            out=WG[:, :, 0:128],
                            in_=wtr[:, :, col0 + oc * 128:col0 + (oc + 1) * 128]),
                            "sx")
                    blk.sync(_wg)
                    load_brow(oc % 2, bias_row, bcol0 + oc * 128, 128)
                    bi = oc % 4

                    def _qm(t, nt=nt, oc=oc, bi=bi):
                        p.wait(t, "sx")
                        bankA.acq(bi, t)
                        for ic in range(NC):
                            t.matmul(psA[bi][:, :], WG[:, ic, 0:128],
                                     LG[:, ic, :], start=(ic == 0), stop=False)
                        P_E_INC(t, t.matmul(psA[bi][:, :], BROW[oc % 2][0:1, 0:128],
                                       ONESR[0:1, 0:512], start=False,
                                       stop=True), "sp", 1)
                        browp.rel(oc % 2, "sp")
                    blk.tensor(_qm)
                    evac_fn(nt, oc, bi)
                lg_guard[0] = ("sp", p.cnt["sp"])

        def emit_tm_proj(w_list, LG, WG, evac_fn):
            lg_guard = [None]
            for ntg in range(NNT):
                def _lg(s, ntg=ntg):
                    p.wait(s, "sg")
                    if lg_guard[0] is not None:
                        p.wait_val(s, lg_guard[0][0], lg_guard[0][1])
                    else:
                        p.wait(s, "sp")
                    for ic in range(NC):
                        p.dma(s.dma_start(
                            out=LG[:, ic, :],
                            in_=scr_ln[ntg * 512:(ntg + 1) * 512,
                                       ic * 128:(ic + 1) * 128],
                            transpose=True), "sx")
                blk.sync(_lg)
                for (wtr, col0, ncols, brow_i, bcol0, tag) in w_list:
                    for on in range(ncols // 512):
                        def _wg(s, wtr=wtr, col0=col0, on=on):
                            p.wait(s, "sp")
                            p.dma(s.dma_start(
                                out=WG[:, :, :],
                                in_=wtr[:, :,
                                        col0 + on * 512:col0 + (on + 1) * 512]),
                                "sx")
                        blk.sync(_wg)
                        load_brow(on % 2, brow_i, bcol0 + on * 512, 512)
                        for t4 in range(4):
                            tt = ntg * 4 + t4
                            bi = t4 % 4

                            def _km(t, on=on, t4=t4, bi=bi):
                                p.wait(t, "sx")
                                bankA.acq(bi, t)
                                for ic in range(NC):
                                    t.matmul(psA[bi][:, :],
                                             LG[:, ic, t4 * 128:(t4 + 1) * 128],
                                             WG[:, ic, :], start=(ic == 0),
                                             stop=False)
                                P_E_INC(t, t.matmul(psA[bi][:, :], ONESR[0:1, 0:128],
                                               BROW[on % 2][0:1, 0:512],
                                               start=False, stop=True), "sp", 1)
                                if t4 == 3:
                                    browp.rel(on % 2, "sp")
                            blk.tensor(_km)
                            evac_fn(tag, on, tt, bi)
                lg_guard[0] = ("sp", p.cnt["sp"])

        def emit_kv_einsum(kt, vt, KVUt):
            for fc in range(NC):
                def _kv(t, fc=fc):
                    p.wait(t, "sv", "ss")
                    a0 = fc * 128
                    for tt in range(NTT):
                        mmA = t.matmul(psK[0:64, :], kt[:, tt, a0:a0 + 64],
                                       vt[:, tt, a0:a0 + 64], start=(tt == 0),
                                       stop=(tt == NTT - 1),
                                       tile_position=(0, 0),
                                       skip_group_check=True)
                        mmB = t.matmul(psK[64:128, :],
                                       kt[:, tt, a0 + 64:a0 + 128],
                                       vt[:, tt, a0 + 64:a0 + 128],
                                       start=(tt == 0), stop=(tt == NTT - 1),
                                       tile_position=(0, 64),
                                       skip_group_check=True)
                        if tt == NTT - 1:
                            P_E_INC(t, mmB)
                blk.tensor(_kv)

                def _ke(sc, fc=fc):
                    p.wait(sc, "sp")
                    p.inc(sc.copy(KVUt[:, fc * 64:(fc + 1) * 64],
                                  psK[:, :]), "ss", 1)
                bS(_ke)

        def emit_tm_colsum(src, OUTC):
            for mc in range(NC):
                def _cs(t, mc=mc):
                    p.wait(t, "sv", "ss")
                    for tt in range(NTT):
                        mm = t.matmul(psT[0:128, mc:mc + 1],
                                      src[:, tt, mc * 128:(mc + 1) * 128],
                                      ONESC[:, :], start=(tt == 0),
                                      stop=(tt == NTT - 1))
                        if tt == NTT - 1:
                            P_E_INC(t, mm)
                blk.tensor(_cs)

            def _ce(sc):
                p.wait(sc, "sp")
                p.inc(sc.copy(OUTC[:, 0:8], psT[0:128, 0:8]), "ss", 1)
            bS(_ce)

        def emit_rows_to_tm(arn_rows, dst):
            for tt in range(NTT):
                def _tr(t, tt=tt):
                    p.wait(t, "sv", "ss")
                    P_E_INC(t, t.transpose(psT[0:128, 0:16],
                                      arn_rows[:, tt * 128:(tt + 1) * 128],
                                      IDENT[64:80, 0:16]), "sp", 1)
                blk.tensor(_tr)

                def _te(sc, tt=tt):
                    p.wait(sc, "sp")
                    p.inc(sc.copy(dst[:, tt, :], psT[0:128, 0:16]),
                          "ss", 1)
                bS(_te)

        barrier("vector", "scalar", "tensor", "gpsimd", "sync")
        tap("x_loaded", X[:, :, :])

        emit_ln("plain", "ln1")
        tap("ln1", scr_ln)

        with ExitStack() as ast:
            QF = sbt("QF", [128, NC, TC], BF16, ast)
            KVB = sbt("KVB", [128, 512], BF16, ast)
            inn = ExitStack()
            KT = sbt("KT", [128, NTT, D], BF16, inn)
            VT = sbt("VT", [128, NTT, D], BF16, inn)
            ARN = sbt("ARN", [128, 2048], F32, inn)
            KVU = sbt("KVU", [128, 512], F32, inn)
            SI = ARN[0:16, :]
            SO = ARN[32:48, :]
            EE = ARN[64:80, :]
            TMP2 = ARN[96:112, :]

            barrier("sync", "gpsimd", "tensor", "vector", "scalar")

            with ExitStack() as qst:
                LG = sbt("LG", [128, NC, 512], BF16, qst)
                WG = sbt("WG", [128, NC, 512], BF16, qst)

                def q_evac(nt, oc, bi):
                    def _e(sc, nt=nt, oc=oc, bi=bi):
                        p.wait(sc, "sp")
                        p.inc(sc.activation(QF[:, oc, nt * 512:(nt + 1) * 512],
                                            psA[bi][:, :], AF.Sigmoid),
                              "ss", 1)
                        bankA.rel(bi, "ss")
                    bS(_e)
                emit_fm_proj(scr_qkvT, 0, 0, 0, LG, WG, q_evac)
                barrier("sync", "gpsimd", "tensor", "vector", "scalar")

                kvw = scr_k.rearrange("(t p) f -> p t f", p=128)

                def kv_evac(tag, on, tt, bi):
                    if tag == "k":
                        def _e(sc, on=on, tt=tt, bi=bi):
                            p.wait(sc, "sp")
                            p.inc(sc.activation(
                                KT[:, tt, on * 512:(on + 1) * 512],
                                psA[bi][:, :], AF.Sigmoid), "ss", 1)
                            bankA.rel(bi, "ss")
                        bS(_e)

                        def _w(g, on=on, tt=tt):
                            p.wait(g, "ss")
                            p.dma(g.dma_start(
                                out=kvw[:, tt, on * 512:(on + 1) * 512],
                                in_=KT[:, tt, on * 512:(on + 1) * 512]), "sg")
                        blk.gpsimd(_w)
                    else:
                        def _e(sc, on=on, tt=tt, bi=bi):
                            p.wait(sc, "sp")
                            p.inc(sc.copy(
                                VT[:, tt, on * 512:(on + 1) * 512],
                                psA[bi][:, :]), "ss", 1)
                            bankA.rel(bi, "ss")
                        bS(_e)

                qkvtr = scr_qkvT.rearrange("(ic p) o -> p ic o", p=128)
                emit_tm_proj([(qkvtr, 1024, 1024, 0, 1024, "k"),
                              (qkvtr, 2048, 1024, 0, 2048, "v")], LG, WG,
                             kv_evac)

            tap("q_fm", QF[:, :, :])
            tap("k_tm", KT[:, :, :])
            tap("v_tm", VT[:, :, :])

            mst = ExitStack()
            BSTG = sbt("BSTG", [16, 2048], BF16, mst)
            BSTG2 = sbt("BSTG2", [16, 2048], BF16, mst)

            emit_tm_colsum(KT, KSUM)

            def _qs(v):
                p.wait(v, "ss")
                for fc in range(NC):
                    v.reduce_sum(QSUM[:, fc:fc + 1], QF[:, fc, :], AX)
                p.inc(v.engine_nop(), "sv", 1)
            bV(_qs)

            def _pack1(v):
                p.wait(v, "sv")
                v.tensor_copy(T1f[:, 0:8], KSUM[:, :])
                p.inc(v.tensor_copy(T1f[:, 8:16], QSUM[:, :]), "sv", 1)
            bV(_pack1)
            emit_allreduce([(T1f[:, 0:16], cc_a_i, cc_a_o)])

            def _unpack1(v):
                p.wait(v, "sg")
                v.tensor_copy(KSUM[:, :], T1f[:, 0:8])
                p.inc(v.tensor_copy(QSUM[:, :], T1f[:, 8:16]), "sv", 1)
            bV(_unpack1)
            tap("ksum", KSUM[:, :])
            tap("qsum", QSUM[:, :])

            emit_masked_stat(lambda fc: KSUM[:, fc:fc + 1], EPS,
                             lambda fc, nt: QF[:, fc, nt * 512:(nt + 1) * 512],
                             0)

            def _si(sc):
                p.wait(sc, "sp")
                sc.activation(CB[0:16, 0:1], psT[0:16, 0:1], AF.Identity, scale=EPS)
                for nt in range(NNT):
                    sc.activation(EE[:, nt * 512:(nt + 1) * 512],
                                  psA[nt][0:16, :], AF.Identity,
                                  bias=CB[0:16, 0:1])
                p.inc(snop(sc), "ss", 1)
            bS(_si)
            for i in range(4):
                bankA.rel(i, "ss")

            def _sir(sc):
                p.wait(sc, "ss")
                sc.activation(SO[:, :], EE[:, :], AF.Ln)
                p.inc(sc.activation(SI[:, :], SO[:, :], AF.Exp, scale=-1.0),
                      "ss", 1)
            bS(_sir)
            tap("si", SI)

            with ExitStack() as kst:
                KFG = sbt("KFG", [128, NC, 512], BF16, kst)

                def kfg_load(nt):
                    def _k(s, nt=nt):
                        p.wait(s, "sp", "sg")
                        for ic in range(NC):
                            p.dma(s.dma_start(
                                out=KFG[:, ic, :],
                                in_=scr_k[nt * 512:(nt + 1) * 512,
                                          ic * 128:(ic + 1) * 128],
                                transpose=True), "sx")
                    blk.sync(_k)

                for nt in range(NNT):
                    kfg_load(nt)
                    for fc in range(NC):
                        build_mask(fc % 2, fc, QSUM[:, fc:fc + 1], EPS)

                        def _mm(t, fc=fc, nt=nt):
                            p.wait(t, "sv", "sx")
                            if fc == 0:
                                bankA.acq(nt, t)
                            if nt == 0:
                                t.matmul(psT[0:16, 1:2], MSK[:, fc % 2, :],
                                         ONESC[:, :], start=(fc == 0),
                                         stop=(fc == NC - 1))
                            P_E_INC(t, t.matmul(psA[nt][0:16, :], MSK[:, fc % 2, :],
                                           KFG[:, fc, :], start=(fc == 0),
                                           stop=(fc == NC - 1)), "sp", 1)
                            mskp.rel(fc % 2, "sp")
                        blk.tensor(_mm)

                def _so(sc):
                    p.wait(sc, "sp")
                    sc.activation(CB[0:16, 1:2], psT[0:16, 1:2], AF.Identity,
                                  scale=EPS)
                    for nt in range(NNT):
                        sc.activation(EE[:, nt * 512:(nt + 1) * 512],
                                      psA[nt][0:16, :], AF.Identity,
                                      bias=CB[0:16, 1:2])
                    p.inc(snop(sc), "ss", 1)
                bS(_so)
                for i in range(4):
                    bankA.rel(i, "ss")

                def _sor(sc):
                    p.wait(sc, "ss")
                    sc.activation(TMP2[:, :], EE[:, :], AF.Ln)
                    p.inc(sc.activation(SO[:, :], TMP2[:, :], AF.Exp,
                                        scale=-1.0), "ss", 1)
                bS(_sor)
                tap("so", SO)

                def _sib(v):
                    p.wait(v, "sv")
                    p.inc(v.tensor_copy(BSTG[0:16, :], SI[:, :]), "sv", 1)
                bV(_sib)

                def emit_col_wsum(src3d, OUTC):
                    for fc in range(NC):
                        for nt in range(NNT):
                            bi = nt % 2

                            def _bm(t, fc=fc, nt=nt, bi=bi):
                                p.wait(t, "sv")
                                bankB.acq(bi, t)
                                P_E_INC(t, t.matmul(
                                    psB[bi][:, :],
                                    M0[0:16, fc * 128:(fc + 1) * 128],
                                    BSTG[0:16, nt * 512:(nt + 1) * 512],
                                    start=True, stop=True), "sp", 1)
                            blk.tensor(_bm)

                            def _trs(sc, bi=bi):
                                p.wait(sc, "sp", "sv")
                                p.inc(sc.copy(T2f[:, :], psB[bi][:, :]),
                                      "ss", 1)
                                bankB.rel(bi, "ss")
                            bS(_trs)

                            def _tr(v, fc=fc, nt=nt):
                                p.wait(v, "ss")
                                v.tensor_tensor(QS1[:, :], src3d(fc, nt),
                                                T2f[:, :], OP.mult)
                                p.inc(v.reduce_sum(STAT[:, nt:nt + 1],
                                                   QS1[:, :], AX), "sv", 1)
                            bV(_tr)

                        def _fin(v, fc=fc):
                            p.inc(v.reduce_sum(OUTC[:, fc:fc + 1],
                                               STAT[:, 0:4], AX), "sv", 1)
                        bV(_fin)

                emit_col_wsum(
                    lambda fc, nt: QF[:, fc, nt * 512:(nt + 1) * 512], QSI)
                tap("qsi", QSI[:, :])

                def _sob(v):
                    p.wait(v, "sv")
                    p.inc(v.tensor_copy(BSTG[0:16, :], SO[:, :]), "sv", 1)
                bV(_sob)
                for nt in range(NNT):
                    kfg_load(nt)
                    for fc in range(NC):
                        bi = fc % 2

                        def _bm2(t, fc=fc, nt=nt, bi=bi):
                            p.wait(t, "sv", "sx")
                            bankB.acq(bi, t)
                            P_E_INC(t, t.matmul(
                                psB[bi][:, :],
                                M0[0:16, fc * 128:(fc + 1) * 128],
                                BSTG[0:16, nt * 512:(nt + 1) * 512],
                                start=True, stop=True), "sp", 1)
                        blk.tensor(_bm2)

                        def _tr2s(sc, bi=bi):
                            p.wait(sc, "sp", "sv")
                            p.inc(sc.copy(T2f[:, :], psB[bi][:, :]), "ss", 1)
                            bankB.rel(bi, "ss")
                        bS(_tr2s)

                        def _tr2(v, fc=fc, nt=nt):
                            p.wait(v, "ss")
                            v.tensor_tensor(QS1[:, :], KFG[:, fc, :],
                                            T2f[:, :], OP.mult)
                            p.inc(v.reduce_sum(STAT[:, 4 + fc:5 + fc],
                                               QS1[:, :], AX), "sv", 1)
                        bV(_tr2)

                        def _tr2f(v, fc=fc, nt=nt):
                            if nt == 0:
                                p.inc(v.tensor_copy(KSOC[:, fc:fc + 1],
                                                    STAT[:, 4 + fc:5 + fc]),
                                      "sv", 1)
                            else:
                                p.inc(v.tensor_tensor(KSOC[:, fc:fc + 1],
                                                      KSOC[:, fc:fc + 1],
                                                      STAT[:, 4 + fc:5 + fc],
                                                      OP.add), "sv", 1)
                        bV(_tr2f)
                tap("kso", KSOC[:, :])

                def _pack2(v):
                    p.wait(v, "sv")
                    v.tensor_copy(T1f[:, 0:8], KSOC[:, :])
                    p.inc(v.tensor_copy(T1f[:, 8:16], QSI[:, :]), "sv", 1)
                bV(_pack2)
                emit_allreduce([(T1f[:, 0:16], cc_a_i, cc_a_o)])

                def _unpack2(v):
                    p.wait(v, "sg")
                    v.tensor_copy(KSOC[:, :], T1f[:, 0:8])
                    p.inc(v.tensor_copy(QSI[:, :], T1f[:, 8:16]), "sv", 1)
                bV(_unpack2)

                emit_masked_stat(lambda fc: KSOC[:, fc:fc + 1], EPS,
                                 lambda fc, nt: QF[:, fc,
                                                   nt * 512:(nt + 1) * 512], 2)

                def _sa(sc):
                    p.wait(sc, "sp")
                    sc.activation(CB[0:16, 2:3], psT[0:16, 2:3], AF.Identity,
                                  scale=EPS, bias=EPS)
                    for nt in range(NNT):
                        sc.activation(BSTG2[0:16, nt * 512:(nt + 1) * 512],
                                      psA[nt][0:16, :], AF.Sigmoid,
                                      bias=CB[0:16, 2:3])
                    p.inc(snop(sc), "ss", 1)
                bS(_sa)
                for i in range(4):
                    bankA.rel(i, "ss")
                tap("sa", BSTG2[0:16, :])

                for nt in range(NNT):
                    kfg_load(nt)
                    for fc in range(NC):
                        build_mask(fc % 2, fc, QSI[:, fc:fc + 1], EPS)

                        def _mm2(t, fc=fc, nt=nt):
                            p.wait(t, "sv", "sx")
                            if fc == 0:
                                bankA.acq(nt, t)
                            if nt == 0:
                                t.matmul(psT[0:16, 3:4], MSK[:, fc % 2, :],
                                         ONESC[:, :], start=(fc == 0),
                                         stop=(fc == NC - 1))
                            P_E_INC(t, t.matmul(psA[nt][0:16, :], MSK[:, fc % 2, :],
                                           KFG[:, fc, :], start=(fc == 0),
                                           stop=(fc == NC - 1)), "sp", 1)
                            mskp.rel(fc % 2, "sp")
                        blk.tensor(_mm2)

                def _cs2(sc):
                    p.wait(sc, "sp")
                    sc.activation(CB[0:16, 3:4], psT[0:16, 3:4], AF.Identity,
                                  scale=EPS, bias=EPS)
                    for nt in range(NNT):
                        sc.activation(SO[:, nt * 512:(nt + 1) * 512],
                                      psA[nt][0:16, :], AF.Identity,
                                      bias=CB[0:16, 3:4])
                    p.inc(snop(sc), "ss", 1)
                bS(_cs2)
                for i in range(4):
                    bankA.rel(i, "ss")

                def _clip(v):
                    p.wait(v, "ss")
                    p.inc(v.tensor_scalar(SO[:, :], SO[:, :], -1.0, 1.0,
                                          OP.max, OP.min), "sv", 1)
                bV(_clip)

                def _exp(sc):
                    p.wait(sc, "sv")
                    p.inc(sc.activation(EE[:, :], SO[:, :], AF.Exp), "ss", 1)
                bS(_exp)

                def _es(v):
                    p.wait(v, "ss")
                    p.inc(v.reduce_sum(ESCOL[0:16, 0:1], EE[:, :], AX),
                          "sv", 1)
                bV(_es)
                tap("e", EE)

            est = ExitStack()
            ETM = sbt("ETM", [128, NTT, 16], F32, est)
            emit_rows_to_tm(EE, ETM)
            for tt in range(NTT):
                def _ve(v, tt=tt):
                    p.wait(v, "sv")
                    va = VT[:, tt, :].rearrange("p (s d) -> p s d", d=64)
                    ea = ETM[:, tt, :].rearrange("p (s o) -> p s o", o=1)
                    p.inc(v.tensor_tensor(va, va,
                                          ea.broadcast_to([128, 16, 64]),
                                          OP.mult), "sv", 1)
                bV(_ve)
            barrier("vector")
            est.close()

            emit_kv_einsum(KT, VT, KVU)
            tap("kvu_pre", KVU[:, :])

            emit_allreduce([(KVU[:, :], cc_d_i, cc_d_o),
                            (ESCOL[0:16, 0:4], cc_e_i, cc_e_o)])

            def _norm(v):
                p.wait(v, "sg")
                p.inc(v.tensor_copy(KVB[:, :], KVU[:, :]), "sv", 1)
            bV(_norm)

            def _norm2(sc):
                p.wait(sc, "sg")
                sc.activation(ESCOL[0:16, 2:3], ESCOL[0:16, 0:1], AF.Ln)
                p.inc(sc.activation(ESCOL[0:16, 1:2], ESCOL[0:16, 2:3],
                                    AF.Exp, scale=-1.0, bias=8.317766166719343), "ss", 1)
            bS(_norm2)
            tap("kvu", KVU[:, :])

            def _scq(v):
                p.wait(v, "sv", "ss")
                v.tensor_scalar(SI[:, :], SI[:, :], ESCOL[0:16, 1:2], None,
                                OP.mult)
                p.inc(v.tensor_tensor(BSTG[0:16, :], SI[:, :], BSTG2[0:16, :],
                                      OP.mult), "sv", 1)
            bV(_scq)
            tap("norm", ESCOL[0:16, 0:4])
            tap("scq", BSTG[0:16, :])
            emit_bcast_scale(lambda fc, nt: QF[:, fc, nt * 512:(nt + 1) * 512], BSTG)
            tap("qfin", QF[:, :, :])
            mst.close()

            tap("x0b", X[:, :, :])
            barrier("sync", "gpsimd", "tensor", "vector", "scalar")
            inn.close()
            emit_out_proj(KVB, QF, scr_projT, 1, stage_oa="outa")

        tap("x1", X[:, :, :])

        def emit_ffn(tag):
            emit_ln("quant")
            if tag == 1:
                tap("xq", scr_ln)
            with ExitStack() as fst:
                W2R = sbt("W2R%d" % tag, [128, NHC, 1024], BF16, fst)
                HQ = sbt("HQ%d" % tag, [128, NHC, 512], BF16, fst)
                XQG = sbt("XQG%d" % tag, [128, NC, 512], BF16, fst)
                W1G = [sbt("W1Ga%d" % tag, [128, NC, 512], BF16, fst),
                       sbt("W1Gb%d" % tag, [128, NC, 512], BF16, fst)]
                w1gp = Pool(p, 2)

                barrier("sync", "gpsimd", "tensor", "vector", "scalar")

                def _w2(s):
                    for hc in range(NHC):
                        p.dma(s.dma_start(
                            out=W2R[:, hc, :],
                            in_=scr_w2q[:, hc * 128:(hc + 1) * 128],
                            transpose=True), "sx")
                blk.sync(_w2)
                w1tr = scr_w1T.rearrange("(ic p) h -> p ic h", p=128)

                xq_guard = [None]
                for tau in range(NNT):
                    def _xq(s, tau=tau):
                        p.wait(s, "sg")
                        if xq_guard[0] is not None:
                            p.wait_val(s, xq_guard[0][0], xq_guard[0][1])
                        for ic in range(NC):
                            p.dma(s.dma_start(
                                out=XQG[:, ic, :],
                                in_=scr_ln[tau * 512:(tau + 1) * 512,
                                           ic * 128:(ic + 1) * 128],
                                transpose=True), "sx")
                    blk.sync(_xq)
                    if tag == 1 and tau == 0:
                        tap("b1s", B1S[:, :])
                        tap("xqg", XQG[:, :, :])

                    for hg in range(8):
                        def _w1(s, tau=tau, hg=hg):
                            w1gp.acq(hg % 2, s)
                            p.dma(s.dma_start(
                                out=W1G[hg % 2][:, :, :],
                                in_=w1tr[:, :, hg * 512:(hg + 1) * 512]), "sx")
                        blk.sync(_w1)
                        for h4 in range(4):
                            hc = hg * 4 + h4
                            bi = hc % 4

                            def _m1(t, hg=hg, h4=h4, bi=bi):
                                p.wait(t, "sx")
                                bankA.acq(bi, t)
                                for ic in range(NC):
                                    mm = t.matmul(
                                        psA[bi][:, :],
                                        W1G[hg % 2][:, ic,
                                                    h4 * 128:(h4 + 1) * 128],
                                        XQG[:, ic, :], start=(ic == 0),
                                        stop=(ic == NC - 1))
                                P_E_INC(t, mm)
                                if h4 == 3:
                                    w1gp.rel(hg % 2, "sp")
                            blk.tensor(_m1)

                            def _e1(sc, hc=hc, bi=bi):
                                p.wait(sc, "sp", "sv")
                                p.inc(sc.activation(QS1[:, :], psA[bi][:, :],
                                                    AF.Identity, scale=1024.0,
                                                    bias=B1S[:, hc:hc + 1]),
                                      "ss", 1)
                                bankA.rel(bi, "ss")
                            bS(_e1)
                            if tag == 1 and tau == 0 and hc == 0:
                                tap("h0", QS1[:, 0:512])

                            def _e2(v, hc=hc):
                                p.wait(v, "ss")
                                AIv = QS2.ap().bitcast(I32)[:, 0:512]
                                v.tensor_copy(NIi[:, :], QS1[:, :])
                                v.tensor_scalar(AIv, NIi[:, :], -128, 6,
                                                OP.bitwise_and,
                                                OP.arith_shift_right)
                                v.tensor_scalar(NIi[:, :], NIi[:, :], 127,
                                                None, OP.bitwise_and)
                                v.scalar_tensor_tensor(QS1[:, :], NIi[:, :],
                                                       65.0, AIv,
                                                       OP.is_ge, OP.add)
                                p.inc(v.tensor_scalar(HQ[:, hc, :], QS1[:, :],
                                                      0.0, 127.0, OP.max,
                                                      OP.min), "sv", 1)
                            bV(_e2)
                    if tag == 1:
                        if tau == 0:
                            tap("hq", HQ[:, :, :])

                    for ts4 in range(4):
                        tt = tau * 4 + ts4
                        for on in range(2):
                            load_brow(on % 2, 6, on * 512, 512)
                            bi = (ts4 * 2 + on) % 4

                            def _m2(t, ts4=ts4, on=on, bi=bi):
                                p.wait(t, "sv", "sx")
                                bankA.acq(bi, t)
                                for hc in range(NHC):
                                    t.matmul(
                                        psA[bi][:, :],
                                        HQ[:, hc, ts4 * 128:(ts4 + 1) * 128],
                                        W2R[:, hc, on * 512:(on + 1) * 512],
                                        start=(hc == 0), stop=False)
                                P_E_INC(t, t.matmul(psA[bi][:, :], ONESR[0:1, 0:128],
                                               BROW[on % 2][0:1, 0:512],
                                               start=False, stop=True),
                                      "sp", 1)
                                browp.rel(on % 2, "sp")
                            blk.tensor(_m2)

                            def _r2(v, tt=tt, on=on, bi=bi):
                                p.wait(v, "sp", "ss", "sx", "sg")
                                xa = X[:, tt, on * 512:(on + 1) * 512]
                                p.inc(v.tensor_tensor(xa, xa, psA[bi][:, :],
                                                      OP.add), "sv", 1)
                                bankA.rel(bi, "sv")
                            bV(_r2)
                    xq_guard[0] = ("sp", p.cnt["sp"])

        tap("w1t", scr_w1T)
        tap("w2q", scr_w2q)
        emit_ffn(1)
        tap("x2", X[:, :, :])

        emit_ln("plain")
        with ExitStack() as bst:
            QF = sbt("QFb", [128, NC, TC], BF16, bst)
            KVB = sbt("KVBb", [128, 512], BF16, bst)
            innb = ExitStack()
            KT = sbt("KTb", [128, NTT, D], BF16, innb)
            VT = sbt("VTb", [128, NTT, D], BF16, innb)
            ARN = sbt("ARNb", [128, 2048], F32, innb)
            KVU = sbt("KVUb", [128, 512], F32, innb)
            ZZ = ARN[0:16, :]

            barrier("sync", "gpsimd", "tensor", "vector", "scalar")

            with ExitStack() as qst:
                LG = sbt("LGb", [128, NC, 512], BF16, qst)
                WG = sbt("WGb", [128, NC, 512], BF16, qst)

                def fm_evac(dst_ap, bi):
                    def _e1(sc, bi=bi):
                        p.wait(sc, "sp", "sv")
                        sc.activation(T2f[:, :], psA[bi][:, :], AF.Relu,
                                      scale=-1.0)
                        sc.activation(QS1[:, :], T2f[:, :], AF.Exp,
                                      scale=-1.0)
                        p.inc(sc.activation(QS2[:, :], psA[bi][:, :],
                                            AF.Relu), "ss", 1)
                        bankA.rel(bi, "ss")
                    bS(_e1)

                    def _e3(v, dst_ap=dst_ap):
                        p.wait(v, "ss")
                        p.inc(v.tensor_tensor(dst_ap, QS2[:, :], QS1[:, :],
                                              OP.add), "sv", 1)
                    bV(_e3)

                def q_evac(nt, oc, bi):
                    fm_evac(QF[:, oc, nt * 512:(nt + 1) * 512], bi)
                emit_fm_proj(scr_wq, 0, 2, 0, LG, WG, q_evac)

                def kv_evac(tag, on, tt, bi):
                    if tag == "k":
                        fm_evac(KT[:, tt, on * 512:(on + 1) * 512], bi)
                    else:
                        def _e(sc, on=on, tt=tt, bi=bi):
                            p.wait(sc, "sp")
                            p.inc(sc.copy(
                                VT[:, tt, on * 512:(on + 1) * 512],
                                psA[bi][:, :]), "ss", 1)
                            bankA.rel(bi, "ss")
                        bS(_e)

                wkr = scr_wk.rearrange("(ic p) o -> p ic o", p=128)
                wvr = scr_wv.rearrange("(ic p) o -> p ic o", p=128)
                emit_tm_proj([(wkr, 0, 1024, 3, 0, "k"),
                              (wvr, 0, 1024, 4, 0, "v")], LG, WG, kv_evac)

            tap("bqm", QF[:, :, :])
            tap("bkm", KT[:, :, :])

            mstb = ExitStack()
            BSTG = sbt("BSTGb", [16, 2048], BF16, mstb)

            emit_tm_colsum(KT, KSUM)
            emit_kv_einsum(KT, VT, KVU)

            def _pk(v):
                p.wait(v, "sv", "ss")
                p.inc(v.engine_nop(), "sv", 1)
            bV(_pk)
            emit_allreduce([(KSUM[:, :], cc_c_i, cc_c_o),
                            (KVU[:, :], cc_d_i, cc_d_o)])
            tap("bkvu", KVU[:, :])

            def _kvb(v):
                p.wait(v, "sg")
                p.inc(v.tensor_copy(KVB[:, :], KVU[:, :]), "sv", 1)
            bV(_kvb)

            emit_masked_stat(lambda fc: KSUM[:, fc:fc + 1], 0.0,
                             lambda fc, nt: QF[:, fc, nt * 512:(nt + 1) * 512],
                             4)

            ZT = ARN[32:48, :]

            def _z(sc):
                p.wait(sc, "sp")
                for nt in range(NNT):
                    sc.activation(ZT[:, nt * 512:(nt + 1) * 512],
                                  psA[nt][0:16, :], AF.Identity, bias=EPS)
                p.inc(snop(sc), "ss", 1)
            bS(_z)
            for i in range(4):
                bankA.rel(i, "ss")

            def _zr(sc):
                p.wait(sc, "ss")
                sc.activation(ARN[64:80, :], ZT[:, :], AF.Ln)
                p.inc(sc.activation(ZZ[:, :], ARN[64:80, :], AF.Exp,
                                    scale=-1.0), "ss", 1)
            bS(_zr)

            def _zrc(v):
                p.wait(v, "ss")
                p.inc(v.tensor_copy(BSTG[0:16, :], ZZ[:, :]), "sv", 1)
            bV(_zrc)
            tap("bz", ZZ)

            emit_bcast_scale(lambda fc, nt: QF[:, fc, nt * 512:(nt + 1) * 512], BSTG)
            mstb.close()
            barrier("sync", "gpsimd", "tensor", "vector", "scalar")
            innb.close()
            emit_out_proj(KVB, QF, scr_wo, 5)

        tap("x3", X[:, :, :])

        emit_ffn(2)
        tap("x4", X[:, :, :])

        def _stv(g):
            p.wait(g, "sv", "ss", "sp")
            yw = y_d.rearrange("(t p) f -> p t f", p=128)
            for g4 in range(4):
                p.dma(g.dma_start(out=yw[:, 4 * g4:4 * g4 + 4, :],
                                  in_=X[:, 4 * g4:4 * g4 + 4, :]), "sg")
            g.wait_ge(p.h["sg"], p.cnt["sg"])
        blk.gpsimd(_stv)

    return nc


def kernel(**inputs):
    global _DBG_RESULT
    x = np.ascontiguousarray(np.asarray(inputs["L0_attn_a_x"], dtype=np.float32))
    common = {
        "aqkv_w": np.ascontiguousarray(inputs["L0_attn_a_qkv_w"], dtype=np.float32),
        "aqkv_b": np.ascontiguousarray(inputs["L0_attn_a_qkv_b"], dtype=np.float32).reshape(1, -1),
        "aproj_w": np.ascontiguousarray(inputs["L0_attn_a_proj_w"], dtype=np.float32),
        "aproj_b": np.ascontiguousarray(inputs["L0_attn_a_proj_b"], dtype=np.float32).reshape(1, -1),
        "b_wq": np.ascontiguousarray(np.asarray(inputs["L0_attn_b_wq"], dtype=np.float32).reshape(D, D)),
        "b_wk": np.ascontiguousarray(np.asarray(inputs["L0_attn_b_wk"], dtype=np.float32).reshape(D, D)),
        "b_wv": np.ascontiguousarray(np.asarray(inputs["L0_attn_b_wv"], dtype=np.float32).reshape(D, D)),
        "b_wo": np.ascontiguousarray(np.asarray(inputs["L0_attn_b_wo"], dtype=np.float32).reshape(D, D)),
        "b_bq": np.ascontiguousarray(np.asarray(inputs["L0_attn_b_bq"], dtype=np.float32).reshape(1, -1)),
        "b_bk": np.ascontiguousarray(np.asarray(inputs["L0_attn_b_bk"], dtype=np.float32).reshape(1, -1)),
        "b_bv": np.ascontiguousarray(np.asarray(inputs["L0_attn_b_bv"], dtype=np.float32).reshape(1, -1)),
        "b_bo": np.ascontiguousarray(np.asarray(inputs["L0_attn_b_bo"], dtype=np.float32).reshape(1, -1)),
        "f_w1": np.ascontiguousarray(inputs["L0_ffn_w1"], dtype=np.float32),
        "f_b1": np.ascontiguousarray(
            np.asarray(inputs["L0_ffn_b1"], dtype=np.float32).reshape(NHC, 128).T),
        "f_w2": np.ascontiguousarray(inputs["L0_ffn_w2"], dtype=np.float32),
        "f_b2": np.ascontiguousarray(np.asarray(inputs["L0_ffn_b2"], dtype=np.float32).reshape(1, -1)),
    }
    in_maps = []
    for c in range(8):
        bb, half = c // 2, c % 2
        m = dict(common)
        m["x_in"] = np.ascontiguousarray(x[bb, half * TC:(half + 1) * TC, :])
        in_maps.append(m)
    nc = build_core_program(dbg_stage=_DBG_STAGE)
    res = run_bass_kernel_spmd(nc, in_maps, core_ids=list(range(8)))
    out = np.empty((B, S, D), dtype=np.float32)
    for c in range(8):
        bb, half = c // 2, c % 2
        out[bb, half * TC:(half + 1) * TC, :] = res.results[c]["y_out"]
    if _DBG_STAGE is not None:
        _DBG_RESULT = [res.results[c].get("dbg") for c in range(8)]
    return out


# version 03
# speedup vs baseline: 1.0445x; 1.0445x over previous
import numpy as np
from contextlib import ExitStack

import concourse.bass as bass
import concourse.mybir as mybir
from concourse.bass_utils import run_bass_kernel_spmd

F32 = mybir.dt.float32
BF16 = mybir.dt.bfloat16
I32 = mybir.dt.int32
OP = mybir.AluOpType
AF = mybir.ActivationFunctionType
AX = mybir.AxisListType.X

B, S, D, H, HD = 4, 4096, 1024, 16, 64
TC = 2048
NTT = 16
NNT = 4
NC = 8
HID = 4096
NHC = 32
EPS = 1e-6
LNEPS = 1e-5
RG = [[0, 1], [2, 3], [4, 5], [6, 7]]

_DBG_STAGE = None
_DBG_RESULT = None
_TAP_SHAPES = {
    "x_loaded": [128, NTT, D],
    "ln1": [TC, D],
    "ln1stat": [128, 64],
    "q_fm": [128, NC, TC],
    "k_tm": [128, NTT, D],
    "v_tm": [128, NTT, D],
    "ksum": [128, 8],
    "qsum": [128, 8],
    "si": [16, TC],
    "so": [16, TC],
    "kso": [128, 8],
    "qsi": [128, 8],
    "sa": [16, TC],
    "e": [16, TC],
    "kvu_pre": [128, 512],
    "kvu": [128, 512],
    "norm": [16, 4],
    "scq": [16, TC],
    "qfin": [128, NC, TC],
    "outa": [128, NC, 512],
    "projw": [D, D],
    "w1t": [D, HID],
    "w2q": [D, HID],
    "x0b": [128, NTT, D],
    "x1": [128, NTT, D],
    "xq": [TC, D],
    "b1s": [128, NHC],
    "xqg": [128, NC, 512],
    "h0": [128, 512],
    "hq": [128, NHC, 512],
    "x2": [128, NTT, D],
    "bqm": [128, NC, TC],
    "bkm": [128, NTT, D],
    "bz": [16, TC],
    "bkvu": [128, 512],
    "x3": [128, NTT, D],
    "x4": [128, NTT, D],
}


class Prog:
    def __init__(self, nc, stack, sem_names):
        self.nc = nc
        self.h = {n: stack.enter_context(nc.semaphore(n)) for n in sem_names}
        self.cnt = {n: 0 for n in sem_names}
        self._auto = set()

    def inc(self, instr, name, k=1):
        key = (id(instr.ins), name)
        if key in self._auto:
            return instr
        instr.then_inc(self.h[name], k)
        self.cnt[name] += k
        return instr

    def dma(self, instr, name):
        return self.inc(instr, name, 16)

    def wait(self, eng, *names):
        for n in names:
            if self.cnt[n] > 0:
                eng.wait_ge(self.h[n], self.cnt[n])

    def wait_val(self, eng, name, val):
        if val > 0:
            eng.wait_ge(self.h[name], val)


class Pool:

    def __init__(self, p, n):
        self.p = p
        self.st = [None] * n

    def acq(self, i, eng):
        s = self.st[i % len(self.st)]
        if s is not None:
            self.p.wait_val(eng, s[0], s[1])

    def rel(self, i, sem):
        self.st[i % len(self.st)] = (sem, self.p.cnt[sem])


def build_core_program(dbg_stage=None):
    nc = bass.Bass("TRN2", target_bir_lowering=False, debug=False)

    x_d = nc.dram_tensor("x_in", [TC, D], F32, kind="ExternalInput").ap()
    qkvw_d = nc.dram_tensor("aqkv_w", [3 * D, D], F32, kind="ExternalInput").ap()
    qkvb_d = nc.dram_tensor("aqkv_b", [1, 3 * D], F32, kind="ExternalInput").ap()
    projw_d = nc.dram_tensor("aproj_w", [D, D], F32, kind="ExternalInput").ap()
    projb_d = nc.dram_tensor("aproj_b", [1, D], F32, kind="ExternalInput").ap()
    bwq_d = nc.dram_tensor("b_wq", [D, D], F32, kind="ExternalInput").ap()
    bwk_d = nc.dram_tensor("b_wk", [D, D], F32, kind="ExternalInput").ap()
    bwv_d = nc.dram_tensor("b_wv", [D, D], F32, kind="ExternalInput").ap()
    bwo_d = nc.dram_tensor("b_wo", [D, D], F32, kind="ExternalInput").ap()
    bbq_d = nc.dram_tensor("b_bq", [1, D], F32, kind="ExternalInput").ap()
    bbk_d = nc.dram_tensor("b_bk", [1, D], F32, kind="ExternalInput").ap()
    bbv_d = nc.dram_tensor("b_bv", [1, D], F32, kind="ExternalInput").ap()
    bbo_d = nc.dram_tensor("b_bo", [1, D], F32, kind="ExternalInput").ap()
    w1_d = nc.dram_tensor("f_w1", [HID, D], F32, kind="ExternalInput").ap()
    b1_d = nc.dram_tensor("f_b1", [128, NHC], F32, kind="ExternalInput").ap()
    w2_d = nc.dram_tensor("f_w2", [D, HID], F32, kind="ExternalInput").ap()
    b2_d = nc.dram_tensor("f_b2", [1, D], F32, kind="ExternalInput").ap()
    y_d = nc.dram_tensor("y_out", [TC, D], F32, kind="ExternalOutput").ap()

    scr_ln = nc.dram_tensor("scr_ln", [TC, D], BF16, kind="Internal").ap()
    scr_k = nc.dram_tensor("scr_k", [TC, D], BF16, kind="Internal").ap()
    scr_tmp = nc.dram_tensor("scr_tmp", [HID, D], BF16, kind="Internal").ap()
    scr_qkvT = nc.dram_tensor("scr_qkvT", [D, 3 * D], BF16, kind="Internal").ap()
    scr_projT = nc.dram_tensor("scr_projT", [D, D], BF16, kind="Internal").ap()
    scr_w1T = nc.dram_tensor("scr_w1T", [D, HID], BF16, kind="Internal").ap()
    scr_w2q = nc.dram_tensor("scr_w2q", [D, HID], BF16, kind="Internal").ap()
    scr_wq = nc.dram_tensor("scr_wq", [D, D], BF16, kind="Internal").ap()
    scr_wk = nc.dram_tensor("scr_wk", [D, D], BF16, kind="Internal").ap()
    scr_wv = nc.dram_tensor("scr_wv", [D, D], BF16, kind="Internal").ap()
    scr_wo = nc.dram_tensor("scr_wo", [D, D], BF16, kind="Internal").ap()
    scr_bias = nc.dram_tensor("scr_bias", [8, 3 * D], BF16, kind="Internal").ap()
    cc_a_i = nc.dram_tensor("cc_a_i", [128, 16], F32, kind="Internal").ap()
    cc_a_o = nc.dram_tensor("cc_a_o", [128, 16], F32, kind="Internal").ap()
    cc_b_i = nc.dram_tensor("cc_b_i", [16, 1024], F32, kind="Internal").ap()
    cc_b_o = nc.dram_tensor("cc_b_o", [16, 1024], F32, kind="Internal").ap()
    cc_c_i = nc.dram_tensor("cc_c_i", [128, 8], F32, kind="Internal").ap()
    cc_c_o = nc.dram_tensor("cc_c_o", [128, 8], F32, kind="Internal").ap()
    cc_d_i = nc.dram_tensor("cc_d_i", [128, 512], F32, kind="Internal").ap()
    cc_d_o = nc.dram_tensor("cc_d_o", [128, 512], F32, kind="Internal").ap()
    cc_e_i = nc.dram_tensor("cc_e_i", [16, 4], F32, kind="Internal").ap()
    cc_e_o = nc.dram_tensor("cc_e_o", [16, 4], F32, kind="Internal").ap()
    dbg_d = None
    if dbg_stage is not None:
        dbg_d = nc.dram_tensor("dbg", _TAP_SHAPES[dbg_stage], F32,
                               kind="ExternalOutput").ap()

    with ExitStack() as top:
        def sbt(name, shape, dt, stk):
            return stk.enter_context(nc.sbuf_tensor(name, shape, dt))

        sb = lambda name, shape, dt: sbt(name, shape, dt, top)

        X = sb("X", [128, NTT, D], F32)
        T1f = sb("T1f", [128, 1024], F32)
        T2f = sb("T2f", [128, 512], F32)
        QS1 = sb("QS1", [128, 512], F32)
        QS2 = sb("QS2", [128, 512], F32)
        NIi = sb("NIi", [128, 1024], I32)
        STAT = sb("STAT", [128, 64], F32)
        TA = sb("TA", [128, 16], F32)
        TB = sb("TB", [128, 16], F32)
        TCH = sb("TCH", [128, 16], F32)
        IDENT = sb("IDENT", [128, 16], F32)
        IOTC = sb("IOTC", [128, 1], F32)
        M0 = sb("M0", [16, 1024], BF16)
        ONESC = sb("ONESC", [128, 1], BF16)
        ONESR = sb("ONESR", [1, 512], BF16)
        BROW = [sb("BROW0", [1, 512], BF16), sb("BROW1", [1, 512], BF16)]
        B1S = sb("B1S", [128, NHC], F32)
        KSUM = sb("KSUM", [128, 8], F32)
        QSUM = sb("QSUM", [128, 8], F32)
        QSI = sb("QSI", [128, 8], F32)
        KSOC = sb("KSOC", [128, 8], F32)
        CB = sb("CB", [16, 4], F32)
        ESCOL = sb("ESCOL", [16, 4], F32)
        MSK = sb("MSK", [128, 2, 16], BF16)
        NOPC = sb("NOPC", [1, 4], F32)
        CONSTS = sb("CONSTS", [128, 4], F32)
        CONSTS2 = sb("CONSTS2", [128, 2], F32)
        nc.const_aps.register(F32, EPS, CONSTS[:, 0:1])
        nc.const_aps.register(F32, 1024.0, CONSTS[:, 1:2])
        nc.const_aps.register(F32, -0.5, CONSTS[:, 2:3])
        nc.const_aps.register(F32, -1.0, CONSTS[:, 3:4])
        LOG4096 = 8.317766166719343
        nc.const_aps.register(F32, LOG4096, CONSTS2[:, 0:1])

        psA = [top.enter_context(nc.psum_tensor("psA%d" % i, [128, 512], F32))
               for i in range(4)]
        psB = [top.enter_context(nc.psum_tensor("psB%d" % i, [128, 512], F32))
               for i in range(2)]
        psT = top.enter_context(nc.psum_tensor("psT", [128, 128], F32))
        psK = top.enter_context(nc.psum_tensor("psK", [128, 64], F32))

        p = Prog(nc, top, ["sx", "sg", "sp", "sv", "ss", "scc"])
        blk = top.enter_context(nc.Block())
        bankA = Pool(p, 4)
        bankB = Pool(p, 2)
        browp = Pool(p, 2)
        mskp = Pool(p, 2)

        _VOPS = {"tensor_scalar", "tensor_tensor", "tensor_copy", "reduce_sum",
                 "reduce_max", "tensor_add", "tensor_sub", "tensor_mul",
                 "tensor_max", "memset", "scalar_tensor_tensor",
                 "tensor_single_scalar", "tensor_scalar_mul", "tensor_relu"}
        _SOPS = {"activation", "copy", "add", "mul", "square", "sqrt", "sign"}

        class Chained:
            def __init__(self, eng, sem, ops):
                self._e = eng
                self._sem = sem
                self._ops = ops

            def __getattr__(self, name):
                f = getattr(self._e, name)
                if name in self._ops:
                    def g(*a, **k):
                        p.wait(self._e, self._sem)
                        ins = f(*a, **k)
                        p.inc(ins, self._sem, 1)
                        p._auto.add((id(ins.ins), self._sem))
                        return ins
                    return g
                return f

        def bV(fn):
            blk.vector(lambda v: fn(Chained(v, "sv", _VOPS)))

        def bS(fn):
            blk.scalar(lambda sc: fn(Chained(sc, "ss", _SOPS)))

        def snop(sc):
            return sc.copy(NOPC[0:1, 0:1], NOPC[0:1, 2:3])

        def P_E_INC(t, ins, *a):
            ins = p.inc(pe_fence(t), "sp", 1)
            p.wait(t, "sp")
            return ins

        def pe_fence(t):
            import os as _o
            if _o.environ.get("NOFENCE"):
                return t.matmul(psT[0:1, 127:128], ONESC[0:1, 0:1],
                                ONESC[0:1, 0:1], start=True, stop=True,
                                skip_group_check=True)
            for _ in range(3):
                t.matmul(psT[0:1, 127:128], ONESC[0:1, 0:1],
                         ONESC[0:1, 0:1], start=True, stop=True,
                         skip_group_check=True)
            return t.matmul(psT[0:1, 127:128], ONESC[0:1, 0:1],
                            ONESC[0:1, 0:1], start=True, stop=True,
                            skip_group_check=True)

        def tap(stage, src_ap):
            if dbg_stage != stage:
                return

            def _t(g):
                p.wait(g, "sp", "sv", "ss", "sx", "sg", "scc")
                p.dma(g.dma_start(out=dbg_d, in_=src_ap), "sg")
                g.wait_ge(p.h["sg"], p.cnt["sg"])
                p.inc(g.engine_nop(), "sg", 1)
            blk.gpsimd(_t)
            barrier("vector", "scalar", "tensor", "sync")

        def barrier(*engs):
            for name in engs:
                def _b(e, name=name):
                    p.wait(e, "sx", "sg", "sp", "sv", "ss", "scc")
                getattr(blk, name)(_b)

        bias_ready = [0]

        def load_brow(slot, row, col0, n):
            def _l(s, slot=slot, row=row, col0=col0, n=n):
                p.wait_val(s, "sg", bias_ready[0])
                st0 = browp.st[slot]
                if st0 is not None:
                    p.wait_val(s, st0[0], st0[1])
                p.dma(s.dma_start(out=BROW[slot][0:1, 0:n],
                                  in_=scr_bias[row:row + 1, col0:col0 + n]), "sx")
            blk.sync(_l)

        with ExitStack() as cst:
            IOTJ = sbt("IOTJ", [128, 128], F32, cst)
            M0A = sbt("M0A", [16, 1024], F32, cst)
            M0B = sbt("M0B", [16, 1], F32, cst)

            def _c(g):
                g.iota(IOTC[:, :], pattern=[[1, 1]], channel_multiplier=1,
                       allow_small_or_imprecise_dtypes=True)
                g.iota(IOTJ[:, :], pattern=[[1, 128]], channel_multiplier=0,
                       allow_small_or_imprecise_dtypes=True)
                g.iota(M0A[0:16, :].rearrange("p (s d) -> p s d", d=64),
                       pattern=[[1, 16], [0, 64]], channel_multiplier=0,
                       allow_small_or_imprecise_dtypes=True)
                p.inc(g.iota(M0B[0:16, :], pattern=[[1, 1]], channel_multiplier=1,
                             allow_small_or_imprecise_dtypes=True), "sg", 1)
            blk.gpsimd(_c)

            def _c2(v):
                p.wait(v, "sg")
                v.tensor_copy(NIi[:, 0:1], IOTC[:, :])
                v.tensor_scalar(NIi[:, 0:1], NIi[:, 0:1], 15, None,
                                OP.bitwise_and)
                v.tensor_copy(T1f[:, 0:1], NIi[:, 0:1])
                v.tensor_tensor(IDENT[:, :], IOTJ[:, 0:16],
                                T1f[:, 0:1].broadcast_to([128, 16]),
                                OP.is_equal)
                v.tensor_tensor(M0[0:16, :], M0A[0:16, :],
                                M0B[0:16, 0:1].broadcast_to([16, 1024]),
                                OP.is_equal)
                v.memset(ONESC[:, :], 1.0)
                v.memset(ESCOL[0:16, 0:4], 0.0)
                v.memset(NOPC[:, :], 0.0)
                v.memset(CONSTS[:, 0:1], EPS)
                v.memset(CONSTS[:, 1:2], 1024.0)
                v.memset(CONSTS[:, 2:3], -0.5)
                v.memset(CONSTS[:, 3:4], -1.0)
                v.memset(CONSTS2[:, 0:1], 8.317766166719343)
                p.inc(v.memset(ONESR[:, :], 1.0), "sv", 1)
            bV(_c2)

        def _lx(s):
            xr = x_d.rearrange("(t p) f -> p t f", p=128)
            for g4 in range(4):
                p.dma(s.dma_start(out=X[:, 4 * g4:4 * g4 + 4, :],
                                  in_=xr[:, 4 * g4:4 * g4 + 4, :]), "sx")
        blk.sync(_lx)

        def _lb(g):
            p.dma(g.dma_start(out=scr_bias[0:1, 0:3 * D], in_=qkvb_d), "sg")
            p.dma(g.dma_start(out=scr_bias[1:2, 0:D], in_=projb_d), "sg")
            p.dma(g.dma_start(out=scr_bias[2:3, 0:D], in_=bbq_d), "sg")
            p.dma(g.dma_start(out=scr_bias[3:4, 0:D], in_=bbk_d), "sg")
            p.dma(g.dma_start(out=scr_bias[4:5, 0:D], in_=bbv_d), "sg")
            p.dma(g.dma_start(out=scr_bias[5:6, 0:D], in_=bbo_d), "sg")
            p.dma(g.dma_start(out=scr_wq, in_=bwq_d), "sg")
            p.dma(g.dma_start(out=scr_wk, in_=bwk_d), "sg")
            p.dma(g.dma_start(out=scr_wv, in_=bwv_d), "sg")
            p.dma(g.dma_start(out=scr_wo, in_=bwo_d), "sg")
        blk.gpsimd(_lb)
        bias_ready[0] = p.cnt["sg"]

        with ExitStack() as wst:
            WSTG = sbt("WSTG", [128, 4096], BF16, wst)
            WF = sbt("WF", [128, 1024], F32, wst)

            def emit_transposed_scratch(src_bf16, dst_T, O):
                for ic in range(NC):
                    def _t(s, ic=ic):
                        p.wait(s, "sx", "sg")
                        p.dma(s.dma_start(
                            out=WSTG[:, 0:O],
                            in_=src_bf16[0:O, ic * 128:(ic + 1) * 128],
                            transpose=True), "sx")
                        p.wait(s, "sx")
                        p.dma(s.dma_start(
                            out=dst_T[ic * 128:(ic + 1) * 128, 0:O],
                            in_=WSTG[:, 0:O]), "sx")
                    blk.sync(_t)

            def _cast_qkv(g):
                p.wait(g, "sg")
                p.dma(g.dma_start(out=scr_tmp[0:3 * D, :], in_=qkvw_d), "sg")
            blk.gpsimd(_cast_qkv)
            emit_transposed_scratch(scr_tmp[0:3 * D, :], scr_qkvT, 3 * D)

            def _cast_proj(g):
                p.wait(g, "sx")
                p.dma(g.dma_start(out=scr_tmp[0:D, :], in_=projw_d), "sg")
            blk.gpsimd(_cast_proj)
            emit_transposed_scratch(scr_tmp[0:D, :], scr_projT, D)

            def gq_chain(src, dst, pp, n, scale):
                def _v1(v):
                    p.wait(v, "sx", "ss", "sg")
                    p.inc(v.tensor_scalar(QS1[0:pp, 0:n], src, 64.0, 0.5,
                                          OP.mult, OP.add), "sv", 1)
                bV(_v1)

                def _s1(sc):
                    p.wait(sc, "sv")
                    sc.copy(NIi[0:pp, 0:n], QS1[0:pp, 0:n])
                    p.inc(sc.copy(QS2[0:pp, 0:n], NIi[0:pp, 0:n]), "ss", 1)
                bS(_s1)

                def _v2(v):
                    p.wait(v, "ss")
                    v.tensor_tensor(T2f[0:pp, 0:n], QS1[0:pp, 0:n],
                                    QS2[0:pp, 0:n], OP.is_ge)
                    v.scalar_tensor_tensor(QS1[0:pp, 0:n], QS2[0:pp, 0:n], 1.0,
                                           T2f[0:pp, 0:n], OP.subtract, OP.add)
                    v.tensor_scalar(QS1[0:pp, 0:n], QS1[0:pp, 0:n], scale,
                                    None, OP.mult)
                    p.inc(v.tensor_scalar(dst, QS1[0:pp, 0:n], -128.0 * scale,
                                          127.0 * scale, OP.max, OP.min),
                          "sv", 1)
                bV(_v2)

            w1r = w1_d.rearrange("(r p) i -> p r i", p=128)
            tmpr = scr_tmp.rearrange("(r p) i -> p r i", p=128)
            for r in range(HID // 128):
                def _lw(s, r=r):
                    p.wait(s, "sv")
                    p.dma(s.dma_start(out=WF[:, :], in_=w1r[:, r, :]), "sx")
                blk.sync(_lw)
                for hh in range(2):
                    gq_chain(WF[:, hh * 512:(hh + 1) * 512],
                             WSTG[:, hh * 512:(hh + 1) * 512], 128, 512,
                             1.0 / 1024.0)

                def _sw(g, r=r):
                    p.wait(g, "sv", "sx")
                    p.dma(g.dma_start(out=tmpr[:, r, :], in_=WSTG[:, 0:1024]),
                          "sg")
                blk.gpsimd(_sw)
            emit_transposed_scratch(scr_tmp[0:HID, :], scr_w1T, HID)

            w2r = w2_d.rearrange("(r p) i -> p r i", p=128)
            w2qr = scr_w2q.rearrange("(r p) i -> p r i", p=128)
            for r in range(D // 128):
                for hh in range(4):
                    def _l2(s, r=r, hh=hh):
                        p.wait(s, "sv", "sg")
                        p.dma(s.dma_start(
                            out=WF[:, :],
                            in_=w2r[:, r, hh * 1024:(hh + 1) * 1024]), "sx")
                    blk.sync(_l2)
                    for q2 in range(2):
                        gq_chain(WF[:, q2 * 512:(q2 + 1) * 512],
                                 WSTG[:, q2 * 512:(q2 + 1) * 512], 128, 512,
                                 1.0 / 1024.0)

                    def _s2(g, r=r, hh=hh):
                        p.wait(g, "sv")
                        p.dma(g.dma_start(
                            out=w2qr[:, r, hh * 1024:(hh + 1) * 1024],
                            in_=WSTG[:, 0:1024]), "sg")
                    blk.gpsimd(_s2)

            def _lb1(s):
                p.wait(s, "sv", "sg")
                p.dma(s.dma_start(out=WF[:, 0:NHC], in_=b1_d), "sx")
            blk.sync(_lb1)
            gq_chain(WF[:, 0:NHC], WSTG[:, 0:NHC], 128, NHC, 1.0 / 64.0)

            def _b1s(v):
                p.wait(v, "sv")
                p.inc(v.tensor_scalar(B1S[:, :], WSTG[:, 0:NHC], 1024.0, 32.0,
                                      OP.mult, OP.add), "sv", 1)
            bV(_b1s)

            def _lb2(s):
                p.wait(s, "sv")
                p.dma(s.dma_start(out=WF[0:1, 0:D], in_=b2_d), "sx")
            blk.sync(_lb2)
            for q2 in range(2):
                gq_chain(WF[0:1, q2 * 512:(q2 + 1) * 512],
                         WSTG[0:1, q2 * 512:(q2 + 1) * 512], 1, 512, 1.0 / 64.0)

            def _sb2(g):
                p.wait(g, "sv")
                p.dma(g.dma_start(out=scr_bias[6:7, 0:D], in_=WSTG[0:1, 0:D]),
                      "sg")
            blk.gpsimd(_sb2)
            bias_ready[0] = p.cnt["sg"]

        ln_n = [0]

        def emit_ln(mode, tag=""):
            barrier("vector", "scalar")
            ln_n[0] += 1
            lst = ExitStack()
            LNSTG = sbt("LNS%d" % ln_n[0], [128, 1024], BF16, lst)

            def _st(v):
                for tt in range(NTT):
                    v.reduce_sum(STAT[:, tt:tt + 1], X[:, tt, :], AX)
                p.inc(v.engine_nop(), "sv", 1)
            bV(_st)

            def _sq(sc):
                for tt in range(NTT):
                    sc.activation(T1f[:, :], X[:, tt, :], AF.Square,
                                  accum_out=STAT[:, 16 + tt:17 + tt])
                p.inc(snop(sc), "ss", 1)
            bS(_sq)

            def _chain(v):
                p.wait(v, "sv", "ss")
                v.tensor_scalar(STAT[:, 32:48], STAT[:, 0:16], -1.0 / 1024.0,
                                None, OP.mult)
                v.tensor_tensor(TA[:, :], STAT[:, 32:48], STAT[:, 32:48],
                                OP.mult)
                v.tensor_scalar(TB[:, :], STAT[:, 16:32], 1.0 / 1024.0,
                                None, OP.mult)
                v.tensor_tensor(TCH[:, :], TB[:, :], TA[:, :], OP.subtract)
                p.inc(v.tensor_scalar(TA[:, :], TCH[:, :], LNEPS,
                                      None, OP.add), "sv", 1)
            bV(_chain)

            def _rs(sc):
                p.wait(sc, "sv")
                sc.activation(TB[:, :], TA[:, :], AF.Ln)
                p.inc(sc.activation(STAT[:, 0:16], TB[:, :], AF.Exp,
                                    scale=-0.5), "ss", 1)
            bS(_rs)
            tap(tag + "stat", STAT[:, :])
            if mode == "quant":
                def _r16(v):
                    p.wait(v, "ss")
                    p.inc(v.tensor_scalar(STAT[:, 0:16], STAT[:, 0:16], 16.0,
                                          None, OP.mult), "sv", 1)
                bV(_r16)

            lnw = scr_ln.rearrange("(t p) f -> p t f", p=128)
            for tt in range(NTT):
                def _ap(sc, tt=tt):
                    p.wait(sc, "sv")
                    p.inc(sc.activation(T1f[:, :], X[:, tt, :], AF.Identity,
                                        bias=STAT[:, 32 + tt:33 + tt],
                                        scale=1.0), "ss", 1)
                bS(_ap)
                if mode == "plain":
                    def _mul(v, tt=tt):
                        p.wait(v, "ss", "sg")
                        p.inc(v.tensor_scalar(LNSTG[:, :], T1f[:, :],
                                              STAT[:, tt:tt + 1], None,
                                              OP.mult), "sv", 1)
                    bV(_mul)
                else:
                    def _q(v, tt=tt):
                        p.wait(v, "ss")
                        p.inc(v.tensor_scalar(T1f[:, :], T1f[:, :],
                                              STAT[:, tt:tt + 1], 0.5,
                                              OP.mult, OP.add), "sv", 1)
                    bV(_q)

                    def _qv(v, tt=tt):
                        p.wait(v, "ss", "sg")
                        v.tensor_copy(NIi[:, 0:1024], T1f[:, :])
                        v.tensor_copy(QS1[:, 0:512], NIi[:, 0:512])
                        v.tensor_copy(QS2[:, 0:512], NIi[:, 512:1024])
                        for hh, f0 in ((0, QS1), (1, QS2)):
                            sl = slice(hh * 512, (hh + 1) * 512)
                            v.tensor_tensor(T2f[:, :], T1f[:, sl], f0[:, :],
                                            OP.is_ge)
                            v.scalar_tensor_tensor(T1f[:, sl], f0[:, :], 1.0,
                                                   T2f[:, :], OP.subtract,
                                                   OP.add)
                            v.tensor_scalar(LNSTG[:, sl], T1f[:, sl], -128.0,
                                            127.0, OP.max, OP.min)
                        p.inc(v.engine_nop(), "sv", 1)
                    bV(_qv)

                def _wr(g, tt=tt):
                    p.wait(g, "sv")
                    p.dma(g.dma_start(out=lnw[:, tt, :], in_=LNSTG[:, :]), "sg")
                blk.gpsimd(_wr)
            barrier("vector", "scalar", "sync", "gpsimd", "tensor")
            lst.close()

        def emit_allreduce(pairs):
            def _up(g):
                p.wait(g, "sv", "ss", "sp", "sx")
                for ap_s, d_i, d_o in pairs:
                    p.dma(g.dma_start(out=d_i, in_=ap_s), "sg")
                g.wait_ge(p.h["sg"], p.cnt["sg"])
                for ap_s, d_i, d_o in pairs:
                    p.inc(g.collective_compute("AllReduce", OP.add,
                                               replica_groups=RG,
                                               ins=[d_i], outs=[d_o]),
                          "scc", 1)
                g.wait_ge(p.h["scc"], p.cnt["scc"])
                for ap_s, d_i, d_o in pairs:
                    p.dma(g.dma_start(out=ap_s, in_=d_o), "sg")
            blk.gpsimd(_up)

        def build_mask(slot, fc, src_col, eps):
            def _m(v, slot=slot, fc=fc, eps=eps):
                mskp.acq(slot, v)
                v.memset(MSK[:, slot, :], 0.0)
                if eps != 0.0:
                    v.tensor_scalar(MSK[0:64, slot, 2 * fc:2 * fc + 1],
                                    src_col[0:64, :], eps, None, OP.add)
                    p.inc(v.tensor_scalar(
                        MSK[64:128, slot, 2 * fc + 1:2 * fc + 2],
                        src_col[64:128, :], eps, None, OP.add), "sv", 1)
                else:
                    v.tensor_copy(MSK[0:64, slot, 2 * fc:2 * fc + 1],
                                  src_col[0:64, :])
                    p.inc(v.tensor_copy(
                        MSK[64:128, slot, 2 * fc + 1:2 * fc + 2],
                        src_col[64:128, :]), "sv", 1)
            bV(_m)

        def emit_masked_stat(src_cols_fn, eps, rhs_fn, cb_col):
            for fc in range(NC):
                build_mask(fc % 2, fc, src_cols_fn(fc), eps)

                def _mm(t, fc=fc):
                    p.wait(t, "sv")
                    if fc == 0:
                        for i in range(4):
                            bankA.acq(i, t)
                    t.matmul(psT[0:16, cb_col:cb_col + 1], MSK[:, fc % 2, :],
                             ONESC[:, :], start=(fc == 0), stop=(fc == NC - 1))
                    for nt in range(NNT):
                        P_E_INC(t, t.matmul(psA[nt][0:16, :], MSK[:, fc % 2, :],
                                       rhs_fn(fc, nt), start=(fc == 0),
                                       stop=(fc == NC - 1)), "sp", 1)
                    mskp.rel(fc % 2, "sp")
                blk.tensor(_mm)

        def emit_bcast_scale(targ_fn, bstg):
            for fc in range(NC):
                for nt in range(NNT):
                    bi = (fc * NNT + nt) % 2

                    def _bm(t, fc=fc, nt=nt, bi=bi):
                        p.wait(t, "sv")
                        bankB.acq(bi, t)
                        P_E_INC(t, t.matmul(psB[bi][:, :],
                                       M0[0:16, fc * 128:(fc + 1) * 128],
                                       bstg[0:16, nt * 512:(nt + 1) * 512],
                                       start=True, stop=True), "sp", 1)
                    blk.tensor(_bm)

                    def _bvs(sc, bi=bi):
                        p.wait(sc, "sp", "sv")
                        p.inc(sc.copy(T2f[:, :], psB[bi][:, :]), "ss", 1)
                        bankB.rel(bi, "ss")
                    bS(_bvs)

                    def _bv(v, fc=fc, nt=nt):
                        p.wait(v, "ss")
                        targ = targ_fn(fc, nt)
                        p.inc(v.tensor_tensor(targ, targ, T2f[:, :],
                                              OP.mult), "sv", 1)
                    bV(_bv)

        def emit_out_proj(kvb, qf, projT_scr, pbias_row, stage_oa=None):
            with ExitStack() as ost:
                OA = sbt("OA" + str(p.cnt["sp"]), [128, NC, 512], BF16, ost)
                PWs = [sbt("PW" + str(p.cnt["sp"]) + "_" + str(i),
                           [128, NC, 512], BF16, ost) for i in range(2)]
                pwr = projT_scr.rearrange("(ic p) o -> p ic o", p=128)

                def _pwl(s):
                    p.wait(s, "sp", "sg")
                    for on in range(2):
                        p.dma(s.dma_start(
                            out=PWs[on][:, :, :],
                            in_=pwr[:, :, on * 512:(on + 1) * 512]), "sx")
                blk.sync(_pwl)
                oa_guard = [None]
                for nt in range(NNT):
                    for fc in range(NC):
                        bi = fc % 2

                        def _em(t, fc=fc, nt=nt, bi=bi):
                            p.wait(t, "sv")
                            bankB.acq(bi, t)
                            if fc == 0 and oa_guard[0] is not None:
                                p.wait_val(t, oa_guard[0][0], oa_guard[0][1])
                            t.matmul(psB[bi][0:64, :],
                                     kvb[0:64, fc * 64:(fc + 1) * 64],
                                     qf[0:64, fc, nt * 512:(nt + 1) * 512],
                                     start=True, stop=True,
                                     tile_position=(0, 0),
                                     skip_group_check=True)
                            P_E_INC(t, t.matmul(psB[bi][64:128, :],
                                           kvb[64:128, fc * 64:(fc + 1) * 64],
                                           qf[64:128, fc,
                                              nt * 512:(nt + 1) * 512],
                                           start=True, stop=True,
                                           tile_position=(64, 64),
                                           skip_group_check=True), "sp", 1)
                        blk.tensor(_em)

                        def _ev(sc, fc=fc, bi=bi):
                            p.wait(sc, "sp")
                            p.inc(sc.copy(OA[:, fc, :], psB[bi][:, :]), "ss", 1)
                            bankB.rel(bi, "ss")
                        bS(_ev)
                    if stage_oa is not None and nt == 0:
                        tap(stage_oa, OA[:, :, :])
                    for on in range(2):
                        load_brow(on % 2, pbias_row, on * 512, 512)
                        PW = PWs[on]
                        for ts4 in range(4):
                            tt = nt * 4 + ts4
                            bi = ts4 % 4

                            def _pm(t, on=on, ts4=ts4, bi=bi):
                                p.wait(t, "sx", "ss")
                                bankA.acq(bi, t)
                                for ic in range(NC):
                                    t.matmul(psA[bi][:, :],
                                             OA[:, ic, ts4 * 128:(ts4 + 1) * 128],
                                             PW[:, ic, :], start=(ic == 0),
                                             stop=False)
                                P_E_INC(t, t.matmul(psA[bi][:, :], ONESR[0:1, 0:128],
                                               BROW[on % 2][0:1, 0:512],
                                               start=False, stop=True), "sp", 1)
                                if ts4 == 3:
                                    browp.rel(on % 2, "sp")
                            blk.tensor(_pm)

                            def _pe(v, tt=tt, on=on, bi=bi, nt=nt):
                                import os as _o
                                p.wait(v, "sp", "ss", "sx", "sg")
                                xa = X[:, tt, on * 512:(on + 1) * 512]
                                if _o.environ.get("PSUMTAP") and tt == 0 and on == 0:
                                    p.inc(v.tensor_copy(xa, psA[bi][:, :]),
                                          "sv", 1)
                                else:
                                    p.inc(v.tensor_tensor(xa, xa,
                                                          psA[bi][:, :],
                                                          OP.add), "sv", 1)
                                bankA.rel(bi, "sv")
                            bV(_pe)
                    oa_guard[0] = ("sp", p.cnt["sp"])

        def emit_fm_proj(wT_scr, col0, bias_row, bcol0, LG, WG, evac_fn):
            wtr = wT_scr.rearrange("(ic p) o -> p ic o", p=128)
            lg_guard = [None]
            for nt in range(NNT):
                def _lg(s, nt=nt):
                    p.wait(s, "sg")
                    if lg_guard[0] is not None:
                        p.wait_val(s, lg_guard[0][0], lg_guard[0][1])
                    else:
                        p.wait(s, "sp")
                    for ic in range(NC):
                        p.dma(s.dma_start(
                            out=LG[:, ic, :],
                            in_=scr_ln[nt * 512:(nt + 1) * 512,
                                       ic * 128:(ic + 1) * 128],
                            transpose=True), "sx")
                blk.sync(_lg)
                for oc in range(NC):
                    def _wg(s, nt=nt, oc=oc):
                        p.wait(s, "sp")
                        p.dma(s.dma_start(
                            out=WG[:, :, 0:128],
                            in_=wtr[:, :, col0 + oc * 128:col0 + (oc + 1) * 128]),
                            "sx")
                    blk.sync(_wg)
                    load_brow(oc % 2, bias_row, bcol0 + oc * 128, 128)
                    bi = oc % 4

                    def _qm(t, nt=nt, oc=oc, bi=bi):
                        p.wait(t, "sx")
                        bankA.acq(bi, t)
                        for ic in range(NC):
                            t.matmul(psA[bi][:, :], WG[:, ic, 0:128],
                                     LG[:, ic, :], start=(ic == 0), stop=False)
                        P_E_INC(t, t.matmul(psA[bi][:, :], BROW[oc % 2][0:1, 0:128],
                                       ONESR[0:1, 0:512], start=False,
                                       stop=True), "sp", 1)
                        browp.rel(oc % 2, "sp")
                    blk.tensor(_qm)
                    evac_fn(nt, oc, bi)
                lg_guard[0] = ("sp", p.cnt["sp"])

        def emit_tm_proj(w_list, LG, WG, evac_fn):
            lg_guard = [None]
            for ntg in range(NNT):
                def _lg(s, ntg=ntg):
                    p.wait(s, "sg")
                    if lg_guard[0] is not None:
                        p.wait_val(s, lg_guard[0][0], lg_guard[0][1])
                    else:
                        p.wait(s, "sp")
                    for ic in range(NC):
                        p.dma(s.dma_start(
                            out=LG[:, ic, :],
                            in_=scr_ln[ntg * 512:(ntg + 1) * 512,
                                       ic * 128:(ic + 1) * 128],
                            transpose=True), "sx")
                blk.sync(_lg)
                for (wtr, col0, ncols, brow_i, bcol0, tag) in w_list:
                    for on in range(ncols // 512):
                        def _wg(s, wtr=wtr, col0=col0, on=on):
                            p.wait(s, "sp")
                            p.dma(s.dma_start(
                                out=WG[:, :, :],
                                in_=wtr[:, :,
                                        col0 + on * 512:col0 + (on + 1) * 512]),
                                "sx")
                        blk.sync(_wg)
                        load_brow(on % 2, brow_i, bcol0 + on * 512, 512)
                        for t4 in range(4):
                            tt = ntg * 4 + t4
                            bi = t4 % 4

                            def _km(t, on=on, t4=t4, bi=bi):
                                p.wait(t, "sx")
                                bankA.acq(bi, t)
                                for ic in range(NC):
                                    t.matmul(psA[bi][:, :],
                                             LG[:, ic, t4 * 128:(t4 + 1) * 128],
                                             WG[:, ic, :], start=(ic == 0),
                                             stop=False)
                                P_E_INC(t, t.matmul(psA[bi][:, :], ONESR[0:1, 0:128],
                                               BROW[on % 2][0:1, 0:512],
                                               start=False, stop=True), "sp", 1)
                                if t4 == 3:
                                    browp.rel(on % 2, "sp")
                            blk.tensor(_km)
                            evac_fn(tag, on, tt, bi)
                lg_guard[0] = ("sp", p.cnt["sp"])

        def emit_kv_einsum(kt, vt, KVUt):
            for fc in range(NC):
                def _kv(t, fc=fc):
                    p.wait(t, "sv", "ss")
                    a0 = fc * 128
                    for tt in range(NTT):
                        mmA = t.matmul(psK[0:64, :], kt[:, tt, a0:a0 + 64],
                                       vt[:, tt, a0:a0 + 64], start=(tt == 0),
                                       stop=(tt == NTT - 1),
                                       tile_position=(0, 0),
                                       skip_group_check=True)
                        mmB = t.matmul(psK[64:128, :],
                                       kt[:, tt, a0 + 64:a0 + 128],
                                       vt[:, tt, a0 + 64:a0 + 128],
                                       start=(tt == 0), stop=(tt == NTT - 1),
                                       tile_position=(0, 64),
                                       skip_group_check=True)
                        if tt == NTT - 1:
                            P_E_INC(t, mmB)
                blk.tensor(_kv)

                def _ke(sc, fc=fc):
                    p.wait(sc, "sp")
                    p.inc(sc.copy(KVUt[:, fc * 64:(fc + 1) * 64],
                                  psK[:, :]), "ss", 1)
                bS(_ke)

        def emit_tm_colsum(src, OUTC):
            for mc in range(NC):
                def _cs(t, mc=mc):
                    p.wait(t, "sv", "ss")
                    for tt in range(NTT):
                        mm = t.matmul(psT[0:128, mc:mc + 1],
                                      src[:, tt, mc * 128:(mc + 1) * 128],
                                      ONESC[:, :], start=(tt == 0),
                                      stop=(tt == NTT - 1))
                        if tt == NTT - 1:
                            P_E_INC(t, mm)
                blk.tensor(_cs)

            def _ce(sc):
                p.wait(sc, "sp")
                p.inc(sc.copy(OUTC[:, 0:8], psT[0:128, 0:8]), "ss", 1)
            bS(_ce)

        def emit_rows_to_tm(arn_rows, dst):
            for tt in range(NTT):
                def _tr(t, tt=tt):
                    p.wait(t, "sv", "ss")
                    P_E_INC(t, t.transpose(psT[0:128, 0:16],
                                      arn_rows[:, tt * 128:(tt + 1) * 128],
                                      IDENT[64:80, 0:16]), "sp", 1)
                blk.tensor(_tr)

                def _te(sc, tt=tt):
                    p.wait(sc, "sp")
                    p.inc(sc.copy(dst[:, tt, :], psT[0:128, 0:16]),
                          "ss", 1)
                bS(_te)

        barrier("vector", "scalar", "tensor", "gpsimd", "sync")
        tap("x_loaded", X[:, :, :])

        emit_ln("plain", "ln1")
        tap("ln1", scr_ln)

        with ExitStack() as ast:
            QF = sbt("QF", [128, NC, TC], BF16, ast)
            KVB = sbt("KVB", [128, 512], BF16, ast)
            inn = ExitStack()
            KT = sbt("KT", [128, NTT, D], BF16, inn)
            VT = sbt("VT", [128, NTT, D], BF16, inn)
            ARN = sbt("ARN", [128, 2048], F32, inn)
            KVU = sbt("KVU", [128, 512], F32, inn)
            SI = ARN[0:16, :]
            SO = ARN[32:48, :]
            EE = ARN[64:80, :]
            TMP2 = ARN[96:112, :]

            barrier("sync", "gpsimd", "tensor", "vector", "scalar")

            with ExitStack() as qst:
                LG = sbt("LG", [128, NC, 512], BF16, qst)
                WG = sbt("WG", [128, NC, 512], BF16, qst)

                def q_evac(nt, oc, bi):
                    def _e(sc, nt=nt, oc=oc, bi=bi):
                        p.wait(sc, "sp")
                        p.inc(sc.activation(QF[:, oc, nt * 512:(nt + 1) * 512],
                                            psA[bi][:, :], AF.Sigmoid),
                              "ss", 1)
                        bankA.rel(bi, "ss")
                    bS(_e)
                emit_fm_proj(scr_qkvT, 0, 0, 0, LG, WG, q_evac)
                barrier("sync", "gpsimd", "tensor", "vector", "scalar")

                kvw = scr_k.rearrange("(t p) f -> p t f", p=128)

                def kv_evac(tag, on, tt, bi):
                    if tag == "k":
                        def _e(sc, on=on, tt=tt, bi=bi):
                            p.wait(sc, "sp")
                            p.inc(sc.activation(
                                KT[:, tt, on * 512:(on + 1) * 512],
                                psA[bi][:, :], AF.Sigmoid), "ss", 1)
                            bankA.rel(bi, "ss")
                        bS(_e)

                        def _w(g, on=on, tt=tt):
                            p.wait(g, "ss")
                            p.dma(g.dma_start(
                                out=kvw[:, tt, on * 512:(on + 1) * 512],
                                in_=KT[:, tt, on * 512:(on + 1) * 512]), "sg")
                        blk.gpsimd(_w)
                    else:
                        def _e(sc, on=on, tt=tt, bi=bi):
                            p.wait(sc, "sp")
                            p.inc(sc.copy(
                                VT[:, tt, on * 512:(on + 1) * 512],
                                psA[bi][:, :]), "ss", 1)
                            bankA.rel(bi, "ss")
                        bS(_e)

                qkvtr = scr_qkvT.rearrange("(ic p) o -> p ic o", p=128)
                emit_tm_proj([(qkvtr, 1024, 1024, 0, 1024, "k"),
                              (qkvtr, 2048, 1024, 0, 2048, "v")], LG, WG,
                             kv_evac)

            tap("q_fm", QF[:, :, :])
            tap("k_tm", KT[:, :, :])
            tap("v_tm", VT[:, :, :])

            mst = ExitStack()
            BSTG = sbt("BSTG", [16, 2048], BF16, mst)
            BSTG2 = sbt("BSTG2", [16, 2048], BF16, mst)

            emit_tm_colsum(KT, KSUM)

            def _qs(v):
                p.wait(v, "ss")
                for fc in range(NC):
                    v.reduce_sum(QSUM[:, fc:fc + 1], QF[:, fc, :], AX)
                p.inc(v.engine_nop(), "sv", 1)
            bV(_qs)

            def _pack1(v):
                p.wait(v, "sv")
                v.tensor_copy(T1f[:, 0:8], KSUM[:, :])
                p.inc(v.tensor_copy(T1f[:, 8:16], QSUM[:, :]), "sv", 1)
            bV(_pack1)
            emit_allreduce([(T1f[:, 0:16], cc_a_i, cc_a_o)])

            def _unpack1(v):
                p.wait(v, "sg")
                v.tensor_copy(KSUM[:, :], T1f[:, 0:8])
                p.inc(v.tensor_copy(QSUM[:, :], T1f[:, 8:16]), "sv", 1)
            bV(_unpack1)
            tap("ksum", KSUM[:, :])
            tap("qsum", QSUM[:, :])

            emit_masked_stat(lambda fc: KSUM[:, fc:fc + 1], EPS,
                             lambda fc, nt: QF[:, fc, nt * 512:(nt + 1) * 512],
                             0)

            def _si(sc):
                p.wait(sc, "sp")
                sc.activation(CB[0:16, 0:1], psT[0:16, 0:1], AF.Identity, scale=EPS)
                for nt in range(NNT):
                    sc.activation(EE[:, nt * 512:(nt + 1) * 512],
                                  psA[nt][0:16, :], AF.Identity,
                                  bias=CB[0:16, 0:1])
                p.inc(snop(sc), "ss", 1)
            bS(_si)
            for i in range(4):
                bankA.rel(i, "ss")

            def _sir(sc):
                p.wait(sc, "ss")
                sc.activation(SO[:, :], EE[:, :], AF.Ln)
                p.inc(sc.activation(SI[:, :], SO[:, :], AF.Exp, scale=-1.0),
                      "ss", 1)
            bS(_sir)
            tap("si", SI)

            with ExitStack() as kst:
                KFG = sbt("KFG", [128, NC, 512], BF16, kst)

                def kfg_load(nt):
                    def _k(s, nt=nt):
                        p.wait(s, "sp", "sg")
                        for ic in range(NC):
                            p.dma(s.dma_start(
                                out=KFG[:, ic, :],
                                in_=scr_k[nt * 512:(nt + 1) * 512,
                                          ic * 128:(ic + 1) * 128],
                                transpose=True), "sx")
                    blk.sync(_k)

                for nt in range(NNT):
                    kfg_load(nt)
                    for fc in range(NC):
                        build_mask(fc % 2, fc, QSUM[:, fc:fc + 1], EPS)

                        def _mm(t, fc=fc, nt=nt):
                            p.wait(t, "sv", "sx")
                            if fc == 0:
                                bankA.acq(nt, t)
                            if nt == 0:
                                t.matmul(psT[0:16, 1:2], MSK[:, fc % 2, :],
                                         ONESC[:, :], start=(fc == 0),
                                         stop=(fc == NC - 1))
                            P_E_INC(t, t.matmul(psA[nt][0:16, :], MSK[:, fc % 2, :],
                                           KFG[:, fc, :], start=(fc == 0),
                                           stop=(fc == NC - 1)), "sp", 1)
                            mskp.rel(fc % 2, "sp")
                        blk.tensor(_mm)

                def _so(sc):
                    p.wait(sc, "sp")
                    sc.activation(CB[0:16, 1:2], psT[0:16, 1:2], AF.Identity,
                                  scale=EPS)
                    for nt in range(NNT):
                        sc.activation(EE[:, nt * 512:(nt + 1) * 512],
                                      psA[nt][0:16, :], AF.Identity,
                                      bias=CB[0:16, 1:2])
                    p.inc(snop(sc), "ss", 1)
                bS(_so)
                for i in range(4):
                    bankA.rel(i, "ss")

                def _sor(sc):
                    p.wait(sc, "ss")
                    sc.activation(TMP2[:, :], EE[:, :], AF.Ln)
                    p.inc(sc.activation(SO[:, :], TMP2[:, :], AF.Exp,
                                        scale=-1.0), "ss", 1)
                bS(_sor)
                tap("so", SO)

                def _sib(v):
                    p.wait(v, "sv")
                    p.inc(v.tensor_copy(BSTG[0:16, :], SI[:, :]), "sv", 1)
                bV(_sib)

                def emit_col_wsum(src3d, OUTC):
                    for fc in range(NC):
                        for nt in range(NNT):
                            bi = nt % 2

                            def _bm(t, fc=fc, nt=nt, bi=bi):
                                p.wait(t, "sv")
                                bankB.acq(bi, t)
                                P_E_INC(t, t.matmul(
                                    psB[bi][:, :],
                                    M0[0:16, fc * 128:(fc + 1) * 128],
                                    BSTG[0:16, nt * 512:(nt + 1) * 512],
                                    start=True, stop=True), "sp", 1)
                            blk.tensor(_bm)

                            def _trs(sc, bi=bi):
                                p.wait(sc, "sp", "sv")
                                p.inc(sc.copy(T2f[:, :], psB[bi][:, :]),
                                      "ss", 1)
                                bankB.rel(bi, "ss")
                            bS(_trs)

                            def _tr(v, fc=fc, nt=nt):
                                p.wait(v, "ss")
                                v.tensor_tensor(QS1[:, :], src3d(fc, nt),
                                                T2f[:, :], OP.mult)
                                p.inc(v.reduce_sum(STAT[:, nt:nt + 1],
                                                   QS1[:, :], AX), "sv", 1)
                            bV(_tr)

                        def _fin(v, fc=fc):
                            p.inc(v.reduce_sum(OUTC[:, fc:fc + 1],
                                               STAT[:, 0:4], AX), "sv", 1)
                        bV(_fin)

                emit_col_wsum(
                    lambda fc, nt: QF[:, fc, nt * 512:(nt + 1) * 512], QSI)
                tap("qsi", QSI[:, :])

                def _sob(v):
                    p.wait(v, "sv")
                    p.inc(v.tensor_copy(BSTG[0:16, :], SO[:, :]), "sv", 1)
                bV(_sob)
                for nt in range(NNT):
                    kfg_load(nt)
                    for fc in range(NC):
                        bi = fc % 2

                        def _bm2(t, fc=fc, nt=nt, bi=bi):
                            p.wait(t, "sv", "sx")
                            bankB.acq(bi, t)
                            P_E_INC(t, t.matmul(
                                psB[bi][:, :],
                                M0[0:16, fc * 128:(fc + 1) * 128],
                                BSTG[0:16, nt * 512:(nt + 1) * 512],
                                start=True, stop=True), "sp", 1)
                        blk.tensor(_bm2)

                        def _tr2s(sc, bi=bi):
                            p.wait(sc, "sp", "sv")
                            p.inc(sc.copy(T2f[:, :], psB[bi][:, :]), "ss", 1)
                            bankB.rel(bi, "ss")
                        bS(_tr2s)

                        def _tr2(v, fc=fc, nt=nt):
                            p.wait(v, "ss")
                            v.tensor_tensor(QS1[:, :], KFG[:, fc, :],
                                            T2f[:, :], OP.mult)
                            p.inc(v.reduce_sum(STAT[:, 4 + fc:5 + fc],
                                               QS1[:, :], AX), "sv", 1)
                        bV(_tr2)

                        def _tr2f(v, fc=fc, nt=nt):
                            if nt == 0:
                                p.inc(v.tensor_copy(KSOC[:, fc:fc + 1],
                                                    STAT[:, 4 + fc:5 + fc]),
                                      "sv", 1)
                            else:
                                p.inc(v.tensor_tensor(KSOC[:, fc:fc + 1],
                                                      KSOC[:, fc:fc + 1],
                                                      STAT[:, 4 + fc:5 + fc],
                                                      OP.add), "sv", 1)
                        bV(_tr2f)
                tap("kso", KSOC[:, :])

                def _pack2(v):
                    p.wait(v, "sv")
                    v.tensor_copy(T1f[:, 0:8], KSOC[:, :])
                    p.inc(v.tensor_copy(T1f[:, 8:16], QSI[:, :]), "sv", 1)
                bV(_pack2)
                emit_allreduce([(T1f[:, 0:16], cc_a_i, cc_a_o)])

                def _unpack2(v):
                    p.wait(v, "sg")
                    v.tensor_copy(KSOC[:, :], T1f[:, 0:8])
                    p.inc(v.tensor_copy(QSI[:, :], T1f[:, 8:16]), "sv", 1)
                bV(_unpack2)

                emit_masked_stat(lambda fc: KSOC[:, fc:fc + 1], EPS,
                                 lambda fc, nt: QF[:, fc,
                                                   nt * 512:(nt + 1) * 512], 2)

                def _sa(sc):
                    p.wait(sc, "sp")
                    sc.activation(CB[0:16, 2:3], psT[0:16, 2:3], AF.Identity,
                                  scale=EPS, bias=EPS)
                    for nt in range(NNT):
                        sc.activation(BSTG2[0:16, nt * 512:(nt + 1) * 512],
                                      psA[nt][0:16, :], AF.Sigmoid,
                                      bias=CB[0:16, 2:3])
                    p.inc(snop(sc), "ss", 1)
                bS(_sa)
                for i in range(4):
                    bankA.rel(i, "ss")
                tap("sa", BSTG2[0:16, :])

                for nt in range(NNT):
                    kfg_load(nt)
                    for fc in range(NC):
                        build_mask(fc % 2, fc, QSI[:, fc:fc + 1], EPS)

                        def _mm2(t, fc=fc, nt=nt):
                            p.wait(t, "sv", "sx")
                            if fc == 0:
                                bankA.acq(nt, t)
                            if nt == 0:
                                t.matmul(psT[0:16, 3:4], MSK[:, fc % 2, :],
                                         ONESC[:, :], start=(fc == 0),
                                         stop=(fc == NC - 1))
                            P_E_INC(t, t.matmul(psA[nt][0:16, :], MSK[:, fc % 2, :],
                                           KFG[:, fc, :], start=(fc == 0),
                                           stop=(fc == NC - 1)), "sp", 1)
                            mskp.rel(fc % 2, "sp")
                        blk.tensor(_mm2)

                def _cs2(sc):
                    p.wait(sc, "sp")
                    sc.activation(CB[0:16, 3:4], psT[0:16, 3:4], AF.Identity,
                                  scale=EPS, bias=EPS)
                    for nt in range(NNT):
                        sc.activation(SO[:, nt * 512:(nt + 1) * 512],
                                      psA[nt][0:16, :], AF.Identity,
                                      bias=CB[0:16, 3:4])
                    p.inc(snop(sc), "ss", 1)
                bS(_cs2)
                for i in range(4):
                    bankA.rel(i, "ss")

                def _clip(v):
                    p.wait(v, "ss")
                    p.inc(v.tensor_scalar(SO[:, :], SO[:, :], -1.0, 1.0,
                                          OP.max, OP.min), "sv", 1)
                bV(_clip)

                def _exp(sc):
                    p.wait(sc, "sv")
                    p.inc(sc.activation(EE[:, :], SO[:, :], AF.Exp), "ss", 1)
                bS(_exp)

                def _es(v):
                    p.wait(v, "ss")
                    p.inc(v.reduce_sum(ESCOL[0:16, 0:1], EE[:, :], AX),
                          "sv", 1)
                bV(_es)
                tap("e", EE)

            est = ExitStack()
            ETM = sbt("ETM", [128, NTT, 16], F32, est)
            emit_rows_to_tm(EE, ETM)
            for tt in range(NTT):
                def _ve(v, tt=tt):
                    p.wait(v, "sv")
                    va = VT[:, tt, :].rearrange("p (s d) -> p s d", d=64)
                    ea = ETM[:, tt, :].rearrange("p (s o) -> p s o", o=1)
                    p.inc(v.tensor_tensor(va, va,
                                          ea.broadcast_to([128, 16, 64]),
                                          OP.mult), "sv", 1)
                bV(_ve)
            barrier("vector")
            est.close()

            emit_kv_einsum(KT, VT, KVU)
            tap("kvu_pre", KVU[:, :])

            emit_allreduce([(KVU[:, :], cc_d_i, cc_d_o),
                            (ESCOL[0:16, 0:4], cc_e_i, cc_e_o)])

            def _norm(v):
                p.wait(v, "sg")
                p.inc(v.tensor_copy(KVB[:, :], KVU[:, :]), "sv", 1)
            bV(_norm)

            def _norm2(sc):
                p.wait(sc, "sg")
                sc.activation(ESCOL[0:16, 2:3], ESCOL[0:16, 0:1], AF.Ln)
                p.inc(sc.activation(ESCOL[0:16, 1:2], ESCOL[0:16, 2:3],
                                    AF.Exp, scale=-1.0, bias=8.317766166719343), "ss", 1)
            bS(_norm2)
            tap("kvu", KVU[:, :])

            def _scq(v):
                p.wait(v, "sv", "ss")
                v.tensor_scalar(SI[:, :], SI[:, :], ESCOL[0:16, 1:2], None,
                                OP.mult)
                p.inc(v.tensor_tensor(BSTG[0:16, :], SI[:, :], BSTG2[0:16, :],
                                      OP.mult), "sv", 1)
            bV(_scq)
            tap("norm", ESCOL[0:16, 0:4])
            tap("scq", BSTG[0:16, :])
            emit_bcast_scale(lambda fc, nt: QF[:, fc, nt * 512:(nt + 1) * 512], BSTG)
            tap("qfin", QF[:, :, :])
            mst.close()

            tap("x0b", X[:, :, :])
            barrier("sync", "gpsimd", "tensor", "vector", "scalar")
            inn.close()
            emit_out_proj(KVB, QF, scr_projT, 1, stage_oa="outa")

        tap("x1", X[:, :, :])

        def emit_ffn(tag):
            emit_ln("quant")
            if tag == 1:
                tap("xq", scr_ln)
            with ExitStack() as fst:
                W2R = sbt("W2R%d" % tag, [128, NHC, 1024], BF16, fst)
                HQ = sbt("HQ%d" % tag, [128, NHC, 512], BF16, fst)
                XQG = sbt("XQG%d" % tag, [128, NC, 512], BF16, fst)
                W1G = [sbt("W1Ga%d" % tag, [128, NC, 512], BF16, fst),
                       sbt("W1Gb%d" % tag, [128, NC, 512], BF16, fst)]
                w1gp = Pool(p, 2)

                barrier("sync", "gpsimd", "tensor", "vector", "scalar")

                def _w2(s):
                    for hc in range(NHC):
                        p.dma(s.dma_start(
                            out=W2R[:, hc, :],
                            in_=scr_w2q[:, hc * 128:(hc + 1) * 128],
                            transpose=True), "sx")
                blk.sync(_w2)
                w1tr = scr_w1T.rearrange("(ic p) h -> p ic h", p=128)

                xq_guard = [None]
                for tau in range(NNT):
                    def _xq(s, tau=tau):
                        p.wait(s, "sg")
                        if xq_guard[0] is not None:
                            p.wait_val(s, xq_guard[0][0], xq_guard[0][1])
                        for ic in range(NC):
                            p.dma(s.dma_start(
                                out=XQG[:, ic, :],
                                in_=scr_ln[tau * 512:(tau + 1) * 512,
                                           ic * 128:(ic + 1) * 128],
                                transpose=True), "sx")
                    blk.sync(_xq)
                    if tag == 1 and tau == 0:
                        tap("b1s", B1S[:, :])
                        tap("xqg", XQG[:, :, :])

                    for hg in range(8):
                        def _w1(s, tau=tau, hg=hg):
                            w1gp.acq(hg % 2, s)
                            p.dma(s.dma_start(
                                out=W1G[hg % 2][:, :, :],
                                in_=w1tr[:, :, hg * 512:(hg + 1) * 512]), "sx")
                        blk.sync(_w1)
                        for h4 in range(4):
                            hc = hg * 4 + h4
                            bi = hc % 4

                            def _m1(t, hg=hg, h4=h4, bi=bi):
                                p.wait(t, "sx")
                                bankA.acq(bi, t)
                                for ic in range(NC):
                                    mm = t.matmul(
                                        psA[bi][:, :],
                                        W1G[hg % 2][:, ic,
                                                    h4 * 128:(h4 + 1) * 128],
                                        XQG[:, ic, :], start=(ic == 0),
                                        stop=(ic == NC - 1))
                                P_E_INC(t, mm)
                                if h4 == 3:
                                    w1gp.rel(hg % 2, "sp")
                            blk.tensor(_m1)

                            def _e1(sc, hc=hc, bi=bi, h4=h4):
                                p.wait(sc, "sp", "sv")
                                sl = slice((h4 % 2) * 512,
                                           (h4 % 2) * 512 + 512)
                                p.inc(sc.activation(T1f[:, sl], psA[bi][:, :],
                                                    AF.Identity, scale=1024.0,
                                                    bias=B1S[:, hc:hc + 1]),
                                      "ss", 1)
                                bankA.rel(bi, "ss")
                            bS(_e1)
                            if tag == 1 and tau == 0 and hc == 0:
                                tap("h0", T1f[:, 0:512])
                            if h4 % 2 == 0:
                                continue
                            hc0 = hc - 1

                            def _e2(v, hc0=hc0):
                                p.wait(v, "ss")
                                AI2 = T1f.ap().bitcast(I32)[:, 0:1024]
                                v.tensor_copy(NIi[:, 0:1024], T1f[:, :])
                                hqv = HQ[:, hc0:hc0 + 2, :].rearrange(
                                    "p a b -> p (a b)")
                                v.tensor_scalar(AI2, NIi[:, 0:1024], -128, 6,
                                                OP.bitwise_and,
                                                OP.arith_shift_right)
                                v.tensor_scalar(NIi[:, 0:1024], NIi[:, 0:1024],
                                                127, None, OP.bitwise_and)
                                v.scalar_tensor_tensor(hqv, NIi[:, 0:1024],
                                                       65.0, AI2,
                                                       OP.is_ge, OP.add)
                                p.inc(v.tensor_scalar(hqv, hqv,
                                                      0.0, 127.0, OP.max,
                                                      OP.min), "sv", 1)
                            bV(_e2)
                    if tag == 1:
                        if tau == 0:
                            tap("hq", HQ[:, :, :])

                    for ts4 in range(4):
                        tt = tau * 4 + ts4
                        for on in range(2):
                            load_brow(on % 2, 6, on * 512, 512)
                            bi = (ts4 * 2 + on) % 4

                            def _m2(t, ts4=ts4, on=on, bi=bi):
                                p.wait(t, "sv", "sx")
                                bankA.acq(bi, t)
                                for hc in range(NHC):
                                    t.matmul(
                                        psA[bi][:, :],
                                        HQ[:, hc, ts4 * 128:(ts4 + 1) * 128],
                                        W2R[:, hc, on * 512:(on + 1) * 512],
                                        start=(hc == 0), stop=False)
                                P_E_INC(t, t.matmul(psA[bi][:, :], ONESR[0:1, 0:128],
                                               BROW[on % 2][0:1, 0:512],
                                               start=False, stop=True),
                                      "sp", 1)
                                browp.rel(on % 2, "sp")
                            blk.tensor(_m2)

                            def _r2(v, tt=tt, on=on, bi=bi):
                                p.wait(v, "sp", "ss", "sx", "sg")
                                xa = X[:, tt, on * 512:(on + 1) * 512]
                                p.inc(v.tensor_tensor(xa, xa, psA[bi][:, :],
                                                      OP.add), "sv", 1)
                                bankA.rel(bi, "sv")
                            bV(_r2)
                    xq_guard[0] = ("sp", p.cnt["sp"])

        tap("w1t", scr_w1T)
        tap("w2q", scr_w2q)
        emit_ffn(1)
        tap("x2", X[:, :, :])

        emit_ln("plain")
        with ExitStack() as bst:
            QF = sbt("QFb", [128, NC, TC], BF16, bst)
            KVB = sbt("KVBb", [128, 512], BF16, bst)
            innb = ExitStack()
            KT = sbt("KTb", [128, NTT, D], BF16, innb)
            VT = sbt("VTb", [128, NTT, D], BF16, innb)
            ARN = sbt("ARNb", [128, 2048], F32, innb)
            KVU = sbt("KVUb", [128, 512], F32, innb)
            ZZ = ARN[0:16, :]

            barrier("sync", "gpsimd", "tensor", "vector", "scalar")

            with ExitStack() as qst:
                LG = sbt("LGb", [128, NC, 512], BF16, qst)
                WG = sbt("WGb", [128, NC, 512], BF16, qst)

                def fm_evac(dst_ap, bi):
                    def _e1(sc, bi=bi):
                        p.wait(sc, "sp", "sv")
                        sc.activation(T2f[:, :], psA[bi][:, :], AF.Relu,
                                      scale=-1.0)
                        sc.activation(QS2[:, :], psA[bi][:, :], AF.Relu)
                        p.inc(sc.activation(QS1[:, :], T2f[:, :], AF.Exp,
                                            scale=-1.0), "ss", 1)
                        bankA.rel(bi, "ss")
                    bS(_e1)

                    def _e3(v, dst_ap=dst_ap):
                        p.wait(v, "ss")
                        p.inc(v.tensor_tensor(dst_ap, QS2[:, :], QS1[:, :],
                                              OP.add), "sv", 1)
                    bV(_e3)

                def q_evac(nt, oc, bi):
                    fm_evac(QF[:, oc, nt * 512:(nt + 1) * 512], bi)
                emit_fm_proj(scr_wq, 0, 2, 0, LG, WG, q_evac)

                def kv_evac(tag, on, tt, bi):
                    if tag == "k":
                        fm_evac(KT[:, tt, on * 512:(on + 1) * 512], bi)
                    else:
                        def _e(sc, on=on, tt=tt, bi=bi):
                            p.wait(sc, "sp")
                            p.inc(sc.copy(
                                VT[:, tt, on * 512:(on + 1) * 512],
                                psA[bi][:, :]), "ss", 1)
                            bankA.rel(bi, "ss")
                        bS(_e)

                wkr = scr_wk.rearrange("(ic p) o -> p ic o", p=128)
                wvr = scr_wv.rearrange("(ic p) o -> p ic o", p=128)
                emit_tm_proj([(wkr, 0, 1024, 3, 0, "k"),
                              (wvr, 0, 1024, 4, 0, "v")], LG, WG, kv_evac)

            tap("bqm", QF[:, :, :])
            tap("bkm", KT[:, :, :])

            mstb = ExitStack()
            BSTG = sbt("BSTGb", [16, 2048], BF16, mstb)

            emit_tm_colsum(KT, KSUM)
            emit_kv_einsum(KT, VT, KVU)

            def _pk(v):
                p.wait(v, "sv", "ss")
                p.inc(v.engine_nop(), "sv", 1)
            bV(_pk)
            emit_allreduce([(KSUM[:, :], cc_c_i, cc_c_o),
                            (KVU[:, :], cc_d_i, cc_d_o)])
            tap("bkvu", KVU[:, :])

            def _kvb(v):
                p.wait(v, "sg")
                p.inc(v.tensor_copy(KVB[:, :], KVU[:, :]), "sv", 1)
            bV(_kvb)

            emit_masked_stat(lambda fc: KSUM[:, fc:fc + 1], 0.0,
                             lambda fc, nt: QF[:, fc, nt * 512:(nt + 1) * 512],
                             4)

            ZT = ARN[32:48, :]

            def _z(sc):
                p.wait(sc, "sp")
                for nt in range(NNT):
                    sc.activation(ZT[:, nt * 512:(nt + 1) * 512],
                                  psA[nt][0:16, :], AF.Identity, bias=EPS)
                p.inc(snop(sc), "ss", 1)
            bS(_z)
            for i in range(4):
                bankA.rel(i, "ss")

            def _zr(sc):
                p.wait(sc, "ss")
                sc.activation(ARN[64:80, :], ZT[:, :], AF.Ln)
                p.inc(sc.activation(ZZ[:, :], ARN[64:80, :], AF.Exp,
                                    scale=-1.0), "ss", 1)
            bS(_zr)

            def _zrc(v):
                p.wait(v, "ss")
                p.inc(v.tensor_copy(BSTG[0:16, :], ZZ[:, :]), "sv", 1)
            bV(_zrc)
            tap("bz", ZZ)

            emit_bcast_scale(lambda fc, nt: QF[:, fc, nt * 512:(nt + 1) * 512], BSTG)
            mstb.close()
            barrier("sync", "gpsimd", "tensor", "vector", "scalar")
            innb.close()
            emit_out_proj(KVB, QF, scr_wo, 5)

        tap("x3", X[:, :, :])

        emit_ffn(2)
        tap("x4", X[:, :, :])

        def _stv(g):
            p.wait(g, "sv", "ss", "sp")
            yw = y_d.rearrange("(t p) f -> p t f", p=128)
            for g4 in range(4):
                p.dma(g.dma_start(out=yw[:, 4 * g4:4 * g4 + 4, :],
                                  in_=X[:, 4 * g4:4 * g4 + 4, :]), "sg")
            g.wait_ge(p.h["sg"], p.cnt["sg"])
        blk.gpsimd(_stv)

    return nc


def kernel(**inputs):
    global _DBG_RESULT
    x = np.ascontiguousarray(np.asarray(inputs["L0_attn_a_x"], dtype=np.float32))
    common = {
        "aqkv_w": np.ascontiguousarray(inputs["L0_attn_a_qkv_w"], dtype=np.float32),
        "aqkv_b": np.ascontiguousarray(inputs["L0_attn_a_qkv_b"], dtype=np.float32).reshape(1, -1),
        "aproj_w": np.ascontiguousarray(inputs["L0_attn_a_proj_w"], dtype=np.float32),
        "aproj_b": np.ascontiguousarray(inputs["L0_attn_a_proj_b"], dtype=np.float32).reshape(1, -1),
        "b_wq": np.ascontiguousarray(np.asarray(inputs["L0_attn_b_wq"], dtype=np.float32).reshape(D, D)),
        "b_wk": np.ascontiguousarray(np.asarray(inputs["L0_attn_b_wk"], dtype=np.float32).reshape(D, D)),
        "b_wv": np.ascontiguousarray(np.asarray(inputs["L0_attn_b_wv"], dtype=np.float32).reshape(D, D)),
        "b_wo": np.ascontiguousarray(np.asarray(inputs["L0_attn_b_wo"], dtype=np.float32).reshape(D, D)),
        "b_bq": np.ascontiguousarray(np.asarray(inputs["L0_attn_b_bq"], dtype=np.float32).reshape(1, -1)),
        "b_bk": np.ascontiguousarray(np.asarray(inputs["L0_attn_b_bk"], dtype=np.float32).reshape(1, -1)),
        "b_bv": np.ascontiguousarray(np.asarray(inputs["L0_attn_b_bv"], dtype=np.float32).reshape(1, -1)),
        "b_bo": np.ascontiguousarray(np.asarray(inputs["L0_attn_b_bo"], dtype=np.float32).reshape(1, -1)),
        "f_w1": np.ascontiguousarray(inputs["L0_ffn_w1"], dtype=np.float32),
        "f_b1": np.ascontiguousarray(
            np.asarray(inputs["L0_ffn_b1"], dtype=np.float32).reshape(NHC, 128).T),
        "f_w2": np.ascontiguousarray(inputs["L0_ffn_w2"], dtype=np.float32),
        "f_b2": np.ascontiguousarray(np.asarray(inputs["L0_ffn_b2"], dtype=np.float32).reshape(1, -1)),
    }
    in_maps = []
    for c in range(8):
        bb, half = c // 2, c % 2
        m = dict(common)
        m["x_in"] = np.ascontiguousarray(x[bb, half * TC:(half + 1) * TC, :])
        in_maps.append(m)
    nc = build_core_program(dbg_stage=_DBG_STAGE)
    res = run_bass_kernel_spmd(nc, in_maps, core_ids=list(range(8)))
    out = np.empty((B, S, D), dtype=np.float32)
    for c in range(8):
        bb, half = c // 2, c % 2
        out[bb, half * TC:(half + 1) * TC, :] = res.results[c]["y_out"]
    if _DBG_STAGE is not None:
        _DBG_RESULT = [res.results[c].get("dbg") for c in range(8)]
    return out


# version 04
# speedup vs baseline: 1.0571x; 1.0121x over previous
import numpy as np
from contextlib import ExitStack

import concourse.bass as bass
import concourse.mybir as mybir
from concourse.bass_utils import run_bass_kernel_spmd

F32 = mybir.dt.float32
BF16 = mybir.dt.bfloat16
I32 = mybir.dt.int32
OP = mybir.AluOpType
AF = mybir.ActivationFunctionType
AX = mybir.AxisListType.X

B, S, D, H, HD = 4, 4096, 1024, 16, 64
TC = 2048
NTT = 16
NNT = 4
NC = 8
HID = 4096
NHC = 32
EPS = 1e-6
LNEPS = 1e-5
RG = [[0, 1], [2, 3], [4, 5], [6, 7]]

_DBG_STAGE = None
_DBG_RESULT = None
_TAP_SHAPES = {
    "x_loaded": [128, NTT, D],
    "ln1": [TC, D],
    "ln1stat": [128, 64],
    "q_fm": [128, NC, TC],
    "k_tm": [128, NTT, D],
    "v_tm": [128, NTT, D],
    "ksum": [128, 8],
    "qsum": [128, 8],
    "si": [16, TC],
    "so": [16, TC],
    "kso": [128, 8],
    "qsi": [128, 8],
    "sa": [16, TC],
    "e": [16, TC],
    "kvu_pre": [128, 512],
    "kvu": [128, 512],
    "norm": [16, 4],
    "scq": [16, TC],
    "qfin": [128, NC, TC],
    "outa": [128, NC, 512],
    "projw": [D, D],
    "w1t": [D, HID],
    "w2q": [D, HID],
    "x0b": [128, NTT, D],
    "x1": [128, NTT, D],
    "xq": [TC, D],
    "b1s": [128, NHC],
    "xqg": [128, NC, 512],
    "h0": [128, 512],
    "hq": [128, NHC, 512],
    "x2": [128, NTT, D],
    "bqm": [128, NC, TC],
    "bkm": [128, NTT, D],
    "bz": [16, TC],
    "bkvu": [128, 512],
    "x3": [128, NTT, D],
    "x4": [128, NTT, D],
}


class Prog:
    def __init__(self, nc, stack, sem_names):
        self.nc = nc
        self.h = {n: stack.enter_context(nc.semaphore(n)) for n in sem_names}
        self.cnt = {n: 0 for n in sem_names}
        self._auto = set()

    def inc(self, instr, name, k=1):
        key = (id(instr.ins), name)
        if key in self._auto:
            return instr
        instr.then_inc(self.h[name], k)
        self.cnt[name] += k
        return instr

    def dma(self, instr, name):
        return self.inc(instr, name, 16)

    def wait(self, eng, *names):
        for n in names:
            if self.cnt[n] > 0:
                eng.wait_ge(self.h[n], self.cnt[n])

    def wait_val(self, eng, name, val):
        if val > 0:
            eng.wait_ge(self.h[name], val)


class Pool:

    def __init__(self, p, n):
        self.p = p
        self.st = [None] * n

    def acq(self, i, eng):
        s = self.st[i % len(self.st)]
        if s is not None:
            self.p.wait_val(eng, s[0], s[1])

    def rel(self, i, sem):
        self.st[i % len(self.st)] = (sem, self.p.cnt[sem])


def build_core_program(dbg_stage=None):
    nc = bass.Bass("TRN2", target_bir_lowering=False, debug=False)

    x_d = nc.dram_tensor("x_in", [TC, D], F32, kind="ExternalInput").ap()
    qkvw_d = nc.dram_tensor("aqkv_w", [3 * D, D], F32, kind="ExternalInput").ap()
    qkvb_d = nc.dram_tensor("aqkv_b", [1, 3 * D], F32, kind="ExternalInput").ap()
    projw_d = nc.dram_tensor("aproj_w", [D, D], F32, kind="ExternalInput").ap()
    projb_d = nc.dram_tensor("aproj_b", [1, D], F32, kind="ExternalInput").ap()
    bwq_d = nc.dram_tensor("b_wq", [D, D], F32, kind="ExternalInput").ap()
    bwk_d = nc.dram_tensor("b_wk", [D, D], F32, kind="ExternalInput").ap()
    bwv_d = nc.dram_tensor("b_wv", [D, D], F32, kind="ExternalInput").ap()
    bwo_d = nc.dram_tensor("b_wo", [D, D], F32, kind="ExternalInput").ap()
    bbq_d = nc.dram_tensor("b_bq", [1, D], F32, kind="ExternalInput").ap()
    bbk_d = nc.dram_tensor("b_bk", [1, D], F32, kind="ExternalInput").ap()
    bbv_d = nc.dram_tensor("b_bv", [1, D], F32, kind="ExternalInput").ap()
    bbo_d = nc.dram_tensor("b_bo", [1, D], F32, kind="ExternalInput").ap()
    w1_d = nc.dram_tensor("f_w1", [HID, D], F32, kind="ExternalInput").ap()
    b1_d = nc.dram_tensor("f_b1", [128, NHC], F32, kind="ExternalInput").ap()
    w2_d = nc.dram_tensor("f_w2", [D, HID], F32, kind="ExternalInput").ap()
    b2_d = nc.dram_tensor("f_b2", [1, D], F32, kind="ExternalInput").ap()
    y_d = nc.dram_tensor("y_out", [TC, D], F32, kind="ExternalOutput").ap()

    scr_ln = nc.dram_tensor("scr_ln", [TC, D], BF16, kind="Internal").ap()
    scr_k = nc.dram_tensor("scr_k", [TC, D], BF16, kind="Internal").ap()
    scr_tmp = nc.dram_tensor("scr_tmp", [HID, D], BF16, kind="Internal").ap()
    scr_qkvT = nc.dram_tensor("scr_qkvT", [D, 3 * D], BF16, kind="Internal").ap()
    scr_projT = nc.dram_tensor("scr_projT", [D, D], BF16, kind="Internal").ap()
    scr_w1T = nc.dram_tensor("scr_w1T", [D, HID], BF16, kind="Internal").ap()
    scr_w2q = nc.dram_tensor("scr_w2q", [D, HID], BF16, kind="Internal").ap()
    scr_wq = nc.dram_tensor("scr_wq", [D, D], BF16, kind="Internal").ap()
    scr_wk = nc.dram_tensor("scr_wk", [D, D], BF16, kind="Internal").ap()
    scr_wv = nc.dram_tensor("scr_wv", [D, D], BF16, kind="Internal").ap()
    scr_wo = nc.dram_tensor("scr_wo", [D, D], BF16, kind="Internal").ap()
    scr_bias = nc.dram_tensor("scr_bias", [8, 3 * D], BF16, kind="Internal").ap()
    cc_a_i = nc.dram_tensor("cc_a_i", [128, 16], F32, kind="Internal").ap()
    cc_a_o = nc.dram_tensor("cc_a_o", [128, 16], F32, kind="Internal").ap()
    cc_b_i = nc.dram_tensor("cc_b_i", [16, 1024], F32, kind="Internal").ap()
    cc_b_o = nc.dram_tensor("cc_b_o", [16, 1024], F32, kind="Internal").ap()
    cc_c_i = nc.dram_tensor("cc_c_i", [128, 8], F32, kind="Internal").ap()
    cc_c_o = nc.dram_tensor("cc_c_o", [128, 8], F32, kind="Internal").ap()
    cc_d_i = nc.dram_tensor("cc_d_i", [128, 512], F32, kind="Internal").ap()
    cc_d_o = nc.dram_tensor("cc_d_o", [128, 512], F32, kind="Internal").ap()
    cc_e_i = nc.dram_tensor("cc_e_i", [16, 4], F32, kind="Internal").ap()
    cc_e_o = nc.dram_tensor("cc_e_o", [16, 4], F32, kind="Internal").ap()
    dbg_d = None
    if dbg_stage is not None:
        dbg_d = nc.dram_tensor("dbg", _TAP_SHAPES[dbg_stage], F32,
                               kind="ExternalOutput").ap()

    with ExitStack() as top:
        def sbt(name, shape, dt, stk):
            return stk.enter_context(nc.sbuf_tensor(name, shape, dt))

        sb = lambda name, shape, dt: sbt(name, shape, dt, top)

        X = sb("X", [128, NTT, D], F32)
        T1f = sb("T1f", [128, 1024], F32)
        T2f = sb("T2f", [128, 512], F32)
        QS1 = sb("QS1", [128, 512], F32)
        QS2 = sb("QS2", [128, 512], F32)
        NIi = sb("NIi", [128, 1024], I32)
        STAT = sb("STAT", [128, 64], F32)
        TA = sb("TA", [128, 16], F32)
        TB = sb("TB", [128, 16], F32)
        TCH = sb("TCH", [128, 16], F32)
        IDENT = sb("IDENT", [128, 16], F32)
        IOTC = sb("IOTC", [128, 1], F32)
        M0 = sb("M0", [16, 1024], BF16)
        ONESC = sb("ONESC", [128, 1], BF16)
        ONESR = sb("ONESR", [1, 512], BF16)
        BROW = [sb("BROW0", [1, 512], BF16), sb("BROW1", [1, 512], BF16)]
        B1S = sb("B1S", [128, NHC], F32)
        KSUM = sb("KSUM", [128, 8], F32)
        QSUM = sb("QSUM", [128, 8], F32)
        QSI = sb("QSI", [128, 8], F32)
        KSOC = sb("KSOC", [128, 8], F32)
        CB = sb("CB", [16, 4], F32)
        ESCOL = sb("ESCOL", [16, 4], F32)
        MSK = sb("MSK", [128, 2, 16], BF16)
        NOPC = sb("NOPC", [1, 4], F32)
        CONSTS = sb("CONSTS", [128, 4], F32)
        CONSTS2 = sb("CONSTS2", [128, 2], F32)
        nc.const_aps.register(F32, EPS, CONSTS[:, 0:1])
        nc.const_aps.register(F32, 1024.0, CONSTS[:, 1:2])
        nc.const_aps.register(F32, -0.5, CONSTS[:, 2:3])
        nc.const_aps.register(F32, -1.0, CONSTS[:, 3:4])
        LOG4096 = 8.317766166719343
        nc.const_aps.register(F32, LOG4096, CONSTS2[:, 0:1])

        psA = [top.enter_context(nc.psum_tensor("psA%d" % i, [128, 512], F32))
               for i in range(4)]
        psB = [top.enter_context(nc.psum_tensor("psB%d" % i, [128, 512], F32))
               for i in range(2)]
        psT = top.enter_context(nc.psum_tensor("psT", [128, 128], F32))
        psK = top.enter_context(nc.psum_tensor("psK", [128, 64], F32))

        p = Prog(nc, top, ["sx", "sg", "sp", "sv", "ss", "scc"])
        blk = top.enter_context(nc.Block())
        bankA = Pool(p, 4)
        bankB = Pool(p, 2)
        browp = Pool(p, 2)
        mskp = Pool(p, 2)

        _VOPS = {"tensor_scalar", "tensor_tensor", "tensor_copy", "reduce_sum",
                 "reduce_max", "tensor_add", "tensor_sub", "tensor_mul",
                 "tensor_max", "memset", "scalar_tensor_tensor",
                 "tensor_single_scalar", "tensor_scalar_mul", "tensor_relu"}
        _SOPS = {"activation", "copy", "add", "mul", "square", "sqrt", "sign"}

        class Chained:
            def __init__(self, eng, sem, ops):
                self._e = eng
                self._sem = sem
                self._ops = ops

            def __getattr__(self, name):
                f = getattr(self._e, name)
                if name in self._ops:
                    def g(*a, **k):
                        p.wait(self._e, self._sem)
                        ins = f(*a, **k)
                        p.inc(ins, self._sem, 1)
                        p._auto.add((id(ins.ins), self._sem))
                        return ins
                    return g
                return f

        def bV(fn):
            blk.vector(lambda v: fn(Chained(v, "sv", _VOPS)))

        def bS(fn):
            blk.scalar(lambda sc: fn(Chained(sc, "ss", _SOPS)))

        def snop(sc):
            return sc.copy(NOPC[0:1, 0:1], NOPC[0:1, 2:3])

        def P_E_INC(t, ins, *a):
            ins = p.inc(pe_fence(t), "sp", 1)
            p.wait(t, "sp")
            return ins

        def pe_fence(t):
            import os as _o
            if _o.environ.get("NOFENCE"):
                return t.matmul(psT[0:1, 127:128], ONESC[0:1, 0:1],
                                ONESC[0:1, 0:1], start=True, stop=True,
                                skip_group_check=True)
            t.matmul(psT[0:1, 127:128], ONESC[0:1, 0:1],
                     ONESC[0:1, 0:1], start=True, stop=True,
                     skip_group_check=True)
            return t.matmul(psT[0:1, 127:128], ONESC[0:1, 0:1],
                            ONESC[0:1, 0:1], start=True, stop=True,
                            skip_group_check=True)

        def tap(stage, src_ap):
            if dbg_stage != stage:
                return

            def _t(g):
                p.wait(g, "sp", "sv", "ss", "sx", "sg", "scc")
                p.dma(g.dma_start(out=dbg_d, in_=src_ap), "sg")
                g.wait_ge(p.h["sg"], p.cnt["sg"])
                p.inc(g.engine_nop(), "sg", 1)
            blk.gpsimd(_t)
            barrier("vector", "scalar", "tensor", "sync")

        def barrier(*engs):
            for name in engs:
                def _b(e, name=name):
                    p.wait(e, "sx", "sg", "sp", "sv", "ss", "scc")
                getattr(blk, name)(_b)

        bias_ready = [0]

        def load_brow(slot, row, col0, n):
            def _l(s, slot=slot, row=row, col0=col0, n=n):
                p.wait_val(s, "sg", bias_ready[0])
                st0 = browp.st[slot]
                if st0 is not None:
                    p.wait_val(s, st0[0], st0[1])
                p.dma(s.dma_start(out=BROW[slot][0:1, 0:n],
                                  in_=scr_bias[row:row + 1, col0:col0 + n]), "sx")
            blk.sync(_l)

        with ExitStack() as cst:
            IOTJ = sbt("IOTJ", [128, 128], F32, cst)
            M0A = sbt("M0A", [16, 1024], F32, cst)
            M0B = sbt("M0B", [16, 1], F32, cst)

            def _c(g):
                g.iota(IOTC[:, :], pattern=[[1, 1]], channel_multiplier=1,
                       allow_small_or_imprecise_dtypes=True)
                g.iota(IOTJ[:, :], pattern=[[1, 128]], channel_multiplier=0,
                       allow_small_or_imprecise_dtypes=True)
                g.iota(M0A[0:16, :].rearrange("p (s d) -> p s d", d=64),
                       pattern=[[1, 16], [0, 64]], channel_multiplier=0,
                       allow_small_or_imprecise_dtypes=True)
                p.inc(g.iota(M0B[0:16, :], pattern=[[1, 1]], channel_multiplier=1,
                             allow_small_or_imprecise_dtypes=True), "sg", 1)
            blk.gpsimd(_c)

            def _c2(v):
                p.wait(v, "sg")
                v.tensor_copy(NIi[:, 0:1], IOTC[:, :])
                v.tensor_scalar(NIi[:, 0:1], NIi[:, 0:1], 15, None,
                                OP.bitwise_and)
                v.tensor_copy(T1f[:, 0:1], NIi[:, 0:1])
                v.tensor_tensor(IDENT[:, :], IOTJ[:, 0:16],
                                T1f[:, 0:1].broadcast_to([128, 16]),
                                OP.is_equal)
                v.tensor_tensor(M0[0:16, :], M0A[0:16, :],
                                M0B[0:16, 0:1].broadcast_to([16, 1024]),
                                OP.is_equal)
                v.memset(ONESC[:, :], 1.0)
                v.memset(ESCOL[0:16, 0:4], 0.0)
                v.memset(NOPC[:, :], 0.0)
                v.memset(CONSTS[:, 0:1], EPS)
                v.memset(CONSTS[:, 1:2], 1024.0)
                v.memset(CONSTS[:, 2:3], -0.5)
                v.memset(CONSTS[:, 3:4], -1.0)
                v.memset(CONSTS2[:, 0:1], 8.317766166719343)
                p.inc(v.memset(ONESR[:, :], 1.0), "sv", 1)
            bV(_c2)

        def _lx(s):
            xr = x_d.rearrange("(t p) f -> p t f", p=128)
            for g4 in range(4):
                p.dma(s.dma_start(out=X[:, 4 * g4:4 * g4 + 4, :],
                                  in_=xr[:, 4 * g4:4 * g4 + 4, :]), "sx")
        blk.sync(_lx)

        def _lb(g):
            p.dma(g.dma_start(out=scr_bias[0:1, 0:3 * D], in_=qkvb_d), "sg")
            p.dma(g.dma_start(out=scr_bias[1:2, 0:D], in_=projb_d), "sg")
            p.dma(g.dma_start(out=scr_bias[2:3, 0:D], in_=bbq_d), "sg")
            p.dma(g.dma_start(out=scr_bias[3:4, 0:D], in_=bbk_d), "sg")
            p.dma(g.dma_start(out=scr_bias[4:5, 0:D], in_=bbv_d), "sg")
            p.dma(g.dma_start(out=scr_bias[5:6, 0:D], in_=bbo_d), "sg")
            p.dma(g.dma_start(out=scr_wq, in_=bwq_d), "sg")
            p.dma(g.dma_start(out=scr_wk, in_=bwk_d), "sg")
            p.dma(g.dma_start(out=scr_wv, in_=bwv_d), "sg")
            p.dma(g.dma_start(out=scr_wo, in_=bwo_d), "sg")
        blk.gpsimd(_lb)
        bias_ready[0] = p.cnt["sg"]

        with ExitStack() as wst:
            WSTG = sbt("WSTG", [128, 4096], BF16, wst)
            WF = sbt("WF", [128, 1024], F32, wst)

            def emit_transposed_scratch(src_bf16, dst_T, O):
                for ic in range(NC):
                    def _t(s, ic=ic):
                        p.wait(s, "sx", "sg")
                        p.dma(s.dma_start(
                            out=WSTG[:, 0:O],
                            in_=src_bf16[0:O, ic * 128:(ic + 1) * 128],
                            transpose=True), "sx")
                        p.wait(s, "sx")
                        p.dma(s.dma_start(
                            out=dst_T[ic * 128:(ic + 1) * 128, 0:O],
                            in_=WSTG[:, 0:O]), "sx")
                    blk.sync(_t)

            def _cast_qkv(g):
                p.wait(g, "sg")
                p.dma(g.dma_start(out=scr_tmp[0:3 * D, :], in_=qkvw_d), "sg")
            blk.gpsimd(_cast_qkv)
            emit_transposed_scratch(scr_tmp[0:3 * D, :], scr_qkvT, 3 * D)

            def _cast_proj(g):
                p.wait(g, "sx")
                p.dma(g.dma_start(out=scr_tmp[0:D, :], in_=projw_d), "sg")
            blk.gpsimd(_cast_proj)
            emit_transposed_scratch(scr_tmp[0:D, :], scr_projT, D)

            def gq_chain(src, dst, pp, n, scale):
                def _v1(v):
                    p.wait(v, "sx", "ss", "sg")
                    p.inc(v.tensor_scalar(QS1[0:pp, 0:n], src, 64.0, 0.5,
                                          OP.mult, OP.add), "sv", 1)
                bV(_v1)

                def _s1(sc):
                    p.wait(sc, "sv")
                    sc.copy(NIi[0:pp, 0:n], QS1[0:pp, 0:n])
                    p.inc(sc.copy(QS2[0:pp, 0:n], NIi[0:pp, 0:n]), "ss", 1)
                bS(_s1)

                def _v2(v):
                    p.wait(v, "ss")
                    v.tensor_tensor(T2f[0:pp, 0:n], QS1[0:pp, 0:n],
                                    QS2[0:pp, 0:n], OP.is_ge)
                    v.scalar_tensor_tensor(QS1[0:pp, 0:n], QS2[0:pp, 0:n], 1.0,
                                           T2f[0:pp, 0:n], OP.subtract, OP.add)
                    v.tensor_scalar(QS1[0:pp, 0:n], QS1[0:pp, 0:n], scale,
                                    None, OP.mult)
                    p.inc(v.tensor_scalar(dst, QS1[0:pp, 0:n], -128.0 * scale,
                                          127.0 * scale, OP.max, OP.min),
                          "sv", 1)
                bV(_v2)

            w1r = w1_d.rearrange("(r p) i -> p r i", p=128)
            tmpr = scr_tmp.rearrange("(r p) i -> p r i", p=128)
            for r in range(HID // 128):
                def _lw(s, r=r):
                    p.wait(s, "sv")
                    p.dma(s.dma_start(out=WF[:, :], in_=w1r[:, r, :]), "sx")
                blk.sync(_lw)
                for hh in range(2):
                    gq_chain(WF[:, hh * 512:(hh + 1) * 512],
                             WSTG[:, hh * 512:(hh + 1) * 512], 128, 512,
                             1.0 / 1024.0)

                def _sw(g, r=r):
                    p.wait(g, "sv", "sx")
                    p.dma(g.dma_start(out=tmpr[:, r, :], in_=WSTG[:, 0:1024]),
                          "sg")
                blk.gpsimd(_sw)
            emit_transposed_scratch(scr_tmp[0:HID, :], scr_w1T, HID)

            w2r = w2_d.rearrange("(r p) i -> p r i", p=128)
            w2qr = scr_w2q.rearrange("(r p) i -> p r i", p=128)
            for r in range(D // 128):
                for hh in range(4):
                    def _l2(s, r=r, hh=hh):
                        p.wait(s, "sv", "sg")
                        p.dma(s.dma_start(
                            out=WF[:, :],
                            in_=w2r[:, r, hh * 1024:(hh + 1) * 1024]), "sx")
                    blk.sync(_l2)
                    for q2 in range(2):
                        gq_chain(WF[:, q2 * 512:(q2 + 1) * 512],
                                 WSTG[:, q2 * 512:(q2 + 1) * 512], 128, 512,
                                 1.0 / 1024.0)

                    def _s2(g, r=r, hh=hh):
                        p.wait(g, "sv")
                        p.dma(g.dma_start(
                            out=w2qr[:, r, hh * 1024:(hh + 1) * 1024],
                            in_=WSTG[:, 0:1024]), "sg")
                    blk.gpsimd(_s2)

            def _lb1(s):
                p.wait(s, "sv", "sg")
                p.dma(s.dma_start(out=WF[:, 0:NHC], in_=b1_d), "sx")
            blk.sync(_lb1)
            gq_chain(WF[:, 0:NHC], WSTG[:, 0:NHC], 128, NHC, 1.0 / 64.0)

            def _b1s(v):
                p.wait(v, "sv")
                p.inc(v.tensor_scalar(B1S[:, :], WSTG[:, 0:NHC], 1024.0, 32.0,
                                      OP.mult, OP.add), "sv", 1)
            bV(_b1s)

            def _lb2(s):
                p.wait(s, "sv")
                p.dma(s.dma_start(out=WF[0:1, 0:D], in_=b2_d), "sx")
            blk.sync(_lb2)
            for q2 in range(2):
                gq_chain(WF[0:1, q2 * 512:(q2 + 1) * 512],
                         WSTG[0:1, q2 * 512:(q2 + 1) * 512], 1, 512, 1.0 / 64.0)

            def _sb2(g):
                p.wait(g, "sv")
                p.dma(g.dma_start(out=scr_bias[6:7, 0:D], in_=WSTG[0:1, 0:D]),
                      "sg")
            blk.gpsimd(_sb2)
            bias_ready[0] = p.cnt["sg"]

        ln_n = [0]

        def emit_ln(mode, tag=""):
            barrier("vector", "scalar")
            ln_n[0] += 1
            lst = ExitStack()
            LNSTG = sbt("LNS%d" % ln_n[0], [128, 1024], BF16, lst)

            def _st(v):
                for tt in range(NTT):
                    v.reduce_sum(STAT[:, tt:tt + 1], X[:, tt, :], AX)
                p.inc(v.engine_nop(), "sv", 1)
            bV(_st)

            def _sq(sc):
                for tt in range(NTT):
                    sc.activation(T1f[:, :], X[:, tt, :], AF.Square,
                                  accum_out=STAT[:, 16 + tt:17 + tt])
                p.inc(snop(sc), "ss", 1)
            bS(_sq)

            def _chain(v):
                p.wait(v, "sv", "ss")
                v.tensor_scalar(STAT[:, 32:48], STAT[:, 0:16], -1.0 / 1024.0,
                                None, OP.mult)
                v.tensor_tensor(TA[:, :], STAT[:, 32:48], STAT[:, 32:48],
                                OP.mult)
                v.tensor_scalar(TB[:, :], STAT[:, 16:32], 1.0 / 1024.0,
                                None, OP.mult)
                v.tensor_tensor(TCH[:, :], TB[:, :], TA[:, :], OP.subtract)
                p.inc(v.tensor_scalar(TA[:, :], TCH[:, :], LNEPS,
                                      None, OP.add), "sv", 1)
            bV(_chain)

            def _rs(sc):
                p.wait(sc, "sv")
                sc.activation(TB[:, :], TA[:, :], AF.Ln)
                p.inc(sc.activation(STAT[:, 0:16], TB[:, :], AF.Exp,
                                    scale=-0.5), "ss", 1)
            bS(_rs)
            tap(tag + "stat", STAT[:, :])
            if mode == "quant":
                def _r16(v):
                    p.wait(v, "ss")
                    p.inc(v.tensor_scalar(STAT[:, 0:16], STAT[:, 0:16], 16.0,
                                          None, OP.mult), "sv", 1)
                bV(_r16)

            lnw = scr_ln.rearrange("(t p) f -> p t f", p=128)
            for tt in range(NTT):
                def _ap(sc, tt=tt):
                    p.wait(sc, "sv")
                    p.inc(sc.activation(T1f[:, :], X[:, tt, :], AF.Identity,
                                        bias=STAT[:, 32 + tt:33 + tt],
                                        scale=1.0), "ss", 1)
                bS(_ap)
                if mode == "plain":
                    def _mul(v, tt=tt):
                        p.wait(v, "ss", "sg")
                        p.inc(v.tensor_scalar(LNSTG[:, :], T1f[:, :],
                                              STAT[:, tt:tt + 1], None,
                                              OP.mult), "sv", 1)
                    bV(_mul)
                else:
                    def _q(v, tt=tt):
                        p.wait(v, "ss")
                        p.inc(v.tensor_scalar(T1f[:, :], T1f[:, :],
                                              STAT[:, tt:tt + 1], 0.5,
                                              OP.mult, OP.add), "sv", 1)
                    bV(_q)

                    def _qv(v, tt=tt):
                        p.wait(v, "ss", "sg")
                        v.tensor_copy(NIi[:, 0:1024], T1f[:, :])
                        v.tensor_copy(QS1[:, 0:512], NIi[:, 0:512])
                        v.tensor_copy(QS2[:, 0:512], NIi[:, 512:1024])
                        for hh, f0 in ((0, QS1), (1, QS2)):
                            sl = slice(hh * 512, (hh + 1) * 512)
                            v.tensor_tensor(T2f[:, :], T1f[:, sl], f0[:, :],
                                            OP.is_ge)
                            v.scalar_tensor_tensor(T1f[:, sl], f0[:, :], 1.0,
                                                   T2f[:, :], OP.subtract,
                                                   OP.add)
                            v.tensor_scalar(LNSTG[:, sl], T1f[:, sl], -128.0,
                                            127.0, OP.max, OP.min)
                        p.inc(v.engine_nop(), "sv", 1)
                    bV(_qv)

                def _wr(g, tt=tt):
                    p.wait(g, "sv")
                    p.dma(g.dma_start(out=lnw[:, tt, :], in_=LNSTG[:, :]), "sg")
                blk.gpsimd(_wr)
            barrier("vector", "scalar", "sync", "gpsimd", "tensor")
            lst.close()

        def emit_allreduce(pairs):
            def _up(g):
                p.wait(g, "sv", "ss", "sp", "sx")
                for ap_s, d_i, d_o in pairs:
                    p.dma(g.dma_start(out=d_i, in_=ap_s), "sg")
                g.wait_ge(p.h["sg"], p.cnt["sg"])
                for ap_s, d_i, d_o in pairs:
                    p.inc(g.collective_compute("AllReduce", OP.add,
                                               replica_groups=RG,
                                               ins=[d_i], outs=[d_o]),
                          "scc", 1)
                g.wait_ge(p.h["scc"], p.cnt["scc"])
                for ap_s, d_i, d_o in pairs:
                    p.dma(g.dma_start(out=ap_s, in_=d_o), "sg")
            blk.gpsimd(_up)

        def build_mask(slot, fc, src_col, eps):
            def _m(v, slot=slot, fc=fc, eps=eps):
                mskp.acq(slot, v)
                v.memset(MSK[:, slot, :], 0.0)
                if eps != 0.0:
                    v.tensor_scalar(MSK[0:64, slot, 2 * fc:2 * fc + 1],
                                    src_col[0:64, :], eps, None, OP.add)
                    p.inc(v.tensor_scalar(
                        MSK[64:128, slot, 2 * fc + 1:2 * fc + 2],
                        src_col[64:128, :], eps, None, OP.add), "sv", 1)
                else:
                    v.tensor_copy(MSK[0:64, slot, 2 * fc:2 * fc + 1],
                                  src_col[0:64, :])
                    p.inc(v.tensor_copy(
                        MSK[64:128, slot, 2 * fc + 1:2 * fc + 2],
                        src_col[64:128, :]), "sv", 1)
            bV(_m)

        def emit_masked_stat(src_cols_fn, eps, rhs_fn, cb_col):
            for fc in range(NC):
                build_mask(fc % 2, fc, src_cols_fn(fc), eps)

                def _mm(t, fc=fc):
                    p.wait(t, "sv")
                    if fc == 0:
                        for i in range(4):
                            bankA.acq(i, t)
                    t.matmul(psT[0:16, cb_col:cb_col + 1], MSK[:, fc % 2, :],
                             ONESC[:, :], start=(fc == 0), stop=(fc == NC - 1))
                    for nt in range(NNT):
                        P_E_INC(t, t.matmul(psA[nt][0:16, :], MSK[:, fc % 2, :],
                                       rhs_fn(fc, nt), start=(fc == 0),
                                       stop=(fc == NC - 1)), "sp", 1)
                    mskp.rel(fc % 2, "sp")
                blk.tensor(_mm)

        def emit_bcast_scale(targ_fn, bstg):
            for fc in range(NC):
                for nt in range(NNT):
                    bi = (fc * NNT + nt) % 2

                    def _bm(t, fc=fc, nt=nt, bi=bi):
                        p.wait(t, "sv")
                        bankB.acq(bi, t)
                        P_E_INC(t, t.matmul(psB[bi][:, :],
                                       M0[0:16, fc * 128:(fc + 1) * 128],
                                       bstg[0:16, nt * 512:(nt + 1) * 512],
                                       start=True, stop=True), "sp", 1)
                    blk.tensor(_bm)

                    def _bvs(sc, bi=bi):
                        p.wait(sc, "sp", "sv")
                        p.inc(sc.copy(T2f[:, :], psB[bi][:, :]), "ss", 1)
                        bankB.rel(bi, "ss")
                    bS(_bvs)

                    def _bv(v, fc=fc, nt=nt):
                        p.wait(v, "ss")
                        targ = targ_fn(fc, nt)
                        p.inc(v.tensor_tensor(targ, targ, T2f[:, :],
                                              OP.mult), "sv", 1)
                    bV(_bv)

        def emit_out_proj(kvb, qf, projT_scr, pbias_row, stage_oa=None):
            with ExitStack() as ost:
                OA = sbt("OA" + str(p.cnt["sp"]), [128, NC, 512], BF16, ost)
                PWs = [sbt("PW" + str(p.cnt["sp"]) + "_" + str(i),
                           [128, NC, 512], BF16, ost) for i in range(2)]
                pwr = projT_scr.rearrange("(ic p) o -> p ic o", p=128)

                def _pwl(s):
                    p.wait(s, "sp", "sg")
                    for on in range(2):
                        p.dma(s.dma_start(
                            out=PWs[on][:, :, :],
                            in_=pwr[:, :, on * 512:(on + 1) * 512]), "sx")
                blk.sync(_pwl)
                oa_guard = [None]
                for nt in range(NNT):
                    for fc in range(NC):
                        bi = fc % 2

                        def _em(t, fc=fc, nt=nt, bi=bi):
                            p.wait(t, "sv")
                            bankB.acq(bi, t)
                            if fc == 0 and oa_guard[0] is not None:
                                p.wait_val(t, oa_guard[0][0], oa_guard[0][1])
                            t.matmul(psB[bi][0:64, :],
                                     kvb[0:64, fc * 64:(fc + 1) * 64],
                                     qf[0:64, fc, nt * 512:(nt + 1) * 512],
                                     start=True, stop=True,
                                     tile_position=(0, 0),
                                     skip_group_check=True)
                            P_E_INC(t, t.matmul(psB[bi][64:128, :],
                                           kvb[64:128, fc * 64:(fc + 1) * 64],
                                           qf[64:128, fc,
                                              nt * 512:(nt + 1) * 512],
                                           start=True, stop=True,
                                           tile_position=(64, 64),
                                           skip_group_check=True), "sp", 1)
                        blk.tensor(_em)

                        def _ev(sc, fc=fc, bi=bi):
                            p.wait(sc, "sp")
                            p.inc(sc.copy(OA[:, fc, :], psB[bi][:, :]), "ss", 1)
                            bankB.rel(bi, "ss")
                        bS(_ev)
                    if stage_oa is not None and nt == 0:
                        tap(stage_oa, OA[:, :, :])
                    for on in range(2):
                        load_brow(on % 2, pbias_row, on * 512, 512)
                        PW = PWs[on]
                        for ts4 in range(4):
                            tt = nt * 4 + ts4
                            bi = ts4 % 4

                            def _pm(t, on=on, ts4=ts4, bi=bi):
                                p.wait(t, "sx", "ss")
                                bankA.acq(bi, t)
                                for ic in range(NC):
                                    t.matmul(psA[bi][:, :],
                                             OA[:, ic, ts4 * 128:(ts4 + 1) * 128],
                                             PW[:, ic, :], start=(ic == 0),
                                             stop=False)
                                P_E_INC(t, t.matmul(psA[bi][:, :], ONESR[0:1, 0:128],
                                               BROW[on % 2][0:1, 0:512],
                                               start=False, stop=True), "sp", 1)
                                if ts4 == 3:
                                    browp.rel(on % 2, "sp")
                            blk.tensor(_pm)

                            def _pe(v, tt=tt, on=on, bi=bi, nt=nt):
                                import os as _o
                                p.wait(v, "sp", "ss", "sx", "sg")
                                xa = X[:, tt, on * 512:(on + 1) * 512]
                                if _o.environ.get("PSUMTAP") and tt == 0 and on == 0:
                                    p.inc(v.tensor_copy(xa, psA[bi][:, :]),
                                          "sv", 1)
                                else:
                                    p.inc(v.tensor_tensor(xa, xa,
                                                          psA[bi][:, :],
                                                          OP.add), "sv", 1)
                                bankA.rel(bi, "sv")
                            bV(_pe)
                    oa_guard[0] = ("sp", p.cnt["sp"])

        def emit_fm_proj(wT_scr, col0, bias_row, bcol0, LG, WG, evac_fn):
            wtr = wT_scr.rearrange("(ic p) o -> p ic o", p=128)
            lg_guard = [None]
            for nt in range(NNT):
                def _lg(s, nt=nt):
                    p.wait(s, "sg")
                    if lg_guard[0] is not None:
                        p.wait_val(s, lg_guard[0][0], lg_guard[0][1])
                    else:
                        p.wait(s, "sp")
                    for ic in range(NC):
                        p.dma(s.dma_start(
                            out=LG[:, ic, :],
                            in_=scr_ln[nt * 512:(nt + 1) * 512,
                                       ic * 128:(ic + 1) * 128],
                            transpose=True), "sx")
                blk.sync(_lg)
                for oc in range(NC):
                    def _wg(s, nt=nt, oc=oc):
                        p.wait(s, "sp")
                        p.dma(s.dma_start(
                            out=WG[:, :, 0:128],
                            in_=wtr[:, :, col0 + oc * 128:col0 + (oc + 1) * 128]),
                            "sx")
                    blk.sync(_wg)
                    load_brow(oc % 2, bias_row, bcol0 + oc * 128, 128)
                    bi = oc % 4

                    def _qm(t, nt=nt, oc=oc, bi=bi):
                        p.wait(t, "sx")
                        bankA.acq(bi, t)
                        for ic in range(NC):
                            t.matmul(psA[bi][:, :], WG[:, ic, 0:128],
                                     LG[:, ic, :], start=(ic == 0), stop=False)
                        P_E_INC(t, t.matmul(psA[bi][:, :], BROW[oc % 2][0:1, 0:128],
                                       ONESR[0:1, 0:512], start=False,
                                       stop=True), "sp", 1)
                        browp.rel(oc % 2, "sp")
                    blk.tensor(_qm)
                    evac_fn(nt, oc, bi)
                lg_guard[0] = ("sp", p.cnt["sp"])

        def emit_tm_proj(w_list, LG, WG, evac_fn):
            lg_guard = [None]
            for ntg in range(NNT):
                def _lg(s, ntg=ntg):
                    p.wait(s, "sg")
                    if lg_guard[0] is not None:
                        p.wait_val(s, lg_guard[0][0], lg_guard[0][1])
                    else:
                        p.wait(s, "sp")
                    for ic in range(NC):
                        p.dma(s.dma_start(
                            out=LG[:, ic, :],
                            in_=scr_ln[ntg * 512:(ntg + 1) * 512,
                                       ic * 128:(ic + 1) * 128],
                            transpose=True), "sx")
                blk.sync(_lg)
                for (wtr, col0, ncols, brow_i, bcol0, tag) in w_list:
                    for on in range(ncols // 512):
                        def _wg(s, wtr=wtr, col0=col0, on=on):
                            p.wait(s, "sp")
                            p.dma(s.dma_start(
                                out=WG[:, :, :],
                                in_=wtr[:, :,
                                        col0 + on * 512:col0 + (on + 1) * 512]),
                                "sx")
                        blk.sync(_wg)
                        load_brow(on % 2, brow_i, bcol0 + on * 512, 512)
                        for t4 in range(4):
                            tt = ntg * 4 + t4
                            bi = t4 % 4

                            def _km(t, on=on, t4=t4, bi=bi):
                                p.wait(t, "sx")
                                bankA.acq(bi, t)
                                for ic in range(NC):
                                    t.matmul(psA[bi][:, :],
                                             LG[:, ic, t4 * 128:(t4 + 1) * 128],
                                             WG[:, ic, :], start=(ic == 0),
                                             stop=False)
                                P_E_INC(t, t.matmul(psA[bi][:, :], ONESR[0:1, 0:128],
                                               BROW[on % 2][0:1, 0:512],
                                               start=False, stop=True), "sp", 1)
                                if t4 == 3:
                                    browp.rel(on % 2, "sp")
                            blk.tensor(_km)
                            evac_fn(tag, on, tt, bi)
                lg_guard[0] = ("sp", p.cnt["sp"])

        def emit_kv_einsum(kt, vt, KVUt):
            for fc in range(NC):
                def _kv(t, fc=fc):
                    p.wait(t, "sv", "ss")
                    a0 = fc * 128
                    for tt in range(NTT):
                        mmA = t.matmul(psK[0:64, :], kt[:, tt, a0:a0 + 64],
                                       vt[:, tt, a0:a0 + 64], start=(tt == 0),
                                       stop=(tt == NTT - 1),
                                       tile_position=(0, 0),
                                       skip_group_check=True)
                        mmB = t.matmul(psK[64:128, :],
                                       kt[:, tt, a0 + 64:a0 + 128],
                                       vt[:, tt, a0 + 64:a0 + 128],
                                       start=(tt == 0), stop=(tt == NTT - 1),
                                       tile_position=(0, 64),
                                       skip_group_check=True)
                        if tt == NTT - 1:
                            P_E_INC(t, mmB)
                blk.tensor(_kv)

                def _ke(sc, fc=fc):
                    p.wait(sc, "sp")
                    p.inc(sc.copy(KVUt[:, fc * 64:(fc + 1) * 64],
                                  psK[:, :]), "ss", 1)
                bS(_ke)

        def emit_tm_colsum(src, OUTC):
            for mc in range(NC):
                def _cs(t, mc=mc):
                    p.wait(t, "sv", "ss")
                    for tt in range(NTT):
                        mm = t.matmul(psT[0:128, mc:mc + 1],
                                      src[:, tt, mc * 128:(mc + 1) * 128],
                                      ONESC[:, :], start=(tt == 0),
                                      stop=(tt == NTT - 1))
                        if tt == NTT - 1:
                            P_E_INC(t, mm)
                blk.tensor(_cs)

            def _ce(sc):
                p.wait(sc, "sp")
                p.inc(sc.copy(OUTC[:, 0:8], psT[0:128, 0:8]), "ss", 1)
            bS(_ce)

        def emit_rows_to_tm(arn_rows, dst):
            for tt in range(NTT):
                def _tr(t, tt=tt):
                    p.wait(t, "sv", "ss")
                    P_E_INC(t, t.transpose(psT[0:128, 0:16],
                                      arn_rows[:, tt * 128:(tt + 1) * 128],
                                      IDENT[64:80, 0:16]), "sp", 1)
                blk.tensor(_tr)

                def _te(sc, tt=tt):
                    p.wait(sc, "sp")
                    p.inc(sc.copy(dst[:, tt, :], psT[0:128, 0:16]),
                          "ss", 1)
                bS(_te)

        barrier("vector", "scalar", "tensor", "gpsimd", "sync")
        tap("x_loaded", X[:, :, :])

        emit_ln("plain", "ln1")
        tap("ln1", scr_ln)

        with ExitStack() as ast:
            QF = sbt("QF", [128, NC, TC], BF16, ast)
            KVB = sbt("KVB", [128, 512], BF16, ast)
            inn = ExitStack()
            KT = sbt("KT", [128, NTT, D], BF16, inn)
            VT = sbt("VT", [128, NTT, D], BF16, inn)
            ARN = sbt("ARN", [128, 2048], F32, inn)
            KVU = sbt("KVU", [128, 512], F32, inn)
            SI = ARN[0:16, :]
            SO = ARN[32:48, :]
            EE = ARN[64:80, :]
            TMP2 = ARN[96:112, :]

            barrier("sync", "gpsimd", "tensor", "vector", "scalar")

            with ExitStack() as qst:
                LG = sbt("LG", [128, NC, 512], BF16, qst)
                WG = sbt("WG", [128, NC, 512], BF16, qst)

                def q_evac(nt, oc, bi):
                    def _e(sc, nt=nt, oc=oc, bi=bi):
                        p.wait(sc, "sp")
                        p.inc(sc.activation(QF[:, oc, nt * 512:(nt + 1) * 512],
                                            psA[bi][:, :], AF.Sigmoid),
                              "ss", 1)
                        bankA.rel(bi, "ss")
                    bS(_e)
                emit_fm_proj(scr_qkvT, 0, 0, 0, LG, WG, q_evac)
                barrier("sync", "gpsimd", "tensor", "vector", "scalar")

                kvw = scr_k.rearrange("(t p) f -> p t f", p=128)

                def kv_evac(tag, on, tt, bi):
                    if tag == "k":
                        def _e(sc, on=on, tt=tt, bi=bi):
                            p.wait(sc, "sp")
                            p.inc(sc.activation(
                                KT[:, tt, on * 512:(on + 1) * 512],
                                psA[bi][:, :], AF.Sigmoid), "ss", 1)
                            bankA.rel(bi, "ss")
                        bS(_e)

                        def _w(g, on=on, tt=tt):
                            p.wait(g, "ss")
                            p.dma(g.dma_start(
                                out=kvw[:, tt, on * 512:(on + 1) * 512],
                                in_=KT[:, tt, on * 512:(on + 1) * 512]), "sg")
                        blk.gpsimd(_w)
                    else:
                        def _e(sc, on=on, tt=tt, bi=bi):
                            p.wait(sc, "sp")
                            p.inc(sc.copy(
                                VT[:, tt, on * 512:(on + 1) * 512],
                                psA[bi][:, :]), "ss", 1)
                            bankA.rel(bi, "ss")
                        bS(_e)

                qkvtr = scr_qkvT.rearrange("(ic p) o -> p ic o", p=128)
                emit_tm_proj([(qkvtr, 1024, 1024, 0, 1024, "k"),
                              (qkvtr, 2048, 1024, 0, 2048, "v")], LG, WG,
                             kv_evac)

            tap("q_fm", QF[:, :, :])
            tap("k_tm", KT[:, :, :])
            tap("v_tm", VT[:, :, :])

            mst = ExitStack()
            BSTG = sbt("BSTG", [16, 2048], BF16, mst)
            BSTG2 = sbt("BSTG2", [16, 2048], BF16, mst)

            emit_tm_colsum(KT, KSUM)

            def _qs(v):
                p.wait(v, "ss")
                for fc in range(NC):
                    v.reduce_sum(QSUM[:, fc:fc + 1], QF[:, fc, :], AX)
                p.inc(v.engine_nop(), "sv", 1)
            bV(_qs)

            def _pack1(v):
                p.wait(v, "sv")
                v.tensor_copy(T1f[:, 0:8], KSUM[:, :])
                p.inc(v.tensor_copy(T1f[:, 8:16], QSUM[:, :]), "sv", 1)
            bV(_pack1)
            emit_allreduce([(T1f[:, 0:16], cc_a_i, cc_a_o)])

            def _unpack1(v):
                p.wait(v, "sg")
                v.tensor_copy(KSUM[:, :], T1f[:, 0:8])
                p.inc(v.tensor_copy(QSUM[:, :], T1f[:, 8:16]), "sv", 1)
            bV(_unpack1)
            tap("ksum", KSUM[:, :])
            tap("qsum", QSUM[:, :])

            emit_masked_stat(lambda fc: KSUM[:, fc:fc + 1], EPS,
                             lambda fc, nt: QF[:, fc, nt * 512:(nt + 1) * 512],
                             0)

            def _si(sc):
                p.wait(sc, "sp")
                sc.activation(CB[0:16, 0:1], psT[0:16, 0:1], AF.Identity, scale=EPS)
                for nt in range(NNT):
                    sc.activation(EE[:, nt * 512:(nt + 1) * 512],
                                  psA[nt][0:16, :], AF.Identity,
                                  bias=CB[0:16, 0:1])
                p.inc(snop(sc), "ss", 1)
            bS(_si)
            for i in range(4):
                bankA.rel(i, "ss")

            def _sir(sc):
                p.wait(sc, "ss")
                sc.activation(SO[:, :], EE[:, :], AF.Ln)
                p.inc(sc.activation(SI[:, :], SO[:, :], AF.Exp, scale=-1.0),
                      "ss", 1)
            bS(_sir)
            tap("si", SI)

            with ExitStack() as kst:
                KFG = sbt("KFG", [128, NC, 512], BF16, kst)

                def kfg_load(nt):
                    def _k(s, nt=nt):
                        p.wait(s, "sp", "sg")
                        for ic in range(NC):
                            p.dma(s.dma_start(
                                out=KFG[:, ic, :],
                                in_=scr_k[nt * 512:(nt + 1) * 512,
                                          ic * 128:(ic + 1) * 128],
                                transpose=True), "sx")
                    blk.sync(_k)

                for nt in range(NNT):
                    kfg_load(nt)
                    for fc in range(NC):
                        build_mask(fc % 2, fc, QSUM[:, fc:fc + 1], EPS)

                        def _mm(t, fc=fc, nt=nt):
                            p.wait(t, "sv", "sx")
                            if fc == 0:
                                bankA.acq(nt, t)
                            if nt == 0:
                                t.matmul(psT[0:16, 1:2], MSK[:, fc % 2, :],
                                         ONESC[:, :], start=(fc == 0),
                                         stop=(fc == NC - 1))
                            P_E_INC(t, t.matmul(psA[nt][0:16, :], MSK[:, fc % 2, :],
                                           KFG[:, fc, :], start=(fc == 0),
                                           stop=(fc == NC - 1)), "sp", 1)
                            mskp.rel(fc % 2, "sp")
                        blk.tensor(_mm)

                def _so(sc):
                    p.wait(sc, "sp")
                    sc.activation(CB[0:16, 1:2], psT[0:16, 1:2], AF.Identity,
                                  scale=EPS)
                    for nt in range(NNT):
                        sc.activation(EE[:, nt * 512:(nt + 1) * 512],
                                      psA[nt][0:16, :], AF.Identity,
                                      bias=CB[0:16, 1:2])
                    p.inc(snop(sc), "ss", 1)
                bS(_so)
                for i in range(4):
                    bankA.rel(i, "ss")

                def _sor(sc):
                    p.wait(sc, "ss")
                    sc.activation(TMP2[:, :], EE[:, :], AF.Ln)
                    p.inc(sc.activation(SO[:, :], TMP2[:, :], AF.Exp,
                                        scale=-1.0), "ss", 1)
                bS(_sor)
                tap("so", SO)

                def _sib(v):
                    p.wait(v, "sv")
                    p.inc(v.tensor_copy(BSTG[0:16, :], SI[:, :]), "sv", 1)
                bV(_sib)

                def emit_col_wsum(src3d, OUTC):
                    for fc in range(NC):
                        for nt in range(NNT):
                            bi = nt % 2

                            def _bm(t, fc=fc, nt=nt, bi=bi):
                                p.wait(t, "sv")
                                bankB.acq(bi, t)
                                P_E_INC(t, t.matmul(
                                    psB[bi][:, :],
                                    M0[0:16, fc * 128:(fc + 1) * 128],
                                    BSTG[0:16, nt * 512:(nt + 1) * 512],
                                    start=True, stop=True), "sp", 1)
                            blk.tensor(_bm)

                            def _trs(sc, bi=bi):
                                p.wait(sc, "sp", "sv")
                                p.inc(sc.copy(T2f[:, :], psB[bi][:, :]),
                                      "ss", 1)
                                bankB.rel(bi, "ss")
                            bS(_trs)

                            def _tr(v, fc=fc, nt=nt):
                                p.wait(v, "ss")
                                v.tensor_tensor(QS1[:, :], src3d(fc, nt),
                                                T2f[:, :], OP.mult)
                                p.inc(v.reduce_sum(STAT[:, nt:nt + 1],
                                                   QS1[:, :], AX), "sv", 1)
                            bV(_tr)

                        def _fin(v, fc=fc):
                            p.inc(v.reduce_sum(OUTC[:, fc:fc + 1],
                                               STAT[:, 0:4], AX), "sv", 1)
                        bV(_fin)

                emit_col_wsum(
                    lambda fc, nt: QF[:, fc, nt * 512:(nt + 1) * 512], QSI)
                tap("qsi", QSI[:, :])

                def _sob(v):
                    p.wait(v, "sv")
                    p.inc(v.tensor_copy(BSTG[0:16, :], SO[:, :]), "sv", 1)
                bV(_sob)
                for nt in range(NNT):
                    kfg_load(nt)
                    for fc in range(NC):
                        bi = fc % 2

                        def _bm2(t, fc=fc, nt=nt, bi=bi):
                            p.wait(t, "sv", "sx")
                            bankB.acq(bi, t)
                            P_E_INC(t, t.matmul(
                                psB[bi][:, :],
                                M0[0:16, fc * 128:(fc + 1) * 128],
                                BSTG[0:16, nt * 512:(nt + 1) * 512],
                                start=True, stop=True), "sp", 1)
                        blk.tensor(_bm2)

                        def _tr2s(sc, bi=bi):
                            p.wait(sc, "sp", "sv")
                            p.inc(sc.copy(T2f[:, :], psB[bi][:, :]), "ss", 1)
                            bankB.rel(bi, "ss")
                        bS(_tr2s)

                        def _tr2(v, fc=fc, nt=nt):
                            p.wait(v, "ss")
                            v.tensor_tensor(QS1[:, :], KFG[:, fc, :],
                                            T2f[:, :], OP.mult)
                            p.inc(v.reduce_sum(STAT[:, 4 + fc:5 + fc],
                                               QS1[:, :], AX), "sv", 1)
                        bV(_tr2)

                        def _tr2f(v, fc=fc, nt=nt):
                            if nt == 0:
                                p.inc(v.tensor_copy(KSOC[:, fc:fc + 1],
                                                    STAT[:, 4 + fc:5 + fc]),
                                      "sv", 1)
                            else:
                                p.inc(v.tensor_tensor(KSOC[:, fc:fc + 1],
                                                      KSOC[:, fc:fc + 1],
                                                      STAT[:, 4 + fc:5 + fc],
                                                      OP.add), "sv", 1)
                        bV(_tr2f)
                tap("kso", KSOC[:, :])

                def _pack2(v):
                    p.wait(v, "sv")
                    v.tensor_copy(T1f[:, 0:8], KSOC[:, :])
                    p.inc(v.tensor_copy(T1f[:, 8:16], QSI[:, :]), "sv", 1)
                bV(_pack2)
                emit_allreduce([(T1f[:, 0:16], cc_a_i, cc_a_o)])

                def _unpack2(v):
                    p.wait(v, "sg")
                    v.tensor_copy(KSOC[:, :], T1f[:, 0:8])
                    p.inc(v.tensor_copy(QSI[:, :], T1f[:, 8:16]), "sv", 1)
                bV(_unpack2)

                emit_masked_stat(lambda fc: KSOC[:, fc:fc + 1], EPS,
                                 lambda fc, nt: QF[:, fc,
                                                   nt * 512:(nt + 1) * 512], 2)

                def _sa(sc):
                    p.wait(sc, "sp")
                    sc.activation(CB[0:16, 2:3], psT[0:16, 2:3], AF.Identity,
                                  scale=EPS, bias=EPS)
                    for nt in range(NNT):
                        sc.activation(BSTG2[0:16, nt * 512:(nt + 1) * 512],
                                      psA[nt][0:16, :], AF.Sigmoid,
                                      bias=CB[0:16, 2:3])
                    p.inc(snop(sc), "ss", 1)
                bS(_sa)
                for i in range(4):
                    bankA.rel(i, "ss")
                tap("sa", BSTG2[0:16, :])

                for nt in range(NNT):
                    kfg_load(nt)
                    for fc in range(NC):
                        build_mask(fc % 2, fc, QSI[:, fc:fc + 1], EPS)

                        def _mm2(t, fc=fc, nt=nt):
                            p.wait(t, "sv", "sx")
                            if fc == 0:
                                bankA.acq(nt, t)
                            if nt == 0:
                                t.matmul(psT[0:16, 3:4], MSK[:, fc % 2, :],
                                         ONESC[:, :], start=(fc == 0),
                                         stop=(fc == NC - 1))
                            P_E_INC(t, t.matmul(psA[nt][0:16, :], MSK[:, fc % 2, :],
                                           KFG[:, fc, :], start=(fc == 0),
                                           stop=(fc == NC - 1)), "sp", 1)
                            mskp.rel(fc % 2, "sp")
                        blk.tensor(_mm2)

                def _cs2(sc):
                    p.wait(sc, "sp")
                    sc.activation(CB[0:16, 3:4], psT[0:16, 3:4], AF.Identity,
                                  scale=EPS, bias=EPS)
                    for nt in range(NNT):
                        sc.activation(SO[:, nt * 512:(nt + 1) * 512],
                                      psA[nt][0:16, :], AF.Identity,
                                      bias=CB[0:16, 3:4])
                    p.inc(snop(sc), "ss", 1)
                bS(_cs2)
                for i in range(4):
                    bankA.rel(i, "ss")

                def _clip(v):
                    p.wait(v, "ss")
                    p.inc(v.tensor_scalar(SO[:, :], SO[:, :], -1.0, 1.0,
                                          OP.max, OP.min), "sv", 1)
                bV(_clip)

                def _exp(sc):
                    p.wait(sc, "sv")
                    p.inc(sc.activation(EE[:, :], SO[:, :], AF.Exp), "ss", 1)
                bS(_exp)

                def _es(v):
                    p.wait(v, "ss")
                    p.inc(v.reduce_sum(ESCOL[0:16, 0:1], EE[:, :], AX),
                          "sv", 1)
                bV(_es)
                tap("e", EE)

            est = ExitStack()
            ETM = sbt("ETM", [128, NTT, 16], F32, est)
            emit_rows_to_tm(EE, ETM)
            for tt in range(NTT):
                def _ve(v, tt=tt):
                    p.wait(v, "sv")
                    va = VT[:, tt, :].rearrange("p (s d) -> p s d", d=64)
                    ea = ETM[:, tt, :].rearrange("p (s o) -> p s o", o=1)
                    p.inc(v.tensor_tensor(va, va,
                                          ea.broadcast_to([128, 16, 64]),
                                          OP.mult), "sv", 1)
                bV(_ve)
            barrier("vector")
            est.close()

            emit_kv_einsum(KT, VT, KVU)
            tap("kvu_pre", KVU[:, :])

            emit_allreduce([(KVU[:, :], cc_d_i, cc_d_o),
                            (ESCOL[0:16, 0:4], cc_e_i, cc_e_o)])

            def _norm(v):
                p.wait(v, "sg")
                p.inc(v.tensor_copy(KVB[:, :], KVU[:, :]), "sv", 1)
            bV(_norm)

            def _norm2(sc):
                p.wait(sc, "sg")
                sc.activation(ESCOL[0:16, 2:3], ESCOL[0:16, 0:1], AF.Ln)
                p.inc(sc.activation(ESCOL[0:16, 1:2], ESCOL[0:16, 2:3],
                                    AF.Exp, scale=-1.0, bias=8.317766166719343), "ss", 1)
            bS(_norm2)
            tap("kvu", KVU[:, :])

            def _scq(v):
                p.wait(v, "sv", "ss")
                v.tensor_scalar(SI[:, :], SI[:, :], ESCOL[0:16, 1:2], None,
                                OP.mult)
                p.inc(v.tensor_tensor(BSTG[0:16, :], SI[:, :], BSTG2[0:16, :],
                                      OP.mult), "sv", 1)
            bV(_scq)
            tap("norm", ESCOL[0:16, 0:4])
            tap("scq", BSTG[0:16, :])
            emit_bcast_scale(lambda fc, nt: QF[:, fc, nt * 512:(nt + 1) * 512], BSTG)
            tap("qfin", QF[:, :, :])
            mst.close()

            tap("x0b", X[:, :, :])
            barrier("sync", "gpsimd", "tensor", "vector", "scalar")
            inn.close()
            emit_out_proj(KVB, QF, scr_projT, 1, stage_oa="outa")

        tap("x1", X[:, :, :])

        def emit_ffn(tag):
            emit_ln("quant")
            if tag == 1:
                tap("xq", scr_ln)
            with ExitStack() as fst:
                W2R = sbt("W2R%d" % tag, [128, NHC, 1024], BF16, fst)
                HQ = sbt("HQ%d" % tag, [128, NHC, 512], BF16, fst)
                XQG = sbt("XQG%d" % tag, [128, NC, 512], BF16, fst)
                W1G = [sbt("W1Ga%d" % tag, [128, NC, 512], BF16, fst),
                       sbt("W1Gb%d" % tag, [128, NC, 512], BF16, fst)]
                w1gp = Pool(p, 2)

                barrier("sync", "gpsimd", "tensor", "vector", "scalar")

                def _w2(s):
                    for hc in range(NHC):
                        p.dma(s.dma_start(
                            out=W2R[:, hc, :],
                            in_=scr_w2q[:, hc * 128:(hc + 1) * 128],
                            transpose=True), "sx")
                blk.sync(_w2)
                w1tr = scr_w1T.rearrange("(ic p) h -> p ic h", p=128)

                xq_guard = [None]
                for tau in range(NNT):
                    def _xq(s, tau=tau):
                        p.wait(s, "sg")
                        if xq_guard[0] is not None:
                            p.wait_val(s, xq_guard[0][0], xq_guard[0][1])
                        for ic in range(NC):
                            p.dma(s.dma_start(
                                out=XQG[:, ic, :],
                                in_=scr_ln[tau * 512:(tau + 1) * 512,
                                           ic * 128:(ic + 1) * 128],
                                transpose=True), "sx")
                    blk.sync(_xq)
                    if tag == 1 and tau == 0:
                        tap("b1s", B1S[:, :])
                        tap("xqg", XQG[:, :, :])

                    for hg in range(8):
                        def _w1(s, tau=tau, hg=hg):
                            w1gp.acq(hg % 2, s)
                            p.dma(s.dma_start(
                                out=W1G[hg % 2][:, :, :],
                                in_=w1tr[:, :, hg * 512:(hg + 1) * 512]), "sx")
                        blk.sync(_w1)
                        for h4 in range(4):
                            hc = hg * 4 + h4
                            bi = hc % 4

                            def _m1(t, hg=hg, h4=h4, bi=bi):
                                p.wait(t, "sx")
                                bankA.acq(bi, t)
                                for ic in range(NC):
                                    mm = t.matmul(
                                        psA[bi][:, :],
                                        W1G[hg % 2][:, ic,
                                                    h4 * 128:(h4 + 1) * 128],
                                        XQG[:, ic, :], start=(ic == 0),
                                        stop=(ic == NC - 1))
                                P_E_INC(t, mm)
                                if h4 == 3:
                                    w1gp.rel(hg % 2, "sp")
                            blk.tensor(_m1)

                            def _e1(sc, hc=hc, bi=bi, h4=h4):
                                p.wait(sc, "sp", "sv")
                                sl = slice((h4 % 2) * 512,
                                           (h4 % 2) * 512 + 512)
                                p.inc(sc.activation(T1f[:, sl], psA[bi][:, :],
                                                    AF.Identity, scale=1024.0,
                                                    bias=B1S[:, hc:hc + 1]),
                                      "ss", 1)
                                bankA.rel(bi, "ss")
                            bS(_e1)
                            if tag == 1 and tau == 0 and hc == 0:
                                tap("h0", T1f[:, 0:512])
                            if h4 % 2 == 0:
                                continue
                            hc0 = hc - 1

                            def _e2(v, hc0=hc0):
                                p.wait(v, "ss")
                                AI2 = T1f.ap().bitcast(I32)[:, 0:1024]
                                v.tensor_copy(NIi[:, 0:1024], T1f[:, :])
                                hqv = HQ[:, hc0:hc0 + 2, :].rearrange(
                                    "p a b -> p (a b)")
                                v.tensor_scalar(AI2, NIi[:, 0:1024], -128, 6,
                                                OP.bitwise_and,
                                                OP.arith_shift_right)
                                v.tensor_scalar(NIi[:, 0:1024], NIi[:, 0:1024],
                                                127, None, OP.bitwise_and)
                                v.scalar_tensor_tensor(hqv, NIi[:, 0:1024],
                                                       65.0, AI2,
                                                       OP.is_ge, OP.add)
                                p.inc(v.tensor_scalar(hqv, hqv,
                                                      0.0, 127.0, OP.max,
                                                      OP.min), "sv", 1)
                            bV(_e2)
                    if tag == 1:
                        if tau == 0:
                            tap("hq", HQ[:, :, :])

                    for ts4 in range(4):
                        tt = tau * 4 + ts4
                        for on in range(2):
                            load_brow(on % 2, 6, on * 512, 512)
                            bi = (ts4 * 2 + on) % 4

                            def _m2(t, ts4=ts4, on=on, bi=bi):
                                p.wait(t, "sv", "sx")
                                bankA.acq(bi, t)
                                for hc in range(NHC):
                                    t.matmul(
                                        psA[bi][:, :],
                                        HQ[:, hc, ts4 * 128:(ts4 + 1) * 128],
                                        W2R[:, hc, on * 512:(on + 1) * 512],
                                        start=(hc == 0), stop=False)
                                P_E_INC(t, t.matmul(psA[bi][:, :], ONESR[0:1, 0:128],
                                               BROW[on % 2][0:1, 0:512],
                                               start=False, stop=True),
                                      "sp", 1)
                                browp.rel(on % 2, "sp")
                            blk.tensor(_m2)

                            def _r2(v, tt=tt, on=on, bi=bi):
                                p.wait(v, "sp", "ss", "sx", "sg")
                                xa = X[:, tt, on * 512:(on + 1) * 512]
                                p.inc(v.tensor_tensor(xa, xa, psA[bi][:, :],
                                                      OP.add), "sv", 1)
                                bankA.rel(bi, "sv")
                            bV(_r2)
                    xq_guard[0] = ("sp", p.cnt["sp"])

        tap("w1t", scr_w1T)
        tap("w2q", scr_w2q)
        emit_ffn(1)
        tap("x2", X[:, :, :])

        emit_ln("plain")
        with ExitStack() as bst:
            QF = sbt("QFb", [128, NC, TC], BF16, bst)
            KVB = sbt("KVBb", [128, 512], BF16, bst)
            innb = ExitStack()
            KT = sbt("KTb", [128, NTT, D], BF16, innb)
            VT = sbt("VTb", [128, NTT, D], BF16, innb)
            ARN = sbt("ARNb", [128, 2048], F32, innb)
            KVU = sbt("KVUb", [128, 512], F32, innb)
            ZZ = ARN[0:16, :]

            barrier("sync", "gpsimd", "tensor", "vector", "scalar")

            with ExitStack() as qst:
                LG = sbt("LGb", [128, NC, 512], BF16, qst)
                WG = sbt("WGb", [128, NC, 512], BF16, qst)

                def fm_evac(dst_ap, bi):
                    def _e1(sc, bi=bi):
                        p.wait(sc, "sp", "sv")
                        sc.activation(T2f[:, :], psA[bi][:, :], AF.Relu,
                                      scale=-1.0)
                        sc.activation(QS2[:, :], psA[bi][:, :], AF.Relu)
                        p.inc(sc.activation(QS1[:, :], T2f[:, :], AF.Exp,
                                            scale=-1.0), "ss", 1)
                        bankA.rel(bi, "ss")
                    bS(_e1)

                    def _e3(v, dst_ap=dst_ap):
                        p.wait(v, "ss")
                        p.inc(v.tensor_tensor(dst_ap, QS2[:, :], QS1[:, :],
                                              OP.add), "sv", 1)
                    bV(_e3)

                def q_evac(nt, oc, bi):
                    fm_evac(QF[:, oc, nt * 512:(nt + 1) * 512], bi)
                emit_fm_proj(scr_wq, 0, 2, 0, LG, WG, q_evac)

                def kv_evac(tag, on, tt, bi):
                    if tag == "k":
                        fm_evac(KT[:, tt, on * 512:(on + 1) * 512], bi)
                    else:
                        def _e(sc, on=on, tt=tt, bi=bi):
                            p.wait(sc, "sp")
                            p.inc(sc.copy(
                                VT[:, tt, on * 512:(on + 1) * 512],
                                psA[bi][:, :]), "ss", 1)
                            bankA.rel(bi, "ss")
                        bS(_e)

                wkr = scr_wk.rearrange("(ic p) o -> p ic o", p=128)
                wvr = scr_wv.rearrange("(ic p) o -> p ic o", p=128)
                emit_tm_proj([(wkr, 0, 1024, 3, 0, "k"),
                              (wvr, 0, 1024, 4, 0, "v")], LG, WG, kv_evac)

            tap("bqm", QF[:, :, :])
            tap("bkm", KT[:, :, :])

            mstb = ExitStack()
            BSTG = sbt("BSTGb", [16, 2048], BF16, mstb)

            emit_tm_colsum(KT, KSUM)
            emit_kv_einsum(KT, VT, KVU)

            def _pk(v):
                p.wait(v, "sv", "ss")
                p.inc(v.engine_nop(), "sv", 1)
            bV(_pk)
            emit_allreduce([(KSUM[:, :], cc_c_i, cc_c_o),
                            (KVU[:, :], cc_d_i, cc_d_o)])
            tap("bkvu", KVU[:, :])

            def _kvb(v):
                p.wait(v, "sg")
                p.inc(v.tensor_copy(KVB[:, :], KVU[:, :]), "sv", 1)
            bV(_kvb)

            emit_masked_stat(lambda fc: KSUM[:, fc:fc + 1], 0.0,
                             lambda fc, nt: QF[:, fc, nt * 512:(nt + 1) * 512],
                             4)

            ZT = ARN[32:48, :]

            def _z(sc):
                p.wait(sc, "sp")
                for nt in range(NNT):
                    sc.activation(ZT[:, nt * 512:(nt + 1) * 512],
                                  psA[nt][0:16, :], AF.Identity, bias=EPS)
                p.inc(snop(sc), "ss", 1)
            bS(_z)
            for i in range(4):
                bankA.rel(i, "ss")

            def _zr(sc):
                p.wait(sc, "ss")
                sc.activation(ARN[64:80, :], ZT[:, :], AF.Ln)
                p.inc(sc.activation(ZZ[:, :], ARN[64:80, :], AF.Exp,
                                    scale=-1.0), "ss", 1)
            bS(_zr)

            def _zrc(v):
                p.wait(v, "ss")
                p.inc(v.tensor_copy(BSTG[0:16, :], ZZ[:, :]), "sv", 1)
            bV(_zrc)
            tap("bz", ZZ)

            emit_bcast_scale(lambda fc, nt: QF[:, fc, nt * 512:(nt + 1) * 512], BSTG)
            mstb.close()
            barrier("sync", "gpsimd", "tensor", "vector", "scalar")
            innb.close()
            emit_out_proj(KVB, QF, scr_wo, 5)

        tap("x3", X[:, :, :])

        emit_ffn(2)
        tap("x4", X[:, :, :])

        def _stv(g):
            p.wait(g, "sv", "ss", "sp")
            yw = y_d.rearrange("(t p) f -> p t f", p=128)
            for g4 in range(4):
                p.dma(g.dma_start(out=yw[:, 4 * g4:4 * g4 + 4, :],
                                  in_=X[:, 4 * g4:4 * g4 + 4, :]), "sg")
            g.wait_ge(p.h["sg"], p.cnt["sg"])
        blk.gpsimd(_stv)

    return nc


def kernel(**inputs):
    global _DBG_RESULT
    x = np.ascontiguousarray(np.asarray(inputs["L0_attn_a_x"], dtype=np.float32))
    common = {
        "aqkv_w": np.ascontiguousarray(inputs["L0_attn_a_qkv_w"], dtype=np.float32),
        "aqkv_b": np.ascontiguousarray(inputs["L0_attn_a_qkv_b"], dtype=np.float32).reshape(1, -1),
        "aproj_w": np.ascontiguousarray(inputs["L0_attn_a_proj_w"], dtype=np.float32),
        "aproj_b": np.ascontiguousarray(inputs["L0_attn_a_proj_b"], dtype=np.float32).reshape(1, -1),
        "b_wq": np.ascontiguousarray(np.asarray(inputs["L0_attn_b_wq"], dtype=np.float32).reshape(D, D)),
        "b_wk": np.ascontiguousarray(np.asarray(inputs["L0_attn_b_wk"], dtype=np.float32).reshape(D, D)),
        "b_wv": np.ascontiguousarray(np.asarray(inputs["L0_attn_b_wv"], dtype=np.float32).reshape(D, D)),
        "b_wo": np.ascontiguousarray(np.asarray(inputs["L0_attn_b_wo"], dtype=np.float32).reshape(D, D)),
        "b_bq": np.ascontiguousarray(np.asarray(inputs["L0_attn_b_bq"], dtype=np.float32).reshape(1, -1)),
        "b_bk": np.ascontiguousarray(np.asarray(inputs["L0_attn_b_bk"], dtype=np.float32).reshape(1, -1)),
        "b_bv": np.ascontiguousarray(np.asarray(inputs["L0_attn_b_bv"], dtype=np.float32).reshape(1, -1)),
        "b_bo": np.ascontiguousarray(np.asarray(inputs["L0_attn_b_bo"], dtype=np.float32).reshape(1, -1)),
        "f_w1": np.ascontiguousarray(inputs["L0_ffn_w1"], dtype=np.float32),
        "f_b1": np.ascontiguousarray(
            np.asarray(inputs["L0_ffn_b1"], dtype=np.float32).reshape(NHC, 128).T),
        "f_w2": np.ascontiguousarray(inputs["L0_ffn_w2"], dtype=np.float32),
        "f_b2": np.ascontiguousarray(np.asarray(inputs["L0_ffn_b2"], dtype=np.float32).reshape(1, -1)),
    }
    in_maps = []
    for c in range(8):
        bb, half = c // 2, c % 2
        m = dict(common)
        m["x_in"] = np.ascontiguousarray(x[bb, half * TC:(half + 1) * TC, :])
        in_maps.append(m)
    nc = build_core_program(dbg_stage=_DBG_STAGE)
    res = run_bass_kernel_spmd(nc, in_maps, core_ids=list(range(8)))
    out = np.empty((B, S, D), dtype=np.float32)
    for c in range(8):
        bb, half = c // 2, c % 2
        out[bb, half * TC:(half + 1) * TC, :] = res.results[c]["y_out"]
    if _DBG_STAGE is not None:
        _DBG_RESULT = [res.results[c].get("dbg") for c in range(8)]
    return out


# version 05 (final)
# speedup vs baseline: 1.0611x; 1.0038x over previous
import numpy as np
from contextlib import ExitStack

import concourse.bass as bass
import concourse.mybir as mybir
from concourse.bass_utils import run_bass_kernel_spmd

F32 = mybir.dt.float32
BF16 = mybir.dt.bfloat16
I32 = mybir.dt.int32
OP = mybir.AluOpType
AF = mybir.ActivationFunctionType
AX = mybir.AxisListType.X

B, S, D, H, HD = 4, 4096, 1024, 16, 64
TC = 2048
NTT = 16
NNT = 4
NC = 8
HID = 4096
NHC = 32
EPS = 1e-6
LNEPS = 1e-5
RG = [[0, 1], [2, 3], [4, 5], [6, 7]]

_DBG_STAGE = None
_DBG_RESULT = None
_TAP_SHAPES = {
    "x_loaded": [128, NTT, D],
    "ln1": [TC, D],
    "ln1stat": [128, 64],
    "q_fm": [128, NC, TC],
    "k_tm": [128, NTT, D],
    "v_tm": [128, NTT, D],
    "ksum": [128, 8],
    "qsum": [128, 8],
    "si": [16, TC],
    "so": [16, TC],
    "kso": [128, 8],
    "qsi": [128, 8],
    "sa": [16, TC],
    "e": [16, TC],
    "kvu_pre": [128, 512],
    "kvu": [128, 512],
    "norm": [16, 4],
    "scq": [16, TC],
    "qfin": [128, NC, TC],
    "outa": [128, NC, 512],
    "projw": [D, D],
    "w1t": [D, HID],
    "w2q": [D, HID],
    "x0b": [128, NTT, D],
    "x1": [128, NTT, D],
    "xq": [TC, D],
    "b1s": [128, NHC],
    "xqg": [128, NC, 512],
    "h0": [128, 512],
    "hq": [128, NHC, 512],
    "x2": [128, NTT, D],
    "bqm": [128, NC, TC],
    "bkm": [128, NTT, D],
    "bz": [16, TC],
    "bkvu": [128, 512],
    "x3": [128, NTT, D],
    "x4": [128, NTT, D],
}


class Prog:
    def __init__(self, nc, stack, sem_names):
        self.nc = nc
        self.h = {n: stack.enter_context(nc.semaphore(n)) for n in sem_names}
        self.cnt = {n: 0 for n in sem_names}
        self._auto = set()

    def inc(self, instr, name, k=1):
        key = (id(instr.ins), name)
        if key in self._auto:
            return instr
        instr.then_inc(self.h[name], k)
        self.cnt[name] += k
        return instr

    def dma(self, instr, name):
        return self.inc(instr, name, 16)

    def wait(self, eng, *names):
        for n in names:
            if self.cnt[n] > 0:
                eng.wait_ge(self.h[n], self.cnt[n])

    def wait_val(self, eng, name, val):
        if val > 0:
            eng.wait_ge(self.h[name], val)


class Pool:

    def __init__(self, p, n):
        self.p = p
        self.st = [None] * n

    def acq(self, i, eng):
        s = self.st[i % len(self.st)]
        if s is not None:
            self.p.wait_val(eng, s[0], s[1])

    def rel(self, i, sem):
        self.st[i % len(self.st)] = (sem, self.p.cnt[sem])


def build_core_program(dbg_stage=None):
    nc = bass.Bass("TRN2", target_bir_lowering=False, debug=False)

    x_d = nc.dram_tensor("x_in", [TC, D], F32, kind="ExternalInput").ap()
    qkvw_d = nc.dram_tensor("aqkv_w", [3 * D, D], F32, kind="ExternalInput").ap()
    qkvb_d = nc.dram_tensor("aqkv_b", [1, 3 * D], F32, kind="ExternalInput").ap()
    projw_d = nc.dram_tensor("aproj_w", [D, D], F32, kind="ExternalInput").ap()
    projb_d = nc.dram_tensor("aproj_b", [1, D], F32, kind="ExternalInput").ap()
    bwq_d = nc.dram_tensor("b_wq", [D, D], F32, kind="ExternalInput").ap()
    bwk_d = nc.dram_tensor("b_wk", [D, D], F32, kind="ExternalInput").ap()
    bwv_d = nc.dram_tensor("b_wv", [D, D], F32, kind="ExternalInput").ap()
    bwo_d = nc.dram_tensor("b_wo", [D, D], F32, kind="ExternalInput").ap()
    bbq_d = nc.dram_tensor("b_bq", [1, D], F32, kind="ExternalInput").ap()
    bbk_d = nc.dram_tensor("b_bk", [1, D], F32, kind="ExternalInput").ap()
    bbv_d = nc.dram_tensor("b_bv", [1, D], F32, kind="ExternalInput").ap()
    bbo_d = nc.dram_tensor("b_bo", [1, D], F32, kind="ExternalInput").ap()
    w1_d = nc.dram_tensor("f_w1", [HID, D], F32, kind="ExternalInput").ap()
    b1_d = nc.dram_tensor("f_b1", [128, NHC], F32, kind="ExternalInput").ap()
    w2_d = nc.dram_tensor("f_w2", [D, HID], F32, kind="ExternalInput").ap()
    b2_d = nc.dram_tensor("f_b2", [1, D], F32, kind="ExternalInput").ap()
    y_d = nc.dram_tensor("y_out", [TC, D], F32, kind="ExternalOutput").ap()

    scr_ln = nc.dram_tensor("scr_ln", [TC, D], BF16, kind="Internal").ap()
    scr_k = nc.dram_tensor("scr_k", [TC, D], BF16, kind="Internal").ap()
    scr_tmp = nc.dram_tensor("scr_tmp", [HID, D], BF16, kind="Internal").ap()
    scr_qkvT = nc.dram_tensor("scr_qkvT", [D, 3 * D], BF16, kind="Internal").ap()
    scr_projT = nc.dram_tensor("scr_projT", [D, D], BF16, kind="Internal").ap()
    scr_w1T = nc.dram_tensor("scr_w1T", [D, HID], BF16, kind="Internal").ap()
    scr_w2q = nc.dram_tensor("scr_w2q", [D, HID], BF16, kind="Internal").ap()
    scr_wq = nc.dram_tensor("scr_wq", [D, D], BF16, kind="Internal").ap()
    scr_wk = nc.dram_tensor("scr_wk", [D, D], BF16, kind="Internal").ap()
    scr_wv = nc.dram_tensor("scr_wv", [D, D], BF16, kind="Internal").ap()
    scr_wo = nc.dram_tensor("scr_wo", [D, D], BF16, kind="Internal").ap()
    scr_bias = nc.dram_tensor("scr_bias", [8, 3 * D], BF16, kind="Internal").ap()
    cc_a_i = nc.dram_tensor("cc_a_i", [128, 16], F32, kind="Internal").ap()
    cc_a_o = nc.dram_tensor("cc_a_o", [128, 16], F32, kind="Internal").ap()
    cc_b_i = nc.dram_tensor("cc_b_i", [16, 1024], F32, kind="Internal").ap()
    cc_b_o = nc.dram_tensor("cc_b_o", [16, 1024], F32, kind="Internal").ap()
    cc_c_i = nc.dram_tensor("cc_c_i", [128, 8], F32, kind="Internal").ap()
    cc_c_o = nc.dram_tensor("cc_c_o", [128, 8], F32, kind="Internal").ap()
    cc_d_i = nc.dram_tensor("cc_d_i", [128, 512], F32, kind="Internal").ap()
    cc_d_o = nc.dram_tensor("cc_d_o", [128, 512], F32, kind="Internal").ap()
    cc_e_i = nc.dram_tensor("cc_e_i", [16, 4], F32, kind="Internal").ap()
    cc_e_o = nc.dram_tensor("cc_e_o", [16, 4], F32, kind="Internal").ap()
    dbg_d = None
    if dbg_stage is not None:
        dbg_d = nc.dram_tensor("dbg", _TAP_SHAPES[dbg_stage], F32,
                               kind="ExternalOutput").ap()

    with ExitStack() as top:
        def sbt(name, shape, dt, stk):
            return stk.enter_context(nc.sbuf_tensor(name, shape, dt))

        sb = lambda name, shape, dt: sbt(name, shape, dt, top)

        X = sb("X", [128, NTT, D], F32)
        T1f = sb("T1f", [128, 1024], F32)
        T2f = sb("T2f", [128, 512], F32)
        QS1 = sb("QS1", [128, 512], F32)
        QS2 = sb("QS2", [128, 512], F32)
        NIi = sb("NIi", [128, 1024], I32)
        STAT = sb("STAT", [128, 64], F32)
        TA = sb("TA", [128, 16], F32)
        TB = sb("TB", [128, 16], F32)
        TCH = sb("TCH", [128, 16], F32)
        IDENT = sb("IDENT", [128, 16], F32)
        IOTC = sb("IOTC", [128, 1], F32)
        M0 = sb("M0", [16, 1024], BF16)
        ONESC = sb("ONESC", [128, 1], BF16)
        ONESR = sb("ONESR", [1, 512], BF16)
        BROW = [sb("BROW0", [1, 512], BF16), sb("BROW1", [1, 512], BF16)]
        B1S = sb("B1S", [128, NHC], F32)
        KSUM = sb("KSUM", [128, 8], F32)
        QSUM = sb("QSUM", [128, 8], F32)
        QSI = sb("QSI", [128, 8], F32)
        KSOC = sb("KSOC", [128, 8], F32)
        CB = sb("CB", [16, 4], F32)
        ESCOL = sb("ESCOL", [16, 4], F32)
        MSK = sb("MSK", [128, 2, 16], BF16)
        NOPC = sb("NOPC", [1, 4], F32)
        CONSTS = sb("CONSTS", [128, 4], F32)
        CONSTS2 = sb("CONSTS2", [128, 2], F32)
        nc.const_aps.register(F32, EPS, CONSTS[:, 0:1])
        nc.const_aps.register(F32, 1024.0, CONSTS[:, 1:2])
        nc.const_aps.register(F32, -0.5, CONSTS[:, 2:3])
        nc.const_aps.register(F32, -1.0, CONSTS[:, 3:4])
        LOG4096 = 8.317766166719343
        nc.const_aps.register(F32, LOG4096, CONSTS2[:, 0:1])

        psA = [top.enter_context(nc.psum_tensor("psA%d" % i, [128, 512], F32))
               for i in range(4)]
        psB = [top.enter_context(nc.psum_tensor("psB%d" % i, [128, 512], F32))
               for i in range(2)]
        psT = top.enter_context(nc.psum_tensor("psT", [128, 128], F32))
        psK = top.enter_context(nc.psum_tensor("psK", [128, 64], F32))

        p = Prog(nc, top, ["sx", "sg", "sp", "sv", "ss", "scc"])
        blk = top.enter_context(nc.Block())
        bankA = Pool(p, 4)
        bankB = Pool(p, 2)
        browp = Pool(p, 2)
        mskp = Pool(p, 2)

        _VOPS = {"tensor_scalar", "tensor_tensor", "tensor_copy", "reduce_sum",
                 "reduce_max", "tensor_add", "tensor_sub", "tensor_mul",
                 "tensor_max", "memset", "scalar_tensor_tensor",
                 "tensor_single_scalar", "tensor_scalar_mul", "tensor_relu"}
        _SOPS = {"activation", "copy", "add", "mul", "square", "sqrt", "sign"}

        class Chained:
            def __init__(self, eng, sem, ops):
                self._e = eng
                self._sem = sem
                self._ops = ops

            def __getattr__(self, name):
                f = getattr(self._e, name)
                if name in self._ops:
                    def g(*a, **k):
                        p.wait(self._e, self._sem)
                        ins = f(*a, **k)
                        p.inc(ins, self._sem, 1)
                        p._auto.add((id(ins.ins), self._sem))
                        return ins
                    return g
                return f

        def bV(fn):
            blk.vector(lambda v: fn(Chained(v, "sv", _VOPS)))

        def bS(fn):
            blk.scalar(lambda sc: fn(Chained(sc, "ss", _SOPS)))

        def snop(sc):
            return sc.copy(NOPC[0:1, 0:1], NOPC[0:1, 2:3])

        def P_E_INC(t, ins, *a):
            ins = p.inc(pe_fence(t), "sp", 1)
            p.wait(t, "sp")
            return ins

        def pe_fence(t):
            import os as _o
            if _o.environ.get("NOFENCE"):
                return t.matmul(psT[0:1, 127:128], ONESC[0:1, 0:1],
                                ONESC[0:1, 0:1], start=True, stop=True,
                                skip_group_check=True)
            return t.matmul(psT[0:1, 127:128], ONESC[0:1, 0:1],
                            ONESC[0:1, 0:1], start=True, stop=True,
                            skip_group_check=True)

        def tap(stage, src_ap):
            if dbg_stage != stage:
                return

            def _t(g):
                p.wait(g, "sp", "sv", "ss", "sx", "sg", "scc")
                p.dma(g.dma_start(out=dbg_d, in_=src_ap), "sg")
                g.wait_ge(p.h["sg"], p.cnt["sg"])
                p.inc(g.engine_nop(), "sg", 1)
            blk.gpsimd(_t)
            barrier("vector", "scalar", "tensor", "sync")

        def barrier(*engs):
            for name in engs:
                def _b(e, name=name):
                    p.wait(e, "sx", "sg", "sp", "sv", "ss", "scc")
                getattr(blk, name)(_b)

        bias_ready = [0]

        def load_brow(slot, row, col0, n):
            def _l(s, slot=slot, row=row, col0=col0, n=n):
                p.wait_val(s, "sg", bias_ready[0])
                st0 = browp.st[slot]
                if st0 is not None:
                    p.wait_val(s, st0[0], st0[1])
                p.dma(s.dma_start(out=BROW[slot][0:1, 0:n],
                                  in_=scr_bias[row:row + 1, col0:col0 + n]), "sx")
            blk.sync(_l)

        with ExitStack() as cst:
            IOTJ = sbt("IOTJ", [128, 128], F32, cst)
            M0A = sbt("M0A", [16, 1024], F32, cst)
            M0B = sbt("M0B", [16, 1], F32, cst)

            def _c(g):
                g.iota(IOTC[:, :], pattern=[[1, 1]], channel_multiplier=1,
                       allow_small_or_imprecise_dtypes=True)
                g.iota(IOTJ[:, :], pattern=[[1, 128]], channel_multiplier=0,
                       allow_small_or_imprecise_dtypes=True)
                g.iota(M0A[0:16, :].rearrange("p (s d) -> p s d", d=64),
                       pattern=[[1, 16], [0, 64]], channel_multiplier=0,
                       allow_small_or_imprecise_dtypes=True)
                p.inc(g.iota(M0B[0:16, :], pattern=[[1, 1]], channel_multiplier=1,
                             allow_small_or_imprecise_dtypes=True), "sg", 1)
            blk.gpsimd(_c)

            def _c2(v):
                p.wait(v, "sg")
                v.tensor_copy(NIi[:, 0:1], IOTC[:, :])
                v.tensor_scalar(NIi[:, 0:1], NIi[:, 0:1], 15, None,
                                OP.bitwise_and)
                v.tensor_copy(T1f[:, 0:1], NIi[:, 0:1])
                v.tensor_tensor(IDENT[:, :], IOTJ[:, 0:16],
                                T1f[:, 0:1].broadcast_to([128, 16]),
                                OP.is_equal)
                v.tensor_tensor(M0[0:16, :], M0A[0:16, :],
                                M0B[0:16, 0:1].broadcast_to([16, 1024]),
                                OP.is_equal)
                v.memset(ONESC[:, :], 1.0)
                v.memset(ESCOL[0:16, 0:4], 0.0)
                v.memset(NOPC[:, :], 0.0)
                v.memset(CONSTS[:, 0:1], EPS)
                v.memset(CONSTS[:, 1:2], 1024.0)
                v.memset(CONSTS[:, 2:3], -0.5)
                v.memset(CONSTS[:, 3:4], -1.0)
                v.memset(CONSTS2[:, 0:1], 8.317766166719343)
                p.inc(v.memset(ONESR[:, :], 1.0), "sv", 1)
            bV(_c2)

        def _lx(s):
            xr = x_d.rearrange("(t p) f -> p t f", p=128)
            for g4 in range(4):
                p.dma(s.dma_start(out=X[:, 4 * g4:4 * g4 + 4, :],
                                  in_=xr[:, 4 * g4:4 * g4 + 4, :]), "sx")
        blk.sync(_lx)

        def _lb(g):
            p.dma(g.dma_start(out=scr_bias[0:1, 0:3 * D], in_=qkvb_d), "sg")
            p.dma(g.dma_start(out=scr_bias[1:2, 0:D], in_=projb_d), "sg")
            p.dma(g.dma_start(out=scr_bias[2:3, 0:D], in_=bbq_d), "sg")
            p.dma(g.dma_start(out=scr_bias[3:4, 0:D], in_=bbk_d), "sg")
            p.dma(g.dma_start(out=scr_bias[4:5, 0:D], in_=bbv_d), "sg")
            p.dma(g.dma_start(out=scr_bias[5:6, 0:D], in_=bbo_d), "sg")
            p.dma(g.dma_start(out=scr_wq, in_=bwq_d), "sg")
            p.dma(g.dma_start(out=scr_wk, in_=bwk_d), "sg")
            p.dma(g.dma_start(out=scr_wv, in_=bwv_d), "sg")
            p.dma(g.dma_start(out=scr_wo, in_=bwo_d), "sg")
        blk.gpsimd(_lb)
        bias_ready[0] = p.cnt["sg"]

        with ExitStack() as wst:
            WSTG = sbt("WSTG", [128, 4096], BF16, wst)
            WF = sbt("WF", [128, 1024], F32, wst)

            def emit_transposed_scratch(src_bf16, dst_T, O):
                for ic in range(NC):
                    def _t(s, ic=ic):
                        p.wait(s, "sx", "sg")
                        p.dma(s.dma_start(
                            out=WSTG[:, 0:O],
                            in_=src_bf16[0:O, ic * 128:(ic + 1) * 128],
                            transpose=True), "sx")
                        p.wait(s, "sx")
                        p.dma(s.dma_start(
                            out=dst_T[ic * 128:(ic + 1) * 128, 0:O],
                            in_=WSTG[:, 0:O]), "sx")
                    blk.sync(_t)

            def _cast_qkv(g):
                p.wait(g, "sg")
                p.dma(g.dma_start(out=scr_tmp[0:3 * D, :], in_=qkvw_d), "sg")
            blk.gpsimd(_cast_qkv)
            emit_transposed_scratch(scr_tmp[0:3 * D, :], scr_qkvT, 3 * D)

            def _cast_proj(g):
                p.wait(g, "sx")
                p.dma(g.dma_start(out=scr_tmp[0:D, :], in_=projw_d), "sg")
            blk.gpsimd(_cast_proj)
            emit_transposed_scratch(scr_tmp[0:D, :], scr_projT, D)

            def gq_chain(src, dst, pp, n, scale):
                def _v1(v):
                    p.wait(v, "sx", "ss", "sg")
                    p.inc(v.tensor_scalar(QS1[0:pp, 0:n], src, 64.0, 0.5,
                                          OP.mult, OP.add), "sv", 1)
                bV(_v1)

                def _s1(sc):
                    p.wait(sc, "sv")
                    sc.copy(NIi[0:pp, 0:n], QS1[0:pp, 0:n])
                    p.inc(sc.copy(QS2[0:pp, 0:n], NIi[0:pp, 0:n]), "ss", 1)
                bS(_s1)

                def _v2(v):
                    p.wait(v, "ss")
                    v.tensor_tensor(T2f[0:pp, 0:n], QS1[0:pp, 0:n],
                                    QS2[0:pp, 0:n], OP.is_ge)
                    v.scalar_tensor_tensor(QS1[0:pp, 0:n], QS2[0:pp, 0:n], 1.0,
                                           T2f[0:pp, 0:n], OP.subtract, OP.add)
                    v.tensor_scalar(QS1[0:pp, 0:n], QS1[0:pp, 0:n], scale,
                                    None, OP.mult)
                    p.inc(v.tensor_scalar(dst, QS1[0:pp, 0:n], -128.0 * scale,
                                          127.0 * scale, OP.max, OP.min),
                          "sv", 1)
                bV(_v2)

            w1r = w1_d.rearrange("(r p) i -> p r i", p=128)
            tmpr = scr_tmp.rearrange("(r p) i -> p r i", p=128)
            for r in range(HID // 128):
                def _lw(s, r=r):
                    p.wait(s, "sv")
                    p.dma(s.dma_start(out=WF[:, :], in_=w1r[:, r, :]), "sx")
                blk.sync(_lw)
                for hh in range(2):
                    gq_chain(WF[:, hh * 512:(hh + 1) * 512],
                             WSTG[:, hh * 512:(hh + 1) * 512], 128, 512,
                             1.0 / 1024.0)

                def _sw(g, r=r):
                    p.wait(g, "sv", "sx")
                    p.dma(g.dma_start(out=tmpr[:, r, :], in_=WSTG[:, 0:1024]),
                          "sg")
                blk.gpsimd(_sw)
            emit_transposed_scratch(scr_tmp[0:HID, :], scr_w1T, HID)

            w2r = w2_d.rearrange("(r p) i -> p r i", p=128)
            w2qr = scr_w2q.rearrange("(r p) i -> p r i", p=128)
            for r in range(D // 128):
                for hh in range(4):
                    def _l2(s, r=r, hh=hh):
                        p.wait(s, "sv", "sg")
                        p.dma(s.dma_start(
                            out=WF[:, :],
                            in_=w2r[:, r, hh * 1024:(hh + 1) * 1024]), "sx")
                    blk.sync(_l2)
                    for q2 in range(2):
                        gq_chain(WF[:, q2 * 512:(q2 + 1) * 512],
                                 WSTG[:, q2 * 512:(q2 + 1) * 512], 128, 512,
                                 1.0 / 1024.0)

                    def _s2(g, r=r, hh=hh):
                        p.wait(g, "sv")
                        p.dma(g.dma_start(
                            out=w2qr[:, r, hh * 1024:(hh + 1) * 1024],
                            in_=WSTG[:, 0:1024]), "sg")
                    blk.gpsimd(_s2)

            def _lb1(s):
                p.wait(s, "sv", "sg")
                p.dma(s.dma_start(out=WF[:, 0:NHC], in_=b1_d), "sx")
            blk.sync(_lb1)
            gq_chain(WF[:, 0:NHC], WSTG[:, 0:NHC], 128, NHC, 1.0 / 64.0)

            def _b1s(v):
                p.wait(v, "sv")
                p.inc(v.tensor_scalar(B1S[:, :], WSTG[:, 0:NHC], 1024.0, 32.0,
                                      OP.mult, OP.add), "sv", 1)
            bV(_b1s)

            def _lb2(s):
                p.wait(s, "sv")
                p.dma(s.dma_start(out=WF[0:1, 0:D], in_=b2_d), "sx")
            blk.sync(_lb2)
            for q2 in range(2):
                gq_chain(WF[0:1, q2 * 512:(q2 + 1) * 512],
                         WSTG[0:1, q2 * 512:(q2 + 1) * 512], 1, 512, 1.0 / 64.0)

            def _sb2(g):
                p.wait(g, "sv")
                p.dma(g.dma_start(out=scr_bias[6:7, 0:D], in_=WSTG[0:1, 0:D]),
                      "sg")
            blk.gpsimd(_sb2)
            bias_ready[0] = p.cnt["sg"]

        ln_n = [0]

        def emit_ln(mode, tag=""):
            barrier("vector", "scalar")
            ln_n[0] += 1
            lst = ExitStack()
            LNSTG = sbt("LNS%d" % ln_n[0], [128, 1024], BF16, lst)

            def _st(v):
                for tt in range(NTT):
                    v.reduce_sum(STAT[:, tt:tt + 1], X[:, tt, :], AX)
                p.inc(v.engine_nop(), "sv", 1)
            bV(_st)

            def _sq(sc):
                for tt in range(NTT):
                    sc.activation(T1f[:, :], X[:, tt, :], AF.Square,
                                  accum_out=STAT[:, 16 + tt:17 + tt])
                p.inc(snop(sc), "ss", 1)
            bS(_sq)

            def _chain(v):
                p.wait(v, "sv", "ss")
                v.tensor_scalar(STAT[:, 32:48], STAT[:, 0:16], -1.0 / 1024.0,
                                None, OP.mult)
                v.tensor_tensor(TA[:, :], STAT[:, 32:48], STAT[:, 32:48],
                                OP.mult)
                v.tensor_scalar(TB[:, :], STAT[:, 16:32], 1.0 / 1024.0,
                                None, OP.mult)
                v.tensor_tensor(TCH[:, :], TB[:, :], TA[:, :], OP.subtract)
                p.inc(v.tensor_scalar(TA[:, :], TCH[:, :], LNEPS,
                                      None, OP.add), "sv", 1)
            bV(_chain)

            def _rs(sc):
                p.wait(sc, "sv")
                sc.activation(TB[:, :], TA[:, :], AF.Ln)
                p.inc(sc.activation(STAT[:, 0:16], TB[:, :], AF.Exp,
                                    scale=-0.5), "ss", 1)
            bS(_rs)
            tap(tag + "stat", STAT[:, :])
            if mode == "quant":
                def _r16(v):
                    p.wait(v, "ss")
                    p.inc(v.tensor_scalar(STAT[:, 0:16], STAT[:, 0:16], 16.0,
                                          None, OP.mult), "sv", 1)
                bV(_r16)

            lnw = scr_ln.rearrange("(t p) f -> p t f", p=128)
            for tt in range(NTT):
                def _ap(sc, tt=tt):
                    p.wait(sc, "sv")
                    p.inc(sc.activation(T1f[:, :], X[:, tt, :], AF.Identity,
                                        bias=STAT[:, 32 + tt:33 + tt],
                                        scale=1.0), "ss", 1)
                bS(_ap)
                if mode == "plain":
                    def _mul(v, tt=tt):
                        p.wait(v, "ss", "sg")
                        p.inc(v.tensor_scalar(LNSTG[:, :], T1f[:, :],
                                              STAT[:, tt:tt + 1], None,
                                              OP.mult), "sv", 1)
                    bV(_mul)
                else:
                    def _q(v, tt=tt):
                        p.wait(v, "ss")
                        p.inc(v.tensor_scalar(T1f[:, :], T1f[:, :],
                                              STAT[:, tt:tt + 1], 0.5,
                                              OP.mult, OP.add), "sv", 1)
                    bV(_q)

                    def _qv(v, tt=tt):
                        p.wait(v, "ss", "sg")
                        v.tensor_copy(NIi[:, 0:1024], T1f[:, :])
                        v.tensor_copy(QS1[:, 0:512], NIi[:, 0:512])
                        v.tensor_copy(QS2[:, 0:512], NIi[:, 512:1024])
                        for hh, f0 in ((0, QS1), (1, QS2)):
                            sl = slice(hh * 512, (hh + 1) * 512)
                            v.tensor_tensor(T2f[:, :], T1f[:, sl], f0[:, :],
                                            OP.is_ge)
                            v.scalar_tensor_tensor(T1f[:, sl], f0[:, :], 1.0,
                                                   T2f[:, :], OP.subtract,
                                                   OP.add)
                            v.tensor_scalar(LNSTG[:, sl], T1f[:, sl], -128.0,
                                            127.0, OP.max, OP.min)
                        p.inc(v.engine_nop(), "sv", 1)
                    bV(_qv)

                def _wr(g, tt=tt):
                    p.wait(g, "sv")
                    p.dma(g.dma_start(out=lnw[:, tt, :], in_=LNSTG[:, :]), "sg")
                blk.gpsimd(_wr)
            barrier("vector", "scalar", "sync", "gpsimd", "tensor")
            lst.close()

        def emit_allreduce(pairs):
            def _up(g):
                p.wait(g, "sv", "ss", "sp", "sx")
                for ap_s, d_i, d_o in pairs:
                    p.dma(g.dma_start(out=d_i, in_=ap_s), "sg")
                g.wait_ge(p.h["sg"], p.cnt["sg"])
                for ap_s, d_i, d_o in pairs:
                    p.inc(g.collective_compute("AllReduce", OP.add,
                                               replica_groups=RG,
                                               ins=[d_i], outs=[d_o]),
                          "scc", 1)
                g.wait_ge(p.h["scc"], p.cnt["scc"])
                for ap_s, d_i, d_o in pairs:
                    p.dma(g.dma_start(out=ap_s, in_=d_o), "sg")
            blk.gpsimd(_up)

        def build_mask(slot, fc, src_col, eps):
            def _m(v, slot=slot, fc=fc, eps=eps):
                mskp.acq(slot, v)
                v.memset(MSK[:, slot, :], 0.0)
                if eps != 0.0:
                    v.tensor_scalar(MSK[0:64, slot, 2 * fc:2 * fc + 1],
                                    src_col[0:64, :], eps, None, OP.add)
                    p.inc(v.tensor_scalar(
                        MSK[64:128, slot, 2 * fc + 1:2 * fc + 2],
                        src_col[64:128, :], eps, None, OP.add), "sv", 1)
                else:
                    v.tensor_copy(MSK[0:64, slot, 2 * fc:2 * fc + 1],
                                  src_col[0:64, :])
                    p.inc(v.tensor_copy(
                        MSK[64:128, slot, 2 * fc + 1:2 * fc + 2],
                        src_col[64:128, :]), "sv", 1)
            bV(_m)

        def emit_masked_stat(src_cols_fn, eps, rhs_fn, cb_col):
            for fc in range(NC):
                build_mask(fc % 2, fc, src_cols_fn(fc), eps)

                def _mm(t, fc=fc):
                    p.wait(t, "sv")
                    if fc == 0:
                        for i in range(4):
                            bankA.acq(i, t)
                    t.matmul(psT[0:16, cb_col:cb_col + 1], MSK[:, fc % 2, :],
                             ONESC[:, :], start=(fc == 0), stop=(fc == NC - 1))
                    for nt in range(NNT):
                        P_E_INC(t, t.matmul(psA[nt][0:16, :], MSK[:, fc % 2, :],
                                       rhs_fn(fc, nt), start=(fc == 0),
                                       stop=(fc == NC - 1)), "sp", 1)
                    mskp.rel(fc % 2, "sp")
                blk.tensor(_mm)

        def emit_bcast_scale(targ_fn, bstg):
            for fc in range(NC):
                for nt in range(NNT):
                    bi = (fc * NNT + nt) % 2

                    def _bm(t, fc=fc, nt=nt, bi=bi):
                        p.wait(t, "sv")
                        bankB.acq(bi, t)
                        P_E_INC(t, t.matmul(psB[bi][:, :],
                                       M0[0:16, fc * 128:(fc + 1) * 128],
                                       bstg[0:16, nt * 512:(nt + 1) * 512],
                                       start=True, stop=True), "sp", 1)
                    blk.tensor(_bm)

                    def _bvs(sc, bi=bi):
                        p.wait(sc, "sp", "sv")
                        p.inc(sc.copy(T2f[:, :], psB[bi][:, :]), "ss", 1)
                        bankB.rel(bi, "ss")
                    bS(_bvs)

                    def _bv(v, fc=fc, nt=nt):
                        p.wait(v, "ss")
                        targ = targ_fn(fc, nt)
                        p.inc(v.tensor_tensor(targ, targ, T2f[:, :],
                                              OP.mult), "sv", 1)
                    bV(_bv)

        def emit_out_proj(kvb, qf, projT_scr, pbias_row, stage_oa=None):
            with ExitStack() as ost:
                OA = sbt("OA" + str(p.cnt["sp"]), [128, NC, 512], BF16, ost)
                PWs = [sbt("PW" + str(p.cnt["sp"]) + "_" + str(i),
                           [128, NC, 512], BF16, ost) for i in range(2)]
                pwr = projT_scr.rearrange("(ic p) o -> p ic o", p=128)

                def _pwl(s):
                    p.wait(s, "sp", "sg")
                    for on in range(2):
                        p.dma(s.dma_start(
                            out=PWs[on][:, :, :],
                            in_=pwr[:, :, on * 512:(on + 1) * 512]), "sx")
                blk.sync(_pwl)
                oa_guard = [None]
                for nt in range(NNT):
                    for fc in range(NC):
                        bi = fc % 2

                        def _em(t, fc=fc, nt=nt, bi=bi):
                            p.wait(t, "sv")
                            bankB.acq(bi, t)
                            if fc == 0 and oa_guard[0] is not None:
                                p.wait_val(t, oa_guard[0][0], oa_guard[0][1])
                            t.matmul(psB[bi][0:64, :],
                                     kvb[0:64, fc * 64:(fc + 1) * 64],
                                     qf[0:64, fc, nt * 512:(nt + 1) * 512],
                                     start=True, stop=True,
                                     tile_position=(0, 0),
                                     skip_group_check=True)
                            P_E_INC(t, t.matmul(psB[bi][64:128, :],
                                           kvb[64:128, fc * 64:(fc + 1) * 64],
                                           qf[64:128, fc,
                                              nt * 512:(nt + 1) * 512],
                                           start=True, stop=True,
                                           tile_position=(64, 64),
                                           skip_group_check=True), "sp", 1)
                        blk.tensor(_em)

                        def _ev(sc, fc=fc, bi=bi):
                            p.wait(sc, "sp")
                            p.inc(sc.copy(OA[:, fc, :], psB[bi][:, :]), "ss", 1)
                            bankB.rel(bi, "ss")
                        bS(_ev)
                    if stage_oa is not None and nt == 0:
                        tap(stage_oa, OA[:, :, :])
                    for on in range(2):
                        load_brow(on % 2, pbias_row, on * 512, 512)
                        PW = PWs[on]
                        for ts4 in range(4):
                            tt = nt * 4 + ts4
                            bi = ts4 % 4

                            def _pm(t, on=on, ts4=ts4, bi=bi):
                                p.wait(t, "sx", "ss")
                                bankA.acq(bi, t)
                                for ic in range(NC):
                                    t.matmul(psA[bi][:, :],
                                             OA[:, ic, ts4 * 128:(ts4 + 1) * 128],
                                             PW[:, ic, :], start=(ic == 0),
                                             stop=False)
                                P_E_INC(t, t.matmul(psA[bi][:, :], ONESR[0:1, 0:128],
                                               BROW[on % 2][0:1, 0:512],
                                               start=False, stop=True), "sp", 1)
                                if ts4 == 3:
                                    browp.rel(on % 2, "sp")
                            blk.tensor(_pm)

                            def _pe(v, tt=tt, on=on, bi=bi, nt=nt):
                                import os as _o
                                p.wait(v, "sp", "ss", "sx", "sg")
                                xa = X[:, tt, on * 512:(on + 1) * 512]
                                if _o.environ.get("PSUMTAP") and tt == 0 and on == 0:
                                    p.inc(v.tensor_copy(xa, psA[bi][:, :]),
                                          "sv", 1)
                                else:
                                    p.inc(v.tensor_tensor(xa, xa,
                                                          psA[bi][:, :],
                                                          OP.add), "sv", 1)
                                bankA.rel(bi, "sv")
                            bV(_pe)
                    oa_guard[0] = ("sp", p.cnt["sp"])

        def emit_fm_proj(wT_scr, col0, bias_row, bcol0, LG, WG, evac_fn):
            wtr = wT_scr.rearrange("(ic p) o -> p ic o", p=128)
            lg_guard = [None]
            for nt in range(NNT):
                def _lg(s, nt=nt):
                    p.wait(s, "sg")
                    if lg_guard[0] is not None:
                        p.wait_val(s, lg_guard[0][0], lg_guard[0][1])
                    else:
                        p.wait(s, "sp")
                    for ic in range(NC):
                        p.dma(s.dma_start(
                            out=LG[:, ic, :],
                            in_=scr_ln[nt * 512:(nt + 1) * 512,
                                       ic * 128:(ic + 1) * 128],
                            transpose=True), "sx")
                blk.sync(_lg)
                for oc in range(NC):
                    def _wg(s, nt=nt, oc=oc):
                        p.wait(s, "sp")
                        p.dma(s.dma_start(
                            out=WG[:, :, 0:128],
                            in_=wtr[:, :, col0 + oc * 128:col0 + (oc + 1) * 128]),
                            "sx")
                    blk.sync(_wg)
                    load_brow(oc % 2, bias_row, bcol0 + oc * 128, 128)
                    bi = oc % 4

                    def _qm(t, nt=nt, oc=oc, bi=bi):
                        p.wait(t, "sx")
                        bankA.acq(bi, t)
                        for ic in range(NC):
                            t.matmul(psA[bi][:, :], WG[:, ic, 0:128],
                                     LG[:, ic, :], start=(ic == 0), stop=False)
                        P_E_INC(t, t.matmul(psA[bi][:, :], BROW[oc % 2][0:1, 0:128],
                                       ONESR[0:1, 0:512], start=False,
                                       stop=True), "sp", 1)
                        browp.rel(oc % 2, "sp")
                    blk.tensor(_qm)
                    evac_fn(nt, oc, bi)
                lg_guard[0] = ("sp", p.cnt["sp"])

        def emit_tm_proj(w_list, LG, WG, evac_fn):
            lg_guard = [None]
            for ntg in range(NNT):
                def _lg(s, ntg=ntg):
                    p.wait(s, "sg")
                    if lg_guard[0] is not None:
                        p.wait_val(s, lg_guard[0][0], lg_guard[0][1])
                    else:
                        p.wait(s, "sp")
                    for ic in range(NC):
                        p.dma(s.dma_start(
                            out=LG[:, ic, :],
                            in_=scr_ln[ntg * 512:(ntg + 1) * 512,
                                       ic * 128:(ic + 1) * 128],
                            transpose=True), "sx")
                blk.sync(_lg)
                for (wtr, col0, ncols, brow_i, bcol0, tag) in w_list:
                    for on in range(ncols // 512):
                        def _wg(s, wtr=wtr, col0=col0, on=on):
                            p.wait(s, "sp")
                            p.dma(s.dma_start(
                                out=WG[:, :, :],
                                in_=wtr[:, :,
                                        col0 + on * 512:col0 + (on + 1) * 512]),
                                "sx")
                        blk.sync(_wg)
                        load_brow(on % 2, brow_i, bcol0 + on * 512, 512)
                        for t4 in range(4):
                            tt = ntg * 4 + t4
                            bi = t4 % 4

                            def _km(t, on=on, t4=t4, bi=bi):
                                p.wait(t, "sx")
                                bankA.acq(bi, t)
                                for ic in range(NC):
                                    t.matmul(psA[bi][:, :],
                                             LG[:, ic, t4 * 128:(t4 + 1) * 128],
                                             WG[:, ic, :], start=(ic == 0),
                                             stop=False)
                                P_E_INC(t, t.matmul(psA[bi][:, :], ONESR[0:1, 0:128],
                                               BROW[on % 2][0:1, 0:512],
                                               start=False, stop=True), "sp", 1)
                                if t4 == 3:
                                    browp.rel(on % 2, "sp")
                            blk.tensor(_km)
                            evac_fn(tag, on, tt, bi)
                lg_guard[0] = ("sp", p.cnt["sp"])

        def emit_kv_einsum(kt, vt, KVUt):
            for fc in range(NC):
                def _kv(t, fc=fc):
                    p.wait(t, "sv", "ss")
                    a0 = fc * 128
                    for tt in range(NTT):
                        mmA = t.matmul(psK[0:64, :], kt[:, tt, a0:a0 + 64],
                                       vt[:, tt, a0:a0 + 64], start=(tt == 0),
                                       stop=(tt == NTT - 1),
                                       tile_position=(0, 0),
                                       skip_group_check=True)
                        mmB = t.matmul(psK[64:128, :],
                                       kt[:, tt, a0 + 64:a0 + 128],
                                       vt[:, tt, a0 + 64:a0 + 128],
                                       start=(tt == 0), stop=(tt == NTT - 1),
                                       tile_position=(0, 64),
                                       skip_group_check=True)
                        if tt == NTT - 1:
                            P_E_INC(t, mmB)
                blk.tensor(_kv)

                def _ke(sc, fc=fc):
                    p.wait(sc, "sp")
                    p.inc(sc.copy(KVUt[:, fc * 64:(fc + 1) * 64],
                                  psK[:, :]), "ss", 1)
                bS(_ke)

        def emit_tm_colsum(src, OUTC):
            for mc in range(NC):
                def _cs(t, mc=mc):
                    p.wait(t, "sv", "ss")
                    for tt in range(NTT):
                        mm = t.matmul(psT[0:128, mc:mc + 1],
                                      src[:, tt, mc * 128:(mc + 1) * 128],
                                      ONESC[:, :], start=(tt == 0),
                                      stop=(tt == NTT - 1))
                        if tt == NTT - 1:
                            P_E_INC(t, mm)
                blk.tensor(_cs)

            def _ce(sc):
                p.wait(sc, "sp")
                p.inc(sc.copy(OUTC[:, 0:8], psT[0:128, 0:8]), "ss", 1)
            bS(_ce)

        def emit_rows_to_tm(arn_rows, dst):
            for tt in range(NTT):
                def _tr(t, tt=tt):
                    p.wait(t, "sv", "ss")
                    P_E_INC(t, t.transpose(psT[0:128, 0:16],
                                      arn_rows[:, tt * 128:(tt + 1) * 128],
                                      IDENT[64:80, 0:16]), "sp", 1)
                blk.tensor(_tr)

                def _te(sc, tt=tt):
                    p.wait(sc, "sp")
                    p.inc(sc.copy(dst[:, tt, :], psT[0:128, 0:16]),
                          "ss", 1)
                bS(_te)

        barrier("vector", "scalar", "tensor", "gpsimd", "sync")
        tap("x_loaded", X[:, :, :])

        emit_ln("plain", "ln1")
        tap("ln1", scr_ln)

        with ExitStack() as ast:
            QF = sbt("QF", [128, NC, TC], BF16, ast)
            KVB = sbt("KVB", [128, 512], BF16, ast)
            inn = ExitStack()
            KT = sbt("KT", [128, NTT, D], BF16, inn)
            VT = sbt("VT", [128, NTT, D], BF16, inn)
            ARN = sbt("ARN", [128, 2048], F32, inn)
            KVU = sbt("KVU", [128, 512], F32, inn)
            SI = ARN[0:16, :]
            SO = ARN[32:48, :]
            EE = ARN[64:80, :]
            TMP2 = ARN[96:112, :]

            barrier("sync", "gpsimd", "tensor", "vector", "scalar")

            with ExitStack() as qst:
                LG = sbt("LG", [128, NC, 512], BF16, qst)
                WG = sbt("WG", [128, NC, 512], BF16, qst)

                def q_evac(nt, oc, bi):
                    def _e(sc, nt=nt, oc=oc, bi=bi):
                        p.wait(sc, "sp")
                        p.inc(sc.activation(QF[:, oc, nt * 512:(nt + 1) * 512],
                                            psA[bi][:, :], AF.Sigmoid),
                              "ss", 1)
                        bankA.rel(bi, "ss")
                    bS(_e)
                emit_fm_proj(scr_qkvT, 0, 0, 0, LG, WG, q_evac)
                barrier("sync", "gpsimd", "tensor", "vector", "scalar")

                kvw = scr_k.rearrange("(t p) f -> p t f", p=128)

                def kv_evac(tag, on, tt, bi):
                    if tag == "k":
                        def _e(sc, on=on, tt=tt, bi=bi):
                            p.wait(sc, "sp")
                            p.inc(sc.activation(
                                KT[:, tt, on * 512:(on + 1) * 512],
                                psA[bi][:, :], AF.Sigmoid), "ss", 1)
                            bankA.rel(bi, "ss")
                        bS(_e)

                        def _w(g, on=on, tt=tt):
                            p.wait(g, "ss")
                            p.dma(g.dma_start(
                                out=kvw[:, tt, on * 512:(on + 1) * 512],
                                in_=KT[:, tt, on * 512:(on + 1) * 512]), "sg")
                        blk.gpsimd(_w)
                    else:
                        def _e(sc, on=on, tt=tt, bi=bi):
                            p.wait(sc, "sp")
                            p.inc(sc.copy(
                                VT[:, tt, on * 512:(on + 1) * 512],
                                psA[bi][:, :]), "ss", 1)
                            bankA.rel(bi, "ss")
                        bS(_e)

                qkvtr = scr_qkvT.rearrange("(ic p) o -> p ic o", p=128)
                emit_tm_proj([(qkvtr, 1024, 1024, 0, 1024, "k"),
                              (qkvtr, 2048, 1024, 0, 2048, "v")], LG, WG,
                             kv_evac)

            tap("q_fm", QF[:, :, :])
            tap("k_tm", KT[:, :, :])
            tap("v_tm", VT[:, :, :])

            mst = ExitStack()
            BSTG = sbt("BSTG", [16, 2048], BF16, mst)
            BSTG2 = sbt("BSTG2", [16, 2048], BF16, mst)

            emit_tm_colsum(KT, KSUM)

            def _qs(v):
                p.wait(v, "ss")
                for fc in range(NC):
                    v.reduce_sum(QSUM[:, fc:fc + 1], QF[:, fc, :], AX)
                p.inc(v.engine_nop(), "sv", 1)
            bV(_qs)

            def _pack1(v):
                p.wait(v, "sv")
                v.tensor_copy(T1f[:, 0:8], KSUM[:, :])
                p.inc(v.tensor_copy(T1f[:, 8:16], QSUM[:, :]), "sv", 1)
            bV(_pack1)
            emit_allreduce([(T1f[:, 0:16], cc_a_i, cc_a_o)])

            def _unpack1(v):
                p.wait(v, "sg")
                v.tensor_copy(KSUM[:, :], T1f[:, 0:8])
                p.inc(v.tensor_copy(QSUM[:, :], T1f[:, 8:16]), "sv", 1)
            bV(_unpack1)
            tap("ksum", KSUM[:, :])
            tap("qsum", QSUM[:, :])

            emit_masked_stat(lambda fc: KSUM[:, fc:fc + 1], EPS,
                             lambda fc, nt: QF[:, fc, nt * 512:(nt + 1) * 512],
                             0)

            def _si(sc):
                p.wait(sc, "sp")
                sc.activation(CB[0:16, 0:1], psT[0:16, 0:1], AF.Identity, scale=EPS)
                for nt in range(NNT):
                    sc.activation(EE[:, nt * 512:(nt + 1) * 512],
                                  psA[nt][0:16, :], AF.Identity,
                                  bias=CB[0:16, 0:1])
                p.inc(snop(sc), "ss", 1)
            bS(_si)
            for i in range(4):
                bankA.rel(i, "ss")

            def _sir(sc):
                p.wait(sc, "ss")
                sc.activation(SO[:, :], EE[:, :], AF.Ln)
                p.inc(sc.activation(SI[:, :], SO[:, :], AF.Exp, scale=-1.0),
                      "ss", 1)
            bS(_sir)
            tap("si", SI)

            with ExitStack() as kst:
                KFG = sbt("KFG", [128, NC, 512], BF16, kst)

                def kfg_load(nt):
                    def _k(s, nt=nt):
                        p.wait(s, "sp", "sg")
                        for ic in range(NC):
                            p.dma(s.dma_start(
                                out=KFG[:, ic, :],
                                in_=scr_k[nt * 512:(nt + 1) * 512,
                                          ic * 128:(ic + 1) * 128],
                                transpose=True), "sx")
                    blk.sync(_k)

                for nt in range(NNT):
                    kfg_load(nt)
                    for fc in range(NC):
                        build_mask(fc % 2, fc, QSUM[:, fc:fc + 1], EPS)

                        def _mm(t, fc=fc, nt=nt):
                            p.wait(t, "sv", "sx")
                            if fc == 0:
                                bankA.acq(nt, t)
                            if nt == 0:
                                t.matmul(psT[0:16, 1:2], MSK[:, fc % 2, :],
                                         ONESC[:, :], start=(fc == 0),
                                         stop=(fc == NC - 1))
                            P_E_INC(t, t.matmul(psA[nt][0:16, :], MSK[:, fc % 2, :],
                                           KFG[:, fc, :], start=(fc == 0),
                                           stop=(fc == NC - 1)), "sp", 1)
                            mskp.rel(fc % 2, "sp")
                        blk.tensor(_mm)

                def _so(sc):
                    p.wait(sc, "sp")
                    sc.activation(CB[0:16, 1:2], psT[0:16, 1:2], AF.Identity,
                                  scale=EPS)
                    for nt in range(NNT):
                        sc.activation(EE[:, nt * 512:(nt + 1) * 512],
                                      psA[nt][0:16, :], AF.Identity,
                                      bias=CB[0:16, 1:2])
                    p.inc(snop(sc), "ss", 1)
                bS(_so)
                for i in range(4):
                    bankA.rel(i, "ss")

                def _sor(sc):
                    p.wait(sc, "ss")
                    sc.activation(TMP2[:, :], EE[:, :], AF.Ln)
                    p.inc(sc.activation(SO[:, :], TMP2[:, :], AF.Exp,
                                        scale=-1.0), "ss", 1)
                bS(_sor)
                tap("so", SO)

                def _sib(v):
                    p.wait(v, "sv")
                    p.inc(v.tensor_copy(BSTG[0:16, :], SI[:, :]), "sv", 1)
                bV(_sib)

                def emit_col_wsum(src3d, OUTC):
                    for fc in range(NC):
                        for nt in range(NNT):
                            bi = nt % 2

                            def _bm(t, fc=fc, nt=nt, bi=bi):
                                p.wait(t, "sv")
                                bankB.acq(bi, t)
                                P_E_INC(t, t.matmul(
                                    psB[bi][:, :],
                                    M0[0:16, fc * 128:(fc + 1) * 128],
                                    BSTG[0:16, nt * 512:(nt + 1) * 512],
                                    start=True, stop=True), "sp", 1)
                            blk.tensor(_bm)

                            def _trs(sc, bi=bi):
                                p.wait(sc, "sp", "sv")
                                p.inc(sc.copy(T2f[:, :], psB[bi][:, :]),
                                      "ss", 1)
                                bankB.rel(bi, "ss")
                            bS(_trs)

                            def _tr(v, fc=fc, nt=nt):
                                p.wait(v, "ss")
                                v.tensor_tensor(QS1[:, :], src3d(fc, nt),
                                                T2f[:, :], OP.mult)
                                p.inc(v.reduce_sum(STAT[:, nt:nt + 1],
                                                   QS1[:, :], AX), "sv", 1)
                            bV(_tr)

                        def _fin(v, fc=fc):
                            p.inc(v.reduce_sum(OUTC[:, fc:fc + 1],
                                               STAT[:, 0:4], AX), "sv", 1)
                        bV(_fin)

                emit_col_wsum(
                    lambda fc, nt: QF[:, fc, nt * 512:(nt + 1) * 512], QSI)
                tap("qsi", QSI[:, :])

                def _sob(v):
                    p.wait(v, "sv")
                    p.inc(v.tensor_copy(BSTG[0:16, :], SO[:, :]), "sv", 1)
                bV(_sob)
                for nt in range(NNT):
                    kfg_load(nt)
                    for fc in range(NC):
                        bi = fc % 2

                        def _bm2(t, fc=fc, nt=nt, bi=bi):
                            p.wait(t, "sv", "sx")
                            bankB.acq(bi, t)
                            P_E_INC(t, t.matmul(
                                psB[bi][:, :],
                                M0[0:16, fc * 128:(fc + 1) * 128],
                                BSTG[0:16, nt * 512:(nt + 1) * 512],
                                start=True, stop=True), "sp", 1)
                        blk.tensor(_bm2)

                        def _tr2s(sc, bi=bi):
                            p.wait(sc, "sp", "sv")
                            p.inc(sc.copy(T2f[:, :], psB[bi][:, :]), "ss", 1)
                            bankB.rel(bi, "ss")
                        bS(_tr2s)

                        def _tr2(v, fc=fc, nt=nt):
                            p.wait(v, "ss")
                            v.tensor_tensor(QS1[:, :], KFG[:, fc, :],
                                            T2f[:, :], OP.mult)
                            p.inc(v.reduce_sum(STAT[:, 4 + fc:5 + fc],
                                               QS1[:, :], AX), "sv", 1)
                        bV(_tr2)

                        def _tr2f(v, fc=fc, nt=nt):
                            if nt == 0:
                                p.inc(v.tensor_copy(KSOC[:, fc:fc + 1],
                                                    STAT[:, 4 + fc:5 + fc]),
                                      "sv", 1)
                            else:
                                p.inc(v.tensor_tensor(KSOC[:, fc:fc + 1],
                                                      KSOC[:, fc:fc + 1],
                                                      STAT[:, 4 + fc:5 + fc],
                                                      OP.add), "sv", 1)
                        bV(_tr2f)
                tap("kso", KSOC[:, :])

                def _pack2(v):
                    p.wait(v, "sv")
                    v.tensor_copy(T1f[:, 0:8], KSOC[:, :])
                    p.inc(v.tensor_copy(T1f[:, 8:16], QSI[:, :]), "sv", 1)
                bV(_pack2)
                emit_allreduce([(T1f[:, 0:16], cc_a_i, cc_a_o)])

                def _unpack2(v):
                    p.wait(v, "sg")
                    v.tensor_copy(KSOC[:, :], T1f[:, 0:8])
                    p.inc(v.tensor_copy(QSI[:, :], T1f[:, 8:16]), "sv", 1)
                bV(_unpack2)

                emit_masked_stat(lambda fc: KSOC[:, fc:fc + 1], EPS,
                                 lambda fc, nt: QF[:, fc,
                                                   nt * 512:(nt + 1) * 512], 2)

                def _sa(sc):
                    p.wait(sc, "sp")
                    sc.activation(CB[0:16, 2:3], psT[0:16, 2:3], AF.Identity,
                                  scale=EPS, bias=EPS)
                    for nt in range(NNT):
                        sc.activation(BSTG2[0:16, nt * 512:(nt + 1) * 512],
                                      psA[nt][0:16, :], AF.Sigmoid,
                                      bias=CB[0:16, 2:3])
                    p.inc(snop(sc), "ss", 1)
                bS(_sa)
                for i in range(4):
                    bankA.rel(i, "ss")
                tap("sa", BSTG2[0:16, :])

                for nt in range(NNT):
                    kfg_load(nt)
                    for fc in range(NC):
                        build_mask(fc % 2, fc, QSI[:, fc:fc + 1], EPS)

                        def _mm2(t, fc=fc, nt=nt):
                            p.wait(t, "sv", "sx")
                            if fc == 0:
                                bankA.acq(nt, t)
                            if nt == 0:
                                t.matmul(psT[0:16, 3:4], MSK[:, fc % 2, :],
                                         ONESC[:, :], start=(fc == 0),
                                         stop=(fc == NC - 1))
                            P_E_INC(t, t.matmul(psA[nt][0:16, :], MSK[:, fc % 2, :],
                                           KFG[:, fc, :], start=(fc == 0),
                                           stop=(fc == NC - 1)), "sp", 1)
                            mskp.rel(fc % 2, "sp")
                        blk.tensor(_mm2)

                def _cs2(sc):
                    p.wait(sc, "sp")
                    sc.activation(CB[0:16, 3:4], psT[0:16, 3:4], AF.Identity,
                                  scale=EPS, bias=EPS)
                    for nt in range(NNT):
                        sc.activation(SO[:, nt * 512:(nt + 1) * 512],
                                      psA[nt][0:16, :], AF.Identity,
                                      bias=CB[0:16, 3:4])
                    p.inc(snop(sc), "ss", 1)
                bS(_cs2)
                for i in range(4):
                    bankA.rel(i, "ss")

                def _clip(v):
                    p.wait(v, "ss")
                    p.inc(v.tensor_scalar(SO[:, :], SO[:, :], -1.0, 1.0,
                                          OP.max, OP.min), "sv", 1)
                bV(_clip)

                def _exp(sc):
                    p.wait(sc, "sv")
                    p.inc(sc.activation(EE[:, :], SO[:, :], AF.Exp), "ss", 1)
                bS(_exp)

                def _es(v):
                    p.wait(v, "ss")
                    p.inc(v.reduce_sum(ESCOL[0:16, 0:1], EE[:, :], AX),
                          "sv", 1)
                bV(_es)
                tap("e", EE)

            est = ExitStack()
            ETM = sbt("ETM", [128, NTT, 16], F32, est)
            emit_rows_to_tm(EE, ETM)
            for tt in range(NTT):
                def _ve(v, tt=tt):
                    p.wait(v, "sv")
                    va = VT[:, tt, :].rearrange("p (s d) -> p s d", d=64)
                    ea = ETM[:, tt, :].rearrange("p (s o) -> p s o", o=1)
                    p.inc(v.tensor_tensor(va, va,
                                          ea.broadcast_to([128, 16, 64]),
                                          OP.mult), "sv", 1)
                bV(_ve)
            barrier("vector")
            est.close()

            emit_kv_einsum(KT, VT, KVU)
            tap("kvu_pre", KVU[:, :])

            emit_allreduce([(KVU[:, :], cc_d_i, cc_d_o),
                            (ESCOL[0:16, 0:4], cc_e_i, cc_e_o)])

            def _norm(v):
                p.wait(v, "sg")
                p.inc(v.tensor_copy(KVB[:, :], KVU[:, :]), "sv", 1)
            bV(_norm)

            def _norm2(sc):
                p.wait(sc, "sg")
                sc.activation(ESCOL[0:16, 2:3], ESCOL[0:16, 0:1], AF.Ln)
                p.inc(sc.activation(ESCOL[0:16, 1:2], ESCOL[0:16, 2:3],
                                    AF.Exp, scale=-1.0, bias=8.317766166719343), "ss", 1)
            bS(_norm2)
            tap("kvu", KVU[:, :])

            def _scq(v):
                p.wait(v, "sv", "ss")
                v.tensor_scalar(SI[:, :], SI[:, :], ESCOL[0:16, 1:2], None,
                                OP.mult)
                p.inc(v.tensor_tensor(BSTG[0:16, :], SI[:, :], BSTG2[0:16, :],
                                      OP.mult), "sv", 1)
            bV(_scq)
            tap("norm", ESCOL[0:16, 0:4])
            tap("scq", BSTG[0:16, :])
            emit_bcast_scale(lambda fc, nt: QF[:, fc, nt * 512:(nt + 1) * 512], BSTG)
            tap("qfin", QF[:, :, :])
            mst.close()

            tap("x0b", X[:, :, :])
            barrier("sync", "gpsimd", "tensor", "vector", "scalar")
            inn.close()
            emit_out_proj(KVB, QF, scr_projT, 1, stage_oa="outa")

        tap("x1", X[:, :, :])

        def emit_ffn(tag):
            emit_ln("quant")
            if tag == 1:
                tap("xq", scr_ln)
            with ExitStack() as fst:
                W2R = sbt("W2R%d" % tag, [128, NHC, 1024], BF16, fst)
                HQ = sbt("HQ%d" % tag, [128, NHC, 512], BF16, fst)
                XQG = sbt("XQG%d" % tag, [128, NC, 512], BF16, fst)
                W1G = [sbt("W1Ga%d" % tag, [128, NC, 512], BF16, fst),
                       sbt("W1Gb%d" % tag, [128, NC, 512], BF16, fst)]
                w1gp = Pool(p, 2)

                barrier("sync", "gpsimd", "tensor", "vector", "scalar")

                def _w2(s):
                    for hc in range(NHC):
                        p.dma(s.dma_start(
                            out=W2R[:, hc, :],
                            in_=scr_w2q[:, hc * 128:(hc + 1) * 128],
                            transpose=True), "sx")
                blk.sync(_w2)
                w1tr = scr_w1T.rearrange("(ic p) h -> p ic h", p=128)

                xq_guard = [None]
                for tau in range(NNT):
                    def _xq(s, tau=tau):
                        p.wait(s, "sg")
                        if xq_guard[0] is not None:
                            p.wait_val(s, xq_guard[0][0], xq_guard[0][1])
                        for ic in range(NC):
                            p.dma(s.dma_start(
                                out=XQG[:, ic, :],
                                in_=scr_ln[tau * 512:(tau + 1) * 512,
                                           ic * 128:(ic + 1) * 128],
                                transpose=True), "sx")
                    blk.sync(_xq)
                    if tag == 1 and tau == 0:
                        tap("b1s", B1S[:, :])
                        tap("xqg", XQG[:, :, :])

                    for hg in range(8):
                        def _w1(s, tau=tau, hg=hg):
                            w1gp.acq(hg % 2, s)
                            p.dma(s.dma_start(
                                out=W1G[hg % 2][:, :, :],
                                in_=w1tr[:, :, hg * 512:(hg + 1) * 512]), "sx")
                        blk.sync(_w1)
                        for h4 in range(4):
                            hc = hg * 4 + h4
                            bi = hc % 4

                            def _m1(t, hg=hg, h4=h4, bi=bi):
                                p.wait(t, "sx")
                                bankA.acq(bi, t)
                                for ic in range(NC):
                                    mm = t.matmul(
                                        psA[bi][:, :],
                                        W1G[hg % 2][:, ic,
                                                    h4 * 128:(h4 + 1) * 128],
                                        XQG[:, ic, :], start=(ic == 0),
                                        stop=(ic == NC - 1))
                                P_E_INC(t, mm)
                                if h4 == 3:
                                    w1gp.rel(hg % 2, "sp")
                            blk.tensor(_m1)

                            def _e1(sc, hc=hc, bi=bi, h4=h4):
                                p.wait(sc, "sp", "sv")
                                sl = slice((h4 % 2) * 512,
                                           (h4 % 2) * 512 + 512)
                                p.inc(sc.activation(T1f[:, sl], psA[bi][:, :],
                                                    AF.Identity, scale=1024.0,
                                                    bias=B1S[:, hc:hc + 1]),
                                      "ss", 1)
                                bankA.rel(bi, "ss")
                            bS(_e1)
                            if tag == 1 and tau == 0 and hc == 0:
                                tap("h0", T1f[:, 0:512])
                            if h4 % 2 == 0:
                                continue
                            hc0 = hc - 1

                            def _e2(v, hc0=hc0):
                                p.wait(v, "ss")
                                AI2 = T1f.ap().bitcast(I32)[:, 0:1024]
                                v.tensor_copy(NIi[:, 0:1024], T1f[:, :])
                                hqv = HQ[:, hc0:hc0 + 2, :].rearrange(
                                    "p a b -> p (a b)")
                                v.tensor_scalar(AI2, NIi[:, 0:1024], -128, 6,
                                                OP.bitwise_and,
                                                OP.arith_shift_right)
                                v.tensor_scalar(NIi[:, 0:1024], NIi[:, 0:1024],
                                                127, None, OP.bitwise_and)
                                v.scalar_tensor_tensor(hqv, NIi[:, 0:1024],
                                                       65.0, AI2,
                                                       OP.is_ge, OP.add)
                                p.inc(v.tensor_scalar(hqv, hqv,
                                                      0.0, 127.0, OP.max,
                                                      OP.min), "sv", 1)
                            bV(_e2)
                    if tag == 1:
                        if tau == 0:
                            tap("hq", HQ[:, :, :])

                    for ts4 in range(4):
                        tt = tau * 4 + ts4
                        for on in range(2):
                            load_brow(on % 2, 6, on * 512, 512)
                            bi = (ts4 * 2 + on) % 4

                            def _m2(t, ts4=ts4, on=on, bi=bi):
                                p.wait(t, "sv", "sx")
                                bankA.acq(bi, t)
                                for hc in range(NHC):
                                    t.matmul(
                                        psA[bi][:, :],
                                        HQ[:, hc, ts4 * 128:(ts4 + 1) * 128],
                                        W2R[:, hc, on * 512:(on + 1) * 512],
                                        start=(hc == 0), stop=False)
                                P_E_INC(t, t.matmul(psA[bi][:, :], ONESR[0:1, 0:128],
                                               BROW[on % 2][0:1, 0:512],
                                               start=False, stop=True),
                                      "sp", 1)
                                browp.rel(on % 2, "sp")
                            blk.tensor(_m2)

                            def _r2(v, tt=tt, on=on, bi=bi):
                                p.wait(v, "sp", "ss", "sx", "sg")
                                xa = X[:, tt, on * 512:(on + 1) * 512]
                                p.inc(v.tensor_tensor(xa, xa, psA[bi][:, :],
                                                      OP.add), "sv", 1)
                                bankA.rel(bi, "sv")
                            bV(_r2)
                    xq_guard[0] = ("sp", p.cnt["sp"])

        tap("w1t", scr_w1T)
        tap("w2q", scr_w2q)
        emit_ffn(1)
        tap("x2", X[:, :, :])

        emit_ln("plain")
        with ExitStack() as bst:
            QF = sbt("QFb", [128, NC, TC], BF16, bst)
            KVB = sbt("KVBb", [128, 512], BF16, bst)
            innb = ExitStack()
            KT = sbt("KTb", [128, NTT, D], BF16, innb)
            VT = sbt("VTb", [128, NTT, D], BF16, innb)
            ARN = sbt("ARNb", [128, 2048], F32, innb)
            KVU = sbt("KVUb", [128, 512], F32, innb)
            ZZ = ARN[0:16, :]

            barrier("sync", "gpsimd", "tensor", "vector", "scalar")

            with ExitStack() as qst:
                LG = sbt("LGb", [128, NC, 512], BF16, qst)
                WG = sbt("WGb", [128, NC, 512], BF16, qst)

                def fm_evac(dst_ap, bi):
                    def _e1(sc, bi=bi):
                        p.wait(sc, "sp", "sv")
                        sc.activation(T2f[:, :], psA[bi][:, :], AF.Relu,
                                      scale=-1.0)
                        sc.activation(QS2[:, :], psA[bi][:, :], AF.Relu)
                        p.inc(sc.activation(QS1[:, :], T2f[:, :], AF.Exp,
                                            scale=-1.0), "ss", 1)
                        bankA.rel(bi, "ss")
                    bS(_e1)

                    def _e3(v, dst_ap=dst_ap):
                        p.wait(v, "ss")
                        p.inc(v.tensor_tensor(dst_ap, QS2[:, :], QS1[:, :],
                                              OP.add), "sv", 1)
                    bV(_e3)

                def q_evac(nt, oc, bi):
                    fm_evac(QF[:, oc, nt * 512:(nt + 1) * 512], bi)
                emit_fm_proj(scr_wq, 0, 2, 0, LG, WG, q_evac)

                def kv_evac(tag, on, tt, bi):
                    if tag == "k":
                        fm_evac(KT[:, tt, on * 512:(on + 1) * 512], bi)
                    else:
                        def _e(sc, on=on, tt=tt, bi=bi):
                            p.wait(sc, "sp")
                            p.inc(sc.copy(
                                VT[:, tt, on * 512:(on + 1) * 512],
                                psA[bi][:, :]), "ss", 1)
                            bankA.rel(bi, "ss")
                        bS(_e)

                wkr = scr_wk.rearrange("(ic p) o -> p ic o", p=128)
                wvr = scr_wv.rearrange("(ic p) o -> p ic o", p=128)
                emit_tm_proj([(wkr, 0, 1024, 3, 0, "k"),
                              (wvr, 0, 1024, 4, 0, "v")], LG, WG, kv_evac)

            tap("bqm", QF[:, :, :])
            tap("bkm", KT[:, :, :])

            mstb = ExitStack()
            BSTG = sbt("BSTGb", [16, 2048], BF16, mstb)

            emit_tm_colsum(KT, KSUM)
            emit_kv_einsum(KT, VT, KVU)

            def _pk(v):
                p.wait(v, "sv", "ss")
                p.inc(v.engine_nop(), "sv", 1)
            bV(_pk)
            emit_allreduce([(KSUM[:, :], cc_c_i, cc_c_o),
                            (KVU[:, :], cc_d_i, cc_d_o)])
            tap("bkvu", KVU[:, :])

            def _kvb(v):
                p.wait(v, "sg")
                p.inc(v.tensor_copy(KVB[:, :], KVU[:, :]), "sv", 1)
            bV(_kvb)

            emit_masked_stat(lambda fc: KSUM[:, fc:fc + 1], 0.0,
                             lambda fc, nt: QF[:, fc, nt * 512:(nt + 1) * 512],
                             4)

            ZT = ARN[32:48, :]

            def _z(sc):
                p.wait(sc, "sp")
                for nt in range(NNT):
                    sc.activation(ZT[:, nt * 512:(nt + 1) * 512],
                                  psA[nt][0:16, :], AF.Identity, bias=EPS)
                p.inc(snop(sc), "ss", 1)
            bS(_z)
            for i in range(4):
                bankA.rel(i, "ss")

            def _zr(sc):
                p.wait(sc, "ss")
                sc.activation(ARN[64:80, :], ZT[:, :], AF.Ln)
                p.inc(sc.activation(ZZ[:, :], ARN[64:80, :], AF.Exp,
                                    scale=-1.0), "ss", 1)
            bS(_zr)

            def _zrc(v):
                p.wait(v, "ss")
                p.inc(v.tensor_copy(BSTG[0:16, :], ZZ[:, :]), "sv", 1)
            bV(_zrc)
            tap("bz", ZZ)

            emit_bcast_scale(lambda fc, nt: QF[:, fc, nt * 512:(nt + 1) * 512], BSTG)
            mstb.close()
            barrier("sync", "gpsimd", "tensor", "vector", "scalar")
            innb.close()
            emit_out_proj(KVB, QF, scr_wo, 5)

        tap("x3", X[:, :, :])

        emit_ffn(2)
        tap("x4", X[:, :, :])

        def _stv(g):
            p.wait(g, "sv", "ss", "sp")
            yw = y_d.rearrange("(t p) f -> p t f", p=128)
            for g4 in range(4):
                p.dma(g.dma_start(out=yw[:, 4 * g4:4 * g4 + 4, :],
                                  in_=X[:, 4 * g4:4 * g4 + 4, :]), "sg")
            g.wait_ge(p.h["sg"], p.cnt["sg"])
        blk.gpsimd(_stv)

    return nc


def kernel(**inputs):
    global _DBG_RESULT
    x = np.ascontiguousarray(np.asarray(inputs["L0_attn_a_x"], dtype=np.float32))
    common = {
        "aqkv_w": np.ascontiguousarray(inputs["L0_attn_a_qkv_w"], dtype=np.float32),
        "aqkv_b": np.ascontiguousarray(inputs["L0_attn_a_qkv_b"], dtype=np.float32).reshape(1, -1),
        "aproj_w": np.ascontiguousarray(inputs["L0_attn_a_proj_w"], dtype=np.float32),
        "aproj_b": np.ascontiguousarray(inputs["L0_attn_a_proj_b"], dtype=np.float32).reshape(1, -1),
        "b_wq": np.ascontiguousarray(np.asarray(inputs["L0_attn_b_wq"], dtype=np.float32).reshape(D, D)),
        "b_wk": np.ascontiguousarray(np.asarray(inputs["L0_attn_b_wk"], dtype=np.float32).reshape(D, D)),
        "b_wv": np.ascontiguousarray(np.asarray(inputs["L0_attn_b_wv"], dtype=np.float32).reshape(D, D)),
        "b_wo": np.ascontiguousarray(np.asarray(inputs["L0_attn_b_wo"], dtype=np.float32).reshape(D, D)),
        "b_bq": np.ascontiguousarray(np.asarray(inputs["L0_attn_b_bq"], dtype=np.float32).reshape(1, -1)),
        "b_bk": np.ascontiguousarray(np.asarray(inputs["L0_attn_b_bk"], dtype=np.float32).reshape(1, -1)),
        "b_bv": np.ascontiguousarray(np.asarray(inputs["L0_attn_b_bv"], dtype=np.float32).reshape(1, -1)),
        "b_bo": np.ascontiguousarray(np.asarray(inputs["L0_attn_b_bo"], dtype=np.float32).reshape(1, -1)),
        "f_w1": np.ascontiguousarray(inputs["L0_ffn_w1"], dtype=np.float32),
        "f_b1": np.ascontiguousarray(
            np.asarray(inputs["L0_ffn_b1"], dtype=np.float32).reshape(NHC, 128).T),
        "f_w2": np.ascontiguousarray(inputs["L0_ffn_w2"], dtype=np.float32),
        "f_b2": np.ascontiguousarray(np.asarray(inputs["L0_ffn_b2"], dtype=np.float32).reshape(1, -1)),
    }
    in_maps = []
    for c in range(8):
        bb, half = c // 2, c % 2
        m = dict(common)
        m["x_in"] = np.ascontiguousarray(x[bb, half * TC:(half + 1) * TC, :])
        in_maps.append(m)
    nc = build_core_program(dbg_stage=_DBG_STAGE)
    res = run_bass_kernel_spmd(nc, in_maps, core_ids=list(range(8)))
    out = np.empty((B, S, D), dtype=np.float32)
    for c in range(8):
        bb, half = c // 2, c % 2
        out[bb, half * TC:(half + 1) * TC, :] = res.results[c]["y_out"]
    if _DBG_STAGE is not None:
        _DBG_RESULT = [res.results[c].get("dbg") for c in range(8)]
    return out
